# version 03
# baseline (speedup 1.0000x reference)
import numpy as np
import concourse.bass as bass
import concourse.mybir as mybir
from concourse.bass_utils import run_bass_kernel_spmd

F32 = mybir.dt.float32
BF16 = mybir.dt.bfloat16
FP8 = mybir.dt.float8e4
I8 = mybir.dt.int8
AF = mybir.ActivationFunctionType
ALU = mybir.AluOpType
AX = mybir.AxisListType

B, T, C, H = 4, 2048, 2048, 16
FC = C // 128
NTOK = 1024
RS = float(1.0 / np.sqrt(128.0))
EXP_SHIFT = -12.0
EPS_LN = 1e-5
EPS_L1 = 1e-5
NCORES = 8

GQ = [(0, 0), (1, 0), (1, 1), (0, 1)]
KV_KQ = 16 * 128 * 512
KV_VQ = 16 * 4 * 128 * 128
KV_Q = KV_KQ + KV_VQ
KV_BLK = 2 * KV_Q


class Sem:
    def __init__(self, nc, stack, name):
        self.h = stack.enter_context(nc.semaphore(name))
        self.n = 0


class Builder:
    def __init__(self):
        from contextlib import ExitStack
        self.nc = bass.Bass("TRN2", target_bir_lowering=False, debug=False)
        self.stack = ExitStack()
        self.q = {e: [] for e in ("tensor", "vector", "scalar", "gpsimd", "sync")}
        self.sems = []
        self.bank_free = {}
        self.buf_free = {}

    def sem(self, name):
        s = Sem(self.nc, self.stack, name)
        self.sems.append(s)
        return s

    def op(self, eng, f, inc=None, k=1):
        ss = getattr(self, "self_serial", {}).get(eng)
        if ss is not None and ss.n > 0:
            self.wait(eng, ss, ss.n)
        if inc is not None:
            inc.n += k
            self.q[eng].append(lambda e, f=f, s=inc.h, k=k: f(e).then_inc(s, k))
        else:
            self.q[eng].append(lambda e, f=f: f(e))

    def wait(self, eng, sem, val=None):
        v = sem.n if val is None else val
        if v <= 0:
            return
        self.q[eng].append(lambda e, s=sem.h, v=v: e.wait_ge(s, v))

    def barrier(self):
        snaps = [(s, s.n) for s in self.sems]
        for e in ("tensor", "vector", "scalar"):
            for s, v in snaps:
                if v > 0:
                    self.wait(e, s, v)

    def bwait(self, eng, key):
        if key in self.bank_free:
            s, v = self.bank_free[key]
            self.wait(eng, s, v)

    def bread(self, key, sem):
        self.bank_free[key] = (sem, sem.n)

    def fwait(self, eng, key):
        if key in self.buf_free:
            s, v = self.buf_free[key]
            self.wait(eng, s, v)

    def fread(self, key, sem):
        self.buf_free[key] = (sem, sem.n)


def build_program():
    _uid = [0]
    b = Builder()
    nc = b.nc
    O, W = b.op, b.wait

    din = lambda n, sh, dt: nc.dram_tensor(n, sh, dt, kind="ExternalInput").ap()
    X = din("X", [C, NTOK], F32)
    WqkvT = din("WqkvT", [12, 16, 128, 512], F32)
    bqkv2 = din("bqkv2", [128, 48], F32)
    Wo0T = din("Wo0T", [4, 16, 128, 512], F32)
    bo2 = din("bo2", [128, 16], F32)
    G0w = din("G0w", [12, 16, 128, 512], F32)
    U0w = din("U0w", [12, 16, 128, 512], F32)
    D0w = din("D0w", [4, 48, 128, 512], F32)
    g0s = din("g0s", [128, 48], F32)
    u0s = din("u0s", [128, 48], F32)
    d0s = din("d0s", [128, 16], F32)
    Wq1T = din("Wq1T", [4, 16, 128, 512], F32)
    Wk1T = din("Wk1T", [4, 16, 128, 512], F32)
    Wv1T = din("Wv1T", [4, 16, 128, 512], F32)
    Wo1T = din("Wo1T", [4, 16, 128, 512], F32)
    GAMB = din("GAMB", [128, 512], F32)
    BETB = din("BETB", [128, 512], F32)
    G1w = din("G1w", [16, 16, 128, 512], I8)
    U1w = din("U1w", [16, 16, 128, 512], I8)
    D1w = din("D1w", [4, 64, 128, 512], I8)
    g1s = din("g1s", [128, 64], F32)
    u1s = din("u1s", [128, 64], F32)
    d1s = din("d1s", [128, 16], F32)
    COS = din("COS", [128, NTOK], F32)
    SIN = din("SIN", [128, NTOK], F32)
    MSK = din("MSK", [16, 128, 512], F32)
    RT = din("RT", [128, 128], F32)
    IDN = din("IDN", [128, 128], F32)
    OUT = nc.dram_tensor("OUT", [C, NTOK], F32, kind="ExternalOutput").ap()
    X1 = nc.dram_tensor("X1", [C, NTOK], F32).ap()
    GD = nc.dram_tensor("GD", [C, NTOK], BF16).ap()
    KVO = nc.dram_tensor("KVO", [KV_BLK], BF16)
    KVG = nc.dram_tensor("KVG", [2 * KV_BLK], BF16)

    def kvo_k(q, h):
        off = q * KV_Q + h * (128 * 512)
        return KVO.ap()[off:off + 128 * 512].rearrange("(p t) -> p t", p=128)

    def kvo_v(q, h, tc):
        off = q * KV_Q + KV_KQ + h * (4 * 128 * 128) + tc * (128 * 128)
        return KVO.ap()[off:off + 128 * 128].rearrange("(p t) -> p t", p=128)

    def kvg_k(gq, h):
        r, qq = GQ[gq]
        off = r * KV_BLK + qq * KV_Q + h * (128 * 512)
        return KVG.ap()[off:off + 128 * 512].rearrange("(p t) -> p t", p=128)

    def kvg_v(gq, h):
        r, qq = GQ[gq]
        off = r * KV_BLK + qq * KV_Q + KV_KQ + h * (4 * 128 * 128)
        return KVG.ap()[off:off + 4 * 128 * 128].rearrange("(tc p t) -> p tc t", p=128, tc=4)

    sb = nc.alloc_sbuf_tensor
    xn = sb("xn", [128, FC, NTOK], BF16)
    wXt = sb("wX", [128, 16, 512], BF16)
    wYt = sb("wY", [128, 16, 512], BF16)
    cosb = sb("cosb", [128, NTOK], F32)
    sinb = sb("sinb", [128, NTOK], F32)
    rtb = sb("rtb", [128, 128], BF16)
    idnb = sb("idnb", [128, 128], BF16)
    onesb = sb("onesb", [1, 128], BF16)
    onesf = sb("onesf", [1, 128], F32)
    onescol = sb("onescol", [128, 1], BF16)
    sc_a = sb("sc_a", [128, 1024], F32)
    sc_b = sb("sc_b", [128, 512], F32)
    sc_c = sb("sc_c", [128, 2, 512], BF16)
    gstg = sb("gstg", [128, 4, 512], BF16)
    sc_d = sb("sc_d", [128, 2, 512], BF16)
    pp = sb("pp", [128, 3, 512], BF16)
    pm = sb("pm", [128, 2, 512], BF16)
    rz = sb("rz", [1, 512], F32)
    rzb = sb("rzb", [1, 512], BF16)
    rzt = sb("rzt", [128, 512], BF16)
    bqs = sb("bqs", [128, 16], F32)
    expsh = sb("expsh", [128, 1], F32)
    gamb = sb("gamb", [128, 512], F32)
    betb = sb("betb", [128, 512], F32)
    st1 = sb("st1", [128, 4], F32)
    st2 = sb("st2", [128, 4], F32)
    stm = sb("stm", [128, 4], F32)
    str_ = sb("str", [128, 4], F32)
    fmy = sb("fmy", [128, 512], F32)
    fmo = sb("fmo", [128, 512], BF16)
    xout = sb("xout", [128, 512], F32)
    bsc = sb("bsc", [128, 48 * 3 + 16 * 3 + 64 * 2], F32)

    ps = nc.alloc_psum_tensor
    pA = ps("pA", [128, 512], F32)
    pB = ps("pB", [128, 512], F32)
    pC = ps("pC", [128, 512], F32)
    pD = ps("pD", [128, 512], F32)
    pE = ps("pE", [128, 512], F32)
    pF = ps("pF", [128, 512], F32)
    pZ = ps("pZ", [128, 512], F32)
    PK = {0: (pA, "pA"), 1: (pB, "pB")}

    dC = b.sem("dC")
    dCg = b.sem("dCg")
    dX = [b.sem("dX0"), b.sem("dX1")]
    dWX, dWY = b.sem("dWX"), b.sem("dWY")
    dWZ, dWW = b.sem("dWZ"), b.sem("dWW")
    dKV = [b.sem("dKV0"), b.sem("dKV1")]
    dKVW = b.sem("dKVW")
    dVW = b.sem("dVW")
    dOUT = b.sem("dOUT")
    dGW = [b.sem("dGW0"), b.sem("dGW1")]
    tT = b.sem("tT")
    vV = b.sem("vV")
    sS = b.sem("sS")
    cc = b.sem("cc")
    gG = b.sem("gG")
    b.self_serial = {"vector": vV, "scalar": sS, "gpsimd": gG}
    WSEM = {id(wXt): dWX, id(wYt): dWY}

    def dma_gate(*engs):
        for eng in engs:
            for s in (tT, vV, sS):
                W(eng, s)

    def dma(eng, out, in_, sem):
        O(eng, lambda e, o=out, i=in_: e.dma_start(out=o, in_=i), sem, 16)
        return sem.n

    dma("gpsimd", rtb[:, :], RT, dCg)
    dma("gpsimd", idnb[:, :], IDN, dCg)
    dma("sync", cosb[:, :], COS, dC)
    dma("sync", sinb[:, :], SIN, dC)
    dma("sync", gamb[:, :], GAMB, dC)
    dma("sync", betb[:, :], BETB, dC)
    boff = {}
    _o = 0
    for nm, ap_, w_ in (("bqkv2", bqkv2, 48), ("g0s", g0s, 48), ("u0s", u0s, 48),
                        ("d0s", d0s, 16), ("bo2", bo2, 16), ("d1s", d1s, 16),
                        ("g1s", g1s, 64), ("u1s", u1s, 64)):
        boff[nm] = _o
        dma("sync", bsc[:, _o:_o + w_], ap_, dC)
        _o += w_

    def bcol(nm, j):
        return bsc[:, boff[nm] + j:boff[nm] + j + 1]

    W("vector", dC)
    W("vector", dCg)
    O("vector", lambda e: e.memset(onesb[:, :], 1.0))
    O("vector", lambda e: e.memset(onesf[:, :], 1.0))
    O("vector", lambda e: e.memset(expsh[:, :], EXP_SHIFT))
    O("vector", lambda e: e.memset(onescol[:, :], 1.0))
    O("vector", lambda e: e.tensor_scalar_mul(
        bqs[:, :], bsc[:, boff["bqkv2"]:boff["bqkv2"] + 16], RS), vV, 1)
    b.barrier()

    def wload(WSRC, grp, wbuf, view=None):
        sem = WSEM[id(wbuf)]
        b.fwait("gpsimd", ("w", id(wbuf)))
        dst = view if view is not None else wbuf[:, :, :]
        return dma("gpsimd", dst, WSRC[grp].rearrange("c p t -> p c t"), sem), sem

    def phase_ln(L):
        XSRC = X if L == 0 else X1
        XR = XSRC.rearrange("(c p) t -> p c t", p=128)
        if L == 1:
            W("sync", dOUT)
        from contextlib import ExitStack
        _uid[0] += 1
        u = _uid[0]
        with ExitStack() as es:
            xsg = es.enter_context(nc.sbuf_tensor(f"xs{u}", [128, 2, 4, 1024], F32))
            xqg = es.enter_context(nc.sbuf_tensor(f"xq{u}", [128, 2, 1024], BF16))
            xqs = es.enter_context(nc.sbuf_tensor(f"xsq{u}", [128, 2, 1024], BF16))
            mrow = es.enter_context(nc.sbuf_tensor(f"mrow{u}", [1, 1024], F32))
            rrow = es.enter_context(nc.sbuf_tensor(f"rrow{u}", [1, 1024], F32))
            trow = es.enter_context(nc.sbuf_tensor(f"trow{u}", [1, 1024], F32))
            t2row = es.enter_context(nc.sbuf_tensor(f"t2row{u}", [1, 1024], F32))
            xs = xsg
            dma_gate("sync")
            accs = [(pC, "pC"), (pD, "pD"), (pE, "pE"), (pF, "pF")]
            for a, k in accs:
                b.bwait("tensor", k)
            for grp in range(4):
                p = grp % 2
                b.fwait("sync", ("xs", p))
                b.fwait("sync", ("xs", p, "v"))
                snap = dma("sync", xs[:, p, :, :], XR[:, grp * 4:grp * 4 + 4, :], dX[p])
                for j in range(4):
                    c = grp * 4 + j
                    src = xs[:, p, j, :]
                    W("vector", dX[p], snap)
                    b.fwait("vector", ("xq", c % 2))
                    O("vector", lambda e, o=xqg[:, c % 2, :], i=src: e.tensor_copy(o, i), vV, 1)
                    W("scalar", dX[p], snap)
                    b.fwait("scalar", ("xsq", c % 2))
                    O("scalar", lambda e, o=xqs[:, c % 2, :], i=src:
                      e.activation(o, i, AF.Square), sS, 1)
                    W("tensor", vV)
                    W("tensor", sS)
                    for half in range(2):
                        hs = slice(half * 512, half * 512 + 512)
                        O("tensor", lambda e, o=accs[half][0][0:1, :], l=onescol[:, :],
                          r=xqg[:, c % 2, hs], s=(c == 0), p_=(c == 15):
                          e.matmul(o, l, r, start=s, stop=p_), tT, 1)
                        O("tensor", lambda e, o=accs[2 + half][0][0:1, :], l=onescol[:, :],
                          r=xqs[:, c % 2, hs], s=(c == 0), p_=(c == 15):
                          e.matmul(o, l, r, start=s, stop=p_), tT, 1)
                    b.fread(("xq", c % 2), tT)
                    b.fread(("xsq", c % 2), tT)
                b.fread(("xs", p), sS)
                b.fread(("xs", p, "v"), vV)
            W("scalar", tT)
            for half in range(2):
                hs = slice(half * 512, half * 512 + 512)
                O("scalar", lambda e, o=trow[0:1, hs], i=accs[half][0][0:1, :]:
                  e.copy(o, i), sS, 1)
                O("scalar", lambda e, o=t2row[0:1, hs], i=accs[2 + half][0][0:1, :]:
                  e.copy(o, i), sS, 1)
            for a, k in accs:
                b.bread(k, sS)
            W("vector", sS)
            O("vector", lambda e: e.tensor_scalar_mul(mrow[:, :], trow[:, :], 1.0 / 2048.0), vV, 1)
            O("vector", lambda e: e.tensor_scalar(t2row[:, :], t2row[:, :], 1.0 / 2048.0, EPS_LN,
                                                  ALU.mult, ALU.add), vV, 1)
            O("vector", lambda e: e.scalar_tensor_tensor(trow[:, :], mrow[:, :], 1.0, mrow[:, :],
                                                         ALU.mult, ALU.mult), vV, 1)
            O("vector", lambda e: e.tensor_tensor(t2row[:, :], t2row[:, :], trow[:, :],
                                                  ALU.subtract), vV, 1)
            O("vector", lambda e: e.reciprocal(t2row[:, :], t2row[:, :]), vV, 1)
            W("scalar", vV)
            O("scalar", lambda e: e.activation(rrow[:, :], t2row[:, :], AF.Sqrt), sS, 1)
            for grp in range(4):
                p = grp % 2
                b.fwait("sync", ("xs", p))
                b.fwait("sync", ("xs", p, "v"))
                snap = dma("sync", xs[:, p, :, :], XR[:, grp * 4:grp * 4 + 4, :], dX[p])
                for half in range(2):
                    hs = slice(half * 512, half * 512 + 512)
                    pk, bank = PK[half]
                    pk2, bank2 = [(pE, "pE"), (pF, "pF")][half]
                    if grp == 0:
                        W("tensor", sS)
                        b.bwait("tensor", bank)
                        O("tensor", lambda e, o=pk[:, :], r=mrow[0:1, hs]:
                          e.matmul(o, onesf[0:1, :], r, start=True, stop=True), tT, 1)
                        b.bwait("tensor", bank2)
                        O("tensor", lambda e, o=pk2[:, :], r=rrow[0:1, hs]:
                          e.matmul(o, onesf[0:1, :], r, start=True, stop=True), tT, 1)
                    for j in range(4):
                        c = grp * 4 + j
                        W("vector", dX[p], snap)
                        W("vector", tT)
                        b.fwait("vector", ("lntmp",))
                        O("vector", lambda e, o=sc_a[:, 0:512], i=xs[:, p, j, hs], m=pk[:, :]:
                          e.tensor_tensor(o, i, m, ALU.subtract), vV, 1)
                        O("vector", lambda e, o=xn[:, c, hs], i=sc_a[:, 0:512], m=pk2[:, :]:
                          e.tensor_tensor(o, i, m, ALU.mult), vV, 1)
                        b.fread(("lntmp",), vV)
                b.fread(("xs", p), vV)
                b.fread(("xs", p, "v"), vV)
            b.bread("pA", vV)
            b.bread("pB", vV)
            b.bread("pE", vV)
            b.bread("pF", vV)
            b.barrier()

    def proj_rope(grps, kind, qdst=None, slot=None):
        pending = [None]

        def emit_fin():
            if pending[0] is not None:
                pending[0]()
                pending[0] = None

        for gi, grp in enumerate(grps):
            wbuf = (wXt, wYt)[gi % 2]
            snap, wsem = wload(WqkvT, grp, wbuf)
            for hh in range(4):
                h = gi * 4 + hh
                halves = range(2) if slot is None else [slot]
                for half in halves:
                    hs = slice(half * 512, half * 512 + 512)
                    pk, bank = PK[half % 2]
                    b.bwait("tensor", bank)
                    W("tensor", wsem, snap)
                    for kc in range(FC):
                        O("tensor", lambda e, o=pk[:, :], l=wbuf[:, kc, hh * 128:hh * 128 + 128],
                          r=xn[:, kc, hs], s=(kc == 0), p_=(kc == 15):
                          e.matmul(o, l, r, start=s, stop=p_), tT, 1)
                    b.fread(("w", id(wbuf)), tT)
                    emit_fin()
                    W("scalar", tT)
                    b.fwait("scalar", ("praw", half))
                    if kind == "q":
                        O("scalar", lambda e, o=sc_a[:, hs], i=pk[:, :], bb=bqs[:, h:h + 1]:
                          e.activation(o, i, AF.Identity, bias=bb, scale=RS), sS, 1)
                    else:
                        O("scalar", lambda e, o=sc_a[:, hs], i=pk[:, :],
                          bb=bcol("bqkv2", 16 + h): e.activation(o, i, AF.Identity, bias=bb), sS, 1)
                    b.bread(bank, sS)
                    W("vector", sS)
                    tile = (h * 2 + half) % 2
                    b.fwait("vector", ("pp", tile))
                    b.fwait("vector", ("pp2", tile))
                    O("vector", lambda e, o=pp[:, tile, :], i=sc_a[:, hs], m=sinb[:, hs]:
                      e.tensor_tensor(o, i, m, ALU.mult), vV, 1)
                    O("vector", lambda e, o=pm[:, tile, :], i=sc_a[:, hs], m=cosb[:, hs]:
                      e.tensor_tensor(o, i, m, ALU.mult), vV, 1)
                    b.fread(("praw", half), vV)
                    vsnap = vV.n

                    def fin(h=h, half=half, hs=hs, vsnap=vsnap, tile=tile):
                        b.bwait("tensor", "pF")
                        W("tensor", vV, vsnap)
                        O("tensor", lambda e, o=pF[:, :], l=rtb[:, :], r=pp[:, tile, :]:
                          e.matmul(o, l, r, start=True, stop=False), tT, 1)
                        O("tensor", lambda e, o=pF[:, :], l=idnb[:, :], r=pm[:, tile, :]:
                          e.matmul(o, l, r, start=False, stop=True), tT, 1)
                        b.fread(("pp", tile), tT)
                        b.fread(("pp2", tile), tT)
                        W("scalar", tT)
                        if kind == "q":
                            O("scalar", lambda e, o=qdst[:, h, 0:512], i=pF[:, :]: e.copy(o, i), sS, 1)
                            b.bread("pF", sS)
                        else:
                            b.fwait("scalar", ("kst",))
                            O("scalar", lambda e, o=rzt[:, :], i=pF[:, :]: e.copy(o, i), sS, 1)
                            b.bread("pF", sS)
                            W("sync", sS)
                            dma("sync", kvo_k(half, h), rzt[:, :], dKVW)
                            b.fread(("kst",), dKVW)
                    pending[0] = fin
        emit_fin()

    def proj_tm(WSRC, grps, consume, slot=None):
        tcs = range(8) if slot is None else range(slot * 4, slot * 4 + 4)
        pending = [None]
        for gi, grp in enumerate(grps):
            wbuf = (wXt, wYt)[gi % 2]
            snap, wsem = wload(WSRC, grp, wbuf)
            for tc in tcs:
                pk, bank = PK[tc % 2]
                b.bwait("tensor", bank)
                W("tensor", wsem, snap)
                for kc in range(FC):
                    O("tensor", lambda e, o=pk[:, :], l=xn[:, kc, tc * 128:tc * 128 + 128],
                      r=wbuf[:, kc, :], s=(kc == 0), p_=(kc == 15):
                      e.matmul(o, l, r, start=s, stop=p_), tT, 1)
                b.fread(("w", id(wbuf)), tT)
                if pending[0] is not None:
                    pending[0]()
                pending[0] = consume(gi, tc, pk, bank)
        if pending[0] is not None:
            pending[0]()

    def v_consume(gi, tc, pk, bank):
        half, tcq = tc // 4, tc % 4
        W("scalar", tT)
        b.fwait("scalar", ("vst",))
        O("scalar", lambda e, o=sc_d[:, 1, :], i=pk[:, :]: e.copy(o, i), sS, 1)
        b.bread(bank, sS)
        W("sync", sS)
        for hh in range(4):
            h = gi * 4 + hh
            dma("sync", kvo_v(half, h, tcq), sc_d[:, 1, hh * 128:hh * 128 + 128], dVW)
        b.fread(("vst",), dVW)
        return None

    def fm_consume_mk(is_q, qdst=None, pT=None):
        def consume(gi, tc, pk, bank):
            half, tcq = tc // 4, tc % 4
            W("vector", tT)
            b.fwait("vector", ("fmy",))
            O("vector", lambda e, o=fmy[:, :], i=pk[:, :], m=gamb[:, :]:
              e.tensor_tensor(o, i, m, ALU.mult), vV, 1)
            O("vector", lambda e, o=fmy[:, :], i=fmy[:, :], m=betb[:, :]:
              e.tensor_tensor(o, i, m, ALU.add), vV, 1)
            b.bread(bank, vV)
            O("vector", lambda e, o=st1[:, :],
              i=fmy.ap().rearrange("p (h d) -> p h d", d=128):
              e.reduce_sum(o, i, axis=AX.X), vV, 1)
            W("scalar", vV)
            b.fwait("scalar", ("fmsq",))
            O("scalar", lambda e, o=sc_b[:, 0:512], i=fmy[:, :]:
              e.activation(o, i, AF.Square), sS, 1)
            W("vector", sS)
            O("vector", lambda e, o=st2[:, :],
              i=sc_b.ap()[:, 0:512].rearrange("p (h d) -> p h d", d=128):
              e.reduce_sum(o, i, axis=AX.X), vV, 1)
            b.fread(("fmsq",), vV)
            O("vector", lambda e: e.tensor_scalar_mul(stm[:, :], st1[:, :], 1.0 / 128.0), vV, 1)
            O("vector", lambda e: e.tensor_scalar(str_[:, :], st2[:, :], 1.0 / 128.0, EPS_L1,
                                                  ALU.mult, ALU.add), vV, 1)
            O("vector", lambda e: e.scalar_tensor_tensor(st1[:, :], stm[:, :], 1.0, stm[:, :],
                                                         ALU.mult, ALU.mult), vV, 1)
            O("vector", lambda e: e.tensor_tensor(str_[:, :], str_[:, :], st1[:, :],
                                                  ALU.subtract), vV, 1)
            O("vector", lambda e: e.reciprocal(str_[:, :], str_[:, :]), vV, 1)
            W("scalar", vV)
            O("scalar", lambda e: e.activation(str_[:, :], str_[:, :], AF.Sqrt), sS, 1)
            if is_q:
                W("vector", sS)
                O("vector", lambda e: e.tensor_scalar_mul(str_[:, :], str_[:, :], RS), vV, 1)
            for hh in range(4):
                W("vector", sS)
                b.fwait("vector", ("fmo", hh))
                O("vector", lambda e, o=fmo[:, hh * 128:hh * 128 + 128],
                  i=fmy[:, hh * 128:hh * 128 + 128], m1=stm[:, hh:hh + 1],
                  m2=str_[:, hh:hh + 1]:
                  e.tensor_scalar(o, i, m1, m2, ALU.subtract, ALU.mult), vV, 1)
            b.fread(("fmy",), vV)
            vsnap = vV.n

            def fin(gi=gi, tc=tc, half=half, tcq=tcq, vsnap=vsnap):
                for hh in range(4):
                    h = gi * 4 + hh
                    b.bwait("tensor", "pT")
                    W("tensor", vV, vsnap)
                    O("tensor", lambda e, o=pT[:, 0:128], i=fmo[:, hh * 128:hh * 128 + 128]:
                      e.transpose(o, i, idnb[:, :]), tT, 1)
                    if hh == 3:
                        b.fread(("fmo", 0), tT)
                        b.fread(("fmo", 1), tT)
                        b.fread(("fmo", 2), tT)
                        b.fread(("fmo", 3), tT)
                    W("scalar", tT)
                    if is_q:
                        tl = tc % 4
                        O("scalar", lambda e, o=qdst[:, h, tl * 128:tl * 128 + 128],
                          i=pT[:, 0:128]: e.copy(o, i), sS, 1)
                        b.bread("pT", sS)
                    else:
                        b.fwait("scalar", ("kst",))
                        O("scalar", lambda e, o=rzt[:, 0:128], i=pT[:, 0:128]: e.copy(o, i), sS, 1)
                        b.bread("pT", sS)
                        W("sync", sS)
                        dma("sync",
                            kvo_k(half, h).rearrange("p (a t) -> p a t", a=4)[:, tcq, :],
                            rzt[:, 0:128], dKVW)
                        b.fread(("kst",), dKVW)
            return fin
        return consume

    def phase_ffn(L):
        if L == 0:
            NMG, NKD, WG, WU, WD = 12, 48, G0w, U0w, D0w
            gsn, usn, dsn = "g0s", "u0s", "d0s"
        else:
            NMG, NKD, WG, WU, WD = 16, 64, G1w, U1w, D1w
            gsn, usn, dsn = "g1s", "u1s", "d1s"
        nseg = NKD // 16
        GDr = GD.rearrange("(c p) t -> p c t", p=128)
        from contextlib import ExitStack
        _uid[0] += 1
        with ExitStack() as es:
            tokw = 1024 if L == 0 else 512
            fuse = es.enter_context(nc.sbuf_tensor(f"fuse{_uid[0]}", [128, NKD, tokw], BF16))
            if L == 0:
                tiles = [wXt, wYt]
                views = [t.ap().bitcast(FP8)[:, :, 0:512] for t in tiles]
                pG = es.enter_context(nc.psum_tensor(f"pG{_uid[0]}", [128, 512], F32))
            else:
                wZt = es.enter_context(nc.sbuf_tensor(f"wZ{_uid[0]}", [128, 16, 512], BF16))
                wWt = es.enter_context(nc.sbuf_tensor(f"wW{_uid[0]}", [128, 16, 512], BF16))
                WSEM[id(wZt)] = dWZ
                WSEM[id(wWt)] = dWW
                tiles = [wXt, wYt, wZt, wWt]
                views = [t[:, :, :] for t in tiles]

            def mk_loader(srcs):
                snaps = [None] * len(srcs)
                emitted = [0]

                nb = len(tiles)
                def ensure(i):
                    while emitted[0] <= min(i + (nb - 2), len(srcs) - 1):
                        j = emitted[0]
                        t = tiles[j % nb]
                        b.fwait("gpsimd", ("w", id(t)))
                        snaps[j] = dma("gpsimd", views[j % nb],
                                       srcs[j].rearrange("c p t -> p c t"), WSEM[id(t)])
                        emitted[0] += 1
                return snaps, ensure

            halves = [0] if L == 0 else [0, 1]
            for hpass in halves:
                srcs = []
                for mg in range(NMG):
                    srcs.append(WG[mg])
                    srcs.append(WU[mg])
                for fg in range(4):
                    WDr = WD[fg].rearrange("(s c) p t -> s c p t", c=16)
                    for seg in range(nseg):
                        srcs.append(WDr[seg])
                snaps, ensure = mk_loader(srcs)
                tok_halves = [0, 1] if L == 0 else [hpass]
                li = 0
                for mg in range(NMG):
                    ensure(li + 1)
                    for mc in range(4):
                        m = mg * 4 + mc
                        for hi, half in enumerate(tok_halves):
                            hs = slice(half * 512, half * 512 + 512)
                            fs = slice(hi * 512, hi * 512 + 512)
                            slot = (m % 2) * 2 + hi
                            for (scn, func, banks, idx) in (
                                    (gsn, AF.Silu, ((pA, "pA"), (pB, "pB")), li),
                                    (usn, AF.Copy, ((pC, "pC"), (pD, "pD")), li + 1)):
                                wv = views[idx % len(tiles)]
                                wt = tiles[idx % len(tiles)]
                                pk, bank = banks[hi % len(banks)]
                                b.bwait("tensor", bank)
                                W("tensor", WSEM[id(wt)], snaps[idx])
                                for kc in range(FC):
                                    O("tensor", lambda e, o=pk[:, :],
                                      l=wv[:, kc, mc * 128:mc * 128 + 128],
                                      r=xn[:, kc, hs], s=(kc == 0), p_=(kc == 15):
                                      e.matmul(o, l, r, start=s, stop=p_), tT, 1)
                                b.fread(("w", id(wt)), tT)
                                W("scalar", tT)
                                if func is AF.Silu:
                                    b.fwait("scalar", ("gstg", slot))
                                    O("scalar", lambda e, o=gstg[:, slot, :], i=pk[:, :],
                                      sc=bcol(scn, m): e.activation(o, i, AF.Silu, scale=sc),
                                      sS, 1)
                                    b.bread(bank, sS)
                                else:
                                    b.fwait("scalar", ("gu", hi))
                                    O("scalar", lambda e, o=sc_d[:, hi, :], i=pk[:, :],
                                      sc=bcol(scn, m): e.activation(o, i, AF.Copy, scale=sc),
                                      sS, 1)
                                    b.bread(bank, sS)
                                    W("vector", sS)
                                    b.fwait("vector", ("fuse", m, hi))
                                    O("vector", lambda e, o=fuse[:, m, fs], a=gstg[:, slot, :],
                                      bb=sc_d[:, hi, :]: e.tensor_tensor(o, a, bb, ALU.mult),
                                      vV, 1)
                                    b.fread(("gstg", slot), vV)
                                    b.fread(("gu", hi), vV)
                    li += 2
                if L == 0:
                    dbanks = [(pA, "pA"), (pB, "pB"), (pC, "pC"), (pD, "pD"),
                              (pE, "pE"), (pF, "pF"), (pZ, "pZ"), (pG, "pG")]
                else:
                    dbanks = [(pA, "pA"), (pB, "pB"), (pC, "pC"), (pD, "pD")]
                nunit = len(tok_halves)
                for fg in range(4):
                    for _, kk in dbanks:
                        b.bwait("tensor", kk)
                    for seg in range(nseg):
                        idx = li + seg
                        ensure(idx)
                        wv = views[idx % len(tiles)]
                        wt = tiles[idx % len(tiles)]
                        W("tensor", WSEM[id(wt)], snaps[idx])
                        if seg == 0:
                            W("tensor", vV)
                        for kc in range(16):
                            k = seg * 16 + kc
                            for fo4 in range(4):
                                for hi in range(nunit):
                                    fs = slice(hi * 512, hi * 512 + 512)
                                    dbk = dbanks[fo4 * nunit + hi][0]
                                    O("tensor", lambda e, o=dbk[:, :],
                                      l=wv[:, kc, fo4 * 128:fo4 * 128 + 128],
                                      r=fuse[:, k, fs],
                                      s=(k == 0), p_=(k == NKD - 1):
                                      e.matmul(o, l, r, start=s, stop=p_), tT, 1)
                        b.fread(("w", id(wt)), tT)
                    li += nseg
                    W("scalar", tT)
                    for fo4 in range(4):
                        fo = fg * 4 + fo4
                        for hi, half in enumerate(tok_halves):
                            hs = slice(half * 512, half * 512 + 512)
                            ui = (fo4 * nunit + hi) % 2
                            b.fwait("scalar", ("pp", ui))
                            O("scalar", lambda e, o=pp[:, ui, :],
                              i=dbanks[fo4 * nunit + hi][0][:, :],
                              sc=bcol(dsn, fo): e.activation(o, i, AF.Silu, scale=sc), sS, 1)
                            W("sync", sS)
                            dma("sync", GDr[:, fo, hs], pp[:, ui, :], dGW[ui])
                            b.fread(("pp", ui), dGW[ui])
                    for _, kk in dbanks:
                        b.bread(kk, sS)
                for m in range(NKD):
                    for hi in range(nunit):
                        b.fread(("fuse", m, hi), tT)
            b.barrier()

    def phase_attn(L, qbuf, obuf, slot, mskb, kbufs, vbufs):
        nch = 8 if slot == 0 else 16
        qs = slice(slot * 512, slot * 512 + 512)
        quarters = [0, 1] if slot == 0 else [0, 1, 2, 3]
        rot = [(pB, "pB"), (pC, "pC"), (pD, "pD")]
        rot_i = [0]

        def next_rot():
            r = rot[rot_i[0] % 3]
            rot_i[0] += 1
            return r

        def issue_kv(h):
            p = h % 2
            b.fwait("sync", ("kv", p))
            for ji, gq in enumerate(quarters):
                dma("sync", kbufs[p][:, ji, :], kvg_k(gq, h), dKV[p])
                dma("sync", vbufs[p][:, ji * 4:ji * 4 + 4, :], kvg_v(gq, h), dKV[p])
            return dKV[p].n

        pend_norm = [None]

        def emit_norm():
            if pend_norm[0] is not None:
                pend_norm[0]()
                pend_norm[0] = None

        kvsnap = issue_kv(0)
        for h in range(H):
            p = h % 2
            mysnap = kvsnap
            if h + 1 < H:
                kvsnap = issue_kv(h + 1)
            ob, obk = (pE, "pE") if h % 2 == 0 else (pA, "pA")
            zb, zbk = (pZ, "pZ") if h % 2 == 0 else (pF, "pF")
            W("tensor", dKV[p], mysnap)
            stage2 = []

            def pop_one():
                c2, pS2, bankS2, tsnap = stage2.pop(0)
                ppj = c2 % 3
                masked = True if slot == 0 else (c2 >= 8)
                mski = slot * 8 + (c2 if slot == 0 else c2 - 8)
                W("scalar", tT, tsnap)
                b.fwait("scalar", ("pp", ppj))
                if L == 0:
                    O("scalar", lambda e, o=pp[:, ppj, :], i=pS2[:, :]:
                      e.activation(o, i, AF.Exp, bias=expsh[:, 0:1]), sS, 1)
                else:
                    O("scalar", lambda e, o=pp[:, ppj, :], i=pS2[:, :]:
                      e.activation(o, i, AF.Square), sS, 1)
                b.bread(bankS2, sS)
                ssnap = sS.n
                if masked:
                    W("gpsimd", sS, ssnap)
                    b.fwait("gpsimd", ("pmm", c2 % 2))
                    O("gpsimd", lambda e, o=pm[:, c2 % 2, :], i=pp[:, ppj, :],
                      m=mskb[:, mski, :]: e.tensor_tensor(o, i, m, ALU.mult), gG, 1)
                    b.fread(("pp", ppj), gG)
                    src_ = pm[:, c2 % 2, :]
                    W("tensor", gG)
                else:
                    src_ = pp[:, ppj, :]
                    W("tensor", sS, ssnap)
                if c2 == 0:
                    b.bwait("tensor", obk)
                    b.bwait("tensor", zbk)
                O("tensor", lambda e, o=ob[:, :], l=vbufs[p][:, c2, :], r=src_,
                  s=(c2 == 0), p_=(c2 == nch - 1):
                  e.matmul(o, l, r, start=s, stop=p_), tT, 1)
                O("tensor", lambda e, o=zb[0:1, :], l=onescol[:, :], r=src_,
                  s=(c2 == 0), p_=(c2 == nch - 1):
                  e.matmul(o, l, r, start=s, stop=p_), tT, 1)
                if masked:
                    b.fread(("pmm", c2 % 2), tT)
                else:
                    b.fread(("pp", ppj), tT)

            for c in range(nch + 2):
                if stage2 and (len(stage2) == 2 or c >= nch):
                    pop_one()
                if c == 2:
                    emit_norm()
                if c < nch:
                    pS, bankS = next_rot()
                    b.bwait("tensor", bankS)
                    O("tensor", lambda e, o=pS[:, :],
                      l=kbufs[p][:, c // 4, (c % 4) * 128:(c % 4) * 128 + 128],
                      r=qbuf[:, h, 0:512]: e.matmul(o, l, r, start=True, stop=True), tT, 1)
                    stage2.append((c, pS, bankS, tT.n))
            b.fread(("kv", p), tT)
            tchunks = tT.n

            def norm(h=h, ob=ob, obk=obk, zb=zb, zbk=zbk, tchunks=tchunks):
                W("vector", tT, tchunks)
                b.fwait("vector", ("rz",))
                if L == 0:
                    O("vector", lambda e: e.reciprocal(rz[:, :], zb[0:1, :]), vV, 1)
                else:
                    O("vector", lambda e: e.tensor_scalar_add(rz[:, :], zb[0:1, :], EPS_L1), vV, 1)
                    O("vector", lambda e: e.reciprocal(rz[:, :], rz[:, :]), vV, 1)
                O("vector", lambda e: e.tensor_copy(rzb[:, :], rz[:, :]), vV, 1)
                b.bread(zbk, vV)
                po, pok = next_rot()
                b.bwait("tensor", pok)
                W("tensor", vV)
                O("tensor", lambda e, o=po[:, :]: e.matmul(o, onesb[:, :], rzb[:, :],
                                                           start=True, stop=True), tT, 1)
                W("scalar", tT)
                b.fwait("scalar", ("kst",))
                O("scalar", lambda e, o=rzt[:, :], i=po[:, :]: e.copy(o, i), sS, 1)
                b.bread(pok, sS)
                W("vector", sS)
                if L == 0:
                    b.fwait("vector", ("onrm",))
                    O("vector", lambda e, o=sc_c[:, 0, :], a=ob[:, :], bb=rzt[:, :]:
                      e.tensor_tensor(o, a, bb, ALU.mult), vV, 1)
                    O("vector", lambda e, o=obuf[:, h, qs], i=sc_c[:, 0, :],
                      bb=bcol("bqkv2", 32 + h): e.tensor_scalar_add(o, i, bb), vV, 1)
                    b.fread(("onrm",), vV)
                else:
                    O("vector", lambda e, o=obuf[:, h, qs], a=ob[:, :], bb=rzt[:, :]:
                      e.tensor_tensor(o, a, bb, ALU.mult), vV, 1)
                b.bread(obk, vV)
                b.fread(("rz",), vV)
                b.fread(("kst",), vV)
            pend_norm[0] = norm
        emit_norm()

    def phase_wo(L, obuf):
        XSRC = X if L == 0 else X1
        XR = XSRC.rearrange("(c p) t -> p c t", p=128)
        GDr = GD.rearrange("(c p) t -> p c t", p=128)
        XD = (X1 if L == 0 else OUT).rearrange("(c p) t -> p c t", p=128)
        WoT = Wo0T if L == 0 else Wo1T
        from contextlib import ExitStack
        _uid[0] += 1
        with ExitStack() as es:
            xf = es.enter_context(nc.sbuf_tensor(f"xf{_uid[0]}", [128, 2, 1024], F32))
            gf = es.enter_context(nc.sbuf_tensor(f"gf{_uid[0]}", [128, 2, 1024], BF16))
            dma_gate("sync")
            W("sync", dGW[0])
            W("sync", dGW[1])
            for grp in range(4):
                wbuf = (wXt, wYt)[grp % 2]
                snap, wsem = wload(WoT, grp, wbuf)
                for hh in range(4):
                    fo = grp * 4 + hh
                    p = fo % 2
                    b.fwait("sync", ("xf", p))
                    xsnap = dma("sync", xf[:, p, :], XR[:, fo, :], dX[p])
                    dma("sync", gf[:, p, :], GDr[:, fo, :], dX[p])
                    xsnap = dX[p].n
                    for half in range(2):
                        hs = slice(half * 512, half * 512 + 512)
                        pk, bank = PK[half]
                        b.bwait("tensor", bank)
                        W("tensor", wsem, snap)
                        for kc in range(FC):
                            O("tensor", lambda e, o=pk[:, :],
                              l=wbuf[:, kc, hh * 128:hh * 128 + 128],
                              r=obuf[:, kc, hs], s=(kc == 0), p_=(kc == 15):
                              e.matmul(o, l, r, start=s, stop=p_), tT, 1)
                        b.fread(("w", id(wbuf)), tT)
                        W("scalar", tT)
                        b.fwait("scalar", ("wos", half))
                        if L == 0:
                            O("scalar", lambda e, o=sc_c[:, half, :], i=pk[:, :],
                              bb=bcol("bo2", fo): e.activation(o, i, AF.Identity, bias=bb), sS, 1)
                        else:
                            O("scalar", lambda e, o=sc_c[:, half, :], i=pk[:, :]:
                              e.copy(o, i), sS, 1)
                        b.bread(bank, sS)
                        W("vector", sS)
                        W("vector", dX[p], xsnap)
                        b.fwait("vector", ("cmb",))
                        O("vector", lambda e, o=sc_b[:, :], i=sc_c[:, half, :],
                          m=gf[:, p, hs]: e.tensor_tensor(o, i, m, ALU.mult), vV, 1)
                        b.fread(("wos", half), vV)
                        b.fwait("vector", ("xout",))
                        O("vector", lambda e, o=xout[:, :], i=sc_b[:, :], m=xf[:, p, hs]:
                          e.tensor_tensor(o, i, m, ALU.add), vV, 1)
                        b.fread(("cmb",), vV)
                        W("sync", vV)
                        dma("sync", XD[:, fo, hs], xout[:, :], dOUT)
                        b.fread(("xout",), dOUT)
                    b.fread(("xf", p), vV)
            b.barrier()

    from contextlib import ExitStack
    for L in range(2):
        esL = ExitStack()
        pT = None
        if L == 1:
            pT = esL.enter_context(nc.psum_tensor("pT1", [128, 1024], BF16))
        phase_ln(L)
        if L == 0:
            proj_rope([4, 5, 6, 7], "k")
            proj_tm(WqkvT, [8, 9, 10, 11], v_consume)
        else:
            proj_tm(Wk1T, [0, 1, 2, 3], fm_consume_mk(False, pT=pT))
            proj_tm(Wv1T, [0, 1, 2, 3], v_consume)
        W("gpsimd", dKVW)
        W("gpsimd", dVW)
        O("gpsimd", lambda e: e.collective_compute(
            "AllGather", ALU.bypass,
            replica_groups=[[0, 1], [2, 3], [4, 5], [6, 7]],
            ins=[KVO.ap().opt()], outs=[KVG.ap().opt()]), cc, 1)
        phase_ffn(L)
        with ExitStack() as es:
            obuf = es.enter_context(nc.sbuf_tensor(f"obuf{L}", [128, FC, NTOK], BF16))
            with ExitStack() as es2:
                qbuf = es2.enter_context(nc.sbuf_tensor(f"qbuf{L}", [128, FC, 512], BF16))
                mskb = es2.enter_context(nc.sbuf_tensor(f"mskb{L}", [128, 16, 512], BF16))
                kbufs = [es2.enter_context(nc.sbuf_tensor(f"kb{L}{p}", [128, 4, 512], BF16))
                         for p in range(2)]
                vbufs = [es2.enter_context(nc.sbuf_tensor(f"vb{L}{p}", [128, 16, 128], BF16))
                         for p in range(2)]
                dma_gate("gpsimd", "sync")
                dma("gpsimd", mskb[:, :, :], MSK.rearrange("c p t -> p c t"), dCg)
                W("sync", cc)
                W("vector", dCg)
                for slot in range(2):
                    if L == 0:
                        proj_rope([0, 1, 2, 3], "q", qbuf, slot)
                    else:
                        proj_tm(Wq1T, [0, 1, 2, 3], fm_consume_mk(True, qbuf, pT=pT), slot)
                    b.barrier()
                    phase_attn(L, qbuf, obuf, slot, mskb, kbufs, vbufs)
                    b.barrier()
            phase_wo(L, obuf)
        esL.close()

    W("gpsimd", dOUT)
    W("gpsimd", dGW[0])
    W("gpsimd", dGW[1])
    W("gpsimd", dKVW)
    W("gpsimd", dVW)
    W("gpsimd", dC)
    W("gpsimd", dCg)
    for s in dX + dKV + [dWX, dWY, dWZ, dWW]:
        W("gpsimd", s)

    with nc.Block() as block:
        @block.tensor
        def _(e):
            for f in b.q["tensor"]:
                f(e)

        @block.vector
        def _(e):
            for f in b.q["vector"]:
                f(e)

        @block.scalar
        def _(e):
            for f in b.q["scalar"]:
                f(e)

        @block.gpsimd
        def _(e):
            for f in b.q["gpsimd"]:
                f(e)

        @block.sync
        def _(e):
            for f in b.q["sync"]:
                f(e)

    b.stack.close()
    return nc


def _blocked(w2d):
    K, M = w2d.shape
    return np.ascontiguousarray(
        w2d.reshape(K // 128, 128, M // 512, 512).transpose(2, 0, 1, 3))


def _chunk_cols(v):
    return np.ascontiguousarray(v.reshape(-1, 128).T)


def _rope_tables(pos):
    p = np.arange(64, dtype=np.float32)
    theta = np.power(np.float32(10000.0),
                     (-(np.float32(2.0) ** p) / np.float32(128.0)).astype(np.float32))
    ang = pos.astype(np.float32)[None, :] * theta[:, None]
    ang = np.concatenate([ang, ang], axis=0)
    return np.cos(ang).astype(np.float32), np.sin(ang).astype(np.float32)


def _rot_matrix():
    R = np.zeros((128, 128), np.float32)
    for i in range(64):
        R[i, i + 64] = -1.0
        R[i + 64, i] = 1.0
    return np.ascontiguousarray(R.T)


def prepare_in_maps(ii):
    x = np.asarray(ii["L0_attn_x"], np.float32)
    shared = {
        "WqkvT": _blocked(np.ascontiguousarray(np.asarray(ii["L0_attn_Wqkv"], np.float32).T)),
        "bqkv2": _chunk_cols(np.asarray(ii["L0_attn_bqkv"], np.float32)),
        "Wo0T": _blocked(np.ascontiguousarray(np.asarray(ii["L0_attn_Wo"], np.float32).T)),
        "bo2": _chunk_cols(np.asarray(ii["L0_attn_bo"], np.float32)),
        "G0w": _blocked(np.asarray(ii["L0_ffn_gate_w"], np.float32)),
        "U0w": _blocked(np.asarray(ii["L0_ffn_up_w"], np.float32)),
        "D0w": _blocked(np.asarray(ii["L0_ffn_down_w"], np.float32)),
        "g0s": _chunk_cols(np.asarray(ii["L0_ffn_gate_scale"], np.float32)),
        "u0s": _chunk_cols(np.asarray(ii["L0_ffn_up_scale"], np.float32)),
        "d0s": _chunk_cols(np.asarray(ii["L0_ffn_down_scale"], np.float32)),
        "Wq1T": _blocked(np.ascontiguousarray(np.asarray(ii["L1_attn_Wq"], np.float32).T)),
        "Wk1T": _blocked(np.ascontiguousarray(np.asarray(ii["L1_attn_Wk"], np.float32).T)),
        "Wv1T": _blocked(np.ascontiguousarray(np.asarray(ii["L1_attn_Wv"], np.float32).T)),
        "Wo1T": _blocked(np.ascontiguousarray(np.asarray(ii["L1_attn_Wo"], np.float32).T)),
        "GAMB": np.ascontiguousarray(
            np.tile(np.asarray(ii["L1_attn_gamma"], np.float32)[None, :], (128, 4))),
        "BETB": np.ascontiguousarray(
            np.tile(np.asarray(ii["L1_attn_beta"], np.float32)[None, :], (128, 4))),
        "G1w": _blocked(np.asarray(ii["L1_ffn_w_gate"])),
        "U1w": _blocked(np.asarray(ii["L1_ffn_w_up"])),
        "D1w": _blocked(np.asarray(ii["L1_ffn_w_down"])),
        "g1s": _chunk_cols(np.asarray(ii["L1_ffn_s_gate"], np.float32)),
        "u1s": _chunk_cols(np.asarray(ii["L1_ffn_s_up"], np.float32)),
        "d1s": _chunk_cols(np.asarray(ii["L1_ffn_s_down"], np.float32)),
        "RT": _rot_matrix(),
        "IDN": np.eye(128, dtype=np.float32),
    }

    in_maps = []
    for c in range(NCORES):
        s, par = c // 2, c % 2
        qE, qL = (0, 3) if par == 0 else (1, 2)
        xc = np.concatenate([x[s, qE * 512:qE * 512 + 512, :],
                             x[s, qL * 512:qL * 512 + 512, :]], axis=0)
        pos = np.concatenate([np.arange(qE * 512, qE * 512 + 512),
                              np.arange(qL * 512, qL * 512 + 512)]) + 1
        cosc, sinc = _rope_tables(pos)
        mskc = np.zeros((16, 128, 512), np.float32)
        for slot in range(2):
            tqb = (qE if slot == 0 else qL) * 512
            for j in range(8):
                cglob = j if slot == 0 else 8 + j
                tk = cglob * 128 + np.arange(128)[:, None]
                tq = tqb + np.arange(512)[None, :]
                mskc[slot * 8 + j] = (tk <= tq).astype(np.float32)
        m = dict(shared)
        m["X"] = np.ascontiguousarray(xc.T)
        m["COS"] = cosc
        m["SIN"] = sinc
        m["MSK"] = mskc
        in_maps.append(m)
    return in_maps


def assemble_out(core_outs):
    out = np.empty((B, T, C), np.float32)
    for c in range(NCORES):
        s, par = c // 2, c % 2
        qE, qL = (0, 3) if par == 0 else (1, 2)
        oc = np.asarray(core_outs[c], np.float32).T
        out[s, qE * 512:qE * 512 + 512, :] = oc[:512]
        out[s, qL * 512:qL * 512 + 512, :] = oc[512:]
    return out


def kernel(**ii):
    in_maps = prepare_in_maps(ii)
    nc = build_program()
    res = run_bass_kernel_spmd(nc, in_maps, core_ids=list(range(NCORES)))
    return assemble_out([res.results[c]["OUT"] for c in range(NCORES)])


# version 05
# speedup vs baseline: 1.0369x; 1.0369x over previous
import numpy as np
import concourse.bass as bass
import concourse.mybir as mybir
from concourse.bass_utils import run_bass_kernel_spmd

F32 = mybir.dt.float32
BF16 = mybir.dt.bfloat16
FP8 = mybir.dt.float8e4
I8 = mybir.dt.int8
AF = mybir.ActivationFunctionType
ALU = mybir.AluOpType
AX = mybir.AxisListType

B, T, C, H = 4, 2048, 2048, 16
FC = C // 128
NTOK = 1024
RS = float(1.0 / np.sqrt(128.0))
EXP_SHIFT = -12.0
EPS_LN = 1e-5
EPS_L1 = 1e-5
NCORES = 8

GQ = [(0, 0), (1, 0), (1, 1), (0, 1)]
KV_KQ = 16 * 128 * 512
KV_VQ = 16 * 4 * 128 * 128
KV_Q = KV_KQ + KV_VQ
KV_BLK = 2 * KV_Q


class Sem:
    def __init__(self, nc, stack, name):
        self.h = stack.enter_context(nc.semaphore(name))
        self.n = 0


class Builder:
    def __init__(self):
        from contextlib import ExitStack
        self.nc = bass.Bass("TRN2", target_bir_lowering=False, debug=False)
        self.stack = ExitStack()
        self.q = {e: [] for e in ("tensor", "vector", "scalar", "gpsimd", "sync")}
        self.sems = []
        self.bank_free = {}
        self.buf_free = {}

    def sem(self, name):
        s = Sem(self.nc, self.stack, name)
        self.sems.append(s)
        return s

    def op(self, eng, f, inc=None, k=1):
        ss = getattr(self, "self_serial", {}).get(eng)
        if ss is not None and ss.n > 0:
            self.wait(eng, ss, ss.n)
        if inc is not None:
            inc.n += k
            self.q[eng].append(lambda e, f=f, s=inc.h, k=k: f(e).then_inc(s, k))
        else:
            self.q[eng].append(lambda e, f=f: f(e))

    def wait(self, eng, sem, val=None):
        v = sem.n if val is None else val
        if v <= 0:
            return
        self.q[eng].append(lambda e, s=sem.h, v=v: e.wait_ge(s, v))

    def barrier(self):
        snaps = [(s, s.n) for s in self.sems]
        for e in ("tensor", "vector", "scalar"):
            for s, v in snaps:
                if v > 0:
                    self.wait(e, s, v)

    def bwait(self, eng, key):
        if key in self.bank_free:
            s, v = self.bank_free[key]
            self.wait(eng, s, v)

    def bread(self, key, sem):
        self.bank_free[key] = (sem, sem.n)

    def fwait(self, eng, key):
        if key in self.buf_free:
            s, v = self.buf_free[key]
            self.wait(eng, s, v)

    def fread(self, key, sem):
        self.buf_free[key] = (sem, sem.n)


def build_program():
    _uid = [0]
    b = Builder()
    nc = b.nc
    O, W = b.op, b.wait

    din = lambda n, sh, dt: nc.dram_tensor(n, sh, dt, kind="ExternalInput").ap()
    X = din("X", [C, NTOK], F32)
    WqkvT = din("WqkvT", [12, 16, 128, 512], F32)
    bqkv2 = din("bqkv2", [128, 48], F32)
    Wo0T = din("Wo0T", [4, 16, 128, 512], F32)
    bo2 = din("bo2", [128, 16], F32)
    G0w = din("G0w", [12, 16, 128, 512], F32)
    U0w = din("U0w", [12, 16, 128, 512], F32)
    D0w = din("D0w", [4, 48, 128, 512], F32)
    g0s = din("g0s", [128, 48], F32)
    u0s = din("u0s", [128, 48], F32)
    d0s = din("d0s", [128, 16], F32)
    Wq1T = din("Wq1T", [4, 16, 128, 512], F32)
    Wk1T = din("Wk1T", [4, 16, 128, 512], F32)
    Wv1T = din("Wv1T", [4, 16, 128, 512], F32)
    Wo1T = din("Wo1T", [4, 16, 128, 512], F32)
    GAMB = din("GAMB", [128, 512], F32)
    BETB = din("BETB", [128, 512], F32)
    G1w = din("G1w", [16, 16, 128, 512], I8)
    U1w = din("U1w", [16, 16, 128, 512], I8)
    D1w = din("D1w", [4, 64, 128, 512], I8)
    g1s = din("g1s", [128, 64], F32)
    u1s = din("u1s", [128, 64], F32)
    d1s = din("d1s", [128, 16], F32)
    COS = din("COS", [128, NTOK], F32)
    SIN = din("SIN", [128, NTOK], F32)
    MSK = din("MSK", [16, 128, 512], F32)
    RT = din("RT", [128, 128], F32)
    IDN = din("IDN", [128, 128], F32)
    OUT = nc.dram_tensor("OUT", [C, NTOK], F32, kind="ExternalOutput").ap()
    X1 = nc.dram_tensor("X1", [C, NTOK], F32).ap()
    GD = nc.dram_tensor("GD", [C, NTOK], BF16).ap()
    KVO = nc.dram_tensor("KVO", [KV_BLK], BF16)
    KVG = nc.dram_tensor("KVG", [2 * KV_BLK], BF16)

    def kvo_k(q, h):
        off = q * KV_Q + h * (128 * 512)
        return KVO.ap()[off:off + 128 * 512].rearrange("(p t) -> p t", p=128)

    def kvo_v(q, h, tc):
        off = q * KV_Q + KV_KQ + h * (4 * 128 * 128) + tc * (128 * 128)
        return KVO.ap()[off:off + 128 * 128].rearrange("(p t) -> p t", p=128)

    def kvg_k(gq, h):
        r, qq = GQ[gq]
        off = r * KV_BLK + qq * KV_Q + h * (128 * 512)
        return KVG.ap()[off:off + 128 * 512].rearrange("(p t) -> p t", p=128)

    def kvg_v(gq, h):
        r, qq = GQ[gq]
        off = r * KV_BLK + qq * KV_Q + KV_KQ + h * (4 * 128 * 128)
        return KVG.ap()[off:off + 4 * 128 * 128].rearrange("(tc p t) -> p tc t", p=128, tc=4)

    sb = nc.alloc_sbuf_tensor
    xn = sb("xn", [128, FC, NTOK], BF16)
    wXt = sb("wX", [128, 16, 512], BF16)
    wYt = sb("wY", [128, 16, 512], BF16)
    cosb = sb("cosb", [128, NTOK], F32)
    sinb = sb("sinb", [128, NTOK], F32)
    rtb = sb("rtb", [128, 128], BF16)
    idnb = sb("idnb", [128, 128], BF16)
    onesb = sb("onesb", [1, 128], BF16)
    onesf = sb("onesf", [1, 128], F32)
    onescol = sb("onescol", [128, 1], BF16)
    sc_a = sb("sc_a", [128, 1024], F32)
    sc_b = sb("sc_b", [128, 512], F32)
    sc_c = sb("sc_c", [128, 2, 512], BF16)
    gstg = sb("gstg", [128, 4, 512], BF16)
    sc_d = sb("sc_d", [128, 2, 512], BF16)
    pp = sb("pp", [128, 3, 512], BF16)
    pm = sb("pm", [128, 2, 512], BF16)
    rz = sb("rz", [1, 512], F32)
    rzb = sb("rzb", [1, 512], BF16)
    rzt = sb("rzt", [128, 512], BF16)
    bqs = sb("bqs", [128, 16], F32)
    expsh = sb("expsh", [128, 1], F32)
    gamb = sb("gamb", [128, 512], F32)
    betb = sb("betb", [128, 512], F32)
    st1 = sb("st1", [128, 4], F32)
    st2 = sb("st2", [128, 4], F32)
    stm = sb("stm", [128, 4], F32)
    str_ = sb("str", [128, 4], F32)
    fmy = sb("fmy", [128, 512], F32)
    fmo = sb("fmo", [128, 512], BF16)
    xout = sb("xout", [128, 512], F32)
    bsc = sb("bsc", [128, 48 * 3 + 16 * 3 + 64 * 2], F32)

    ps = nc.alloc_psum_tensor
    pA = ps("pA", [128, 512], F32)
    pB = ps("pB", [128, 512], F32)
    pC = ps("pC", [128, 512], F32)
    pD = ps("pD", [128, 512], F32)
    pE = ps("pE", [128, 512], F32)
    pF = ps("pF", [128, 512], F32)
    pZ = ps("pZ", [128, 512], F32)
    PK = {0: (pA, "pA"), 1: (pB, "pB")}

    dC = b.sem("dC")
    dCg = b.sem("dCg")
    dX = [b.sem("dX0"), b.sem("dX1")]
    dWX, dWY = b.sem("dWX"), b.sem("dWY")
    dWZ, dWW = b.sem("dWZ"), b.sem("dWW")
    dKV = [b.sem("dKV0"), b.sem("dKV1")]
    dKVW = b.sem("dKVW")
    dVW = b.sem("dVW")
    dOUT = b.sem("dOUT")
    dGW = [b.sem("dGW0"), b.sem("dGW1")]
    tT = b.sem("tT")
    vV = b.sem("vV")
    sS = b.sem("sS")
    cc = b.sem("cc")
    gG = b.sem("gG")
    b.self_serial = {"vector": vV, "scalar": sS, "gpsimd": gG}
    WSEM = {id(wXt): dWX, id(wYt): dWY}

    def dma_gate(*engs):
        for eng in engs:
            for s in (tT, vV, sS):
                W(eng, s)

    def dma(eng, out, in_, sem):
        O(eng, lambda e, o=out, i=in_: e.dma_start(out=o, in_=i), sem, 16)
        return sem.n

    dma("gpsimd", rtb[:, :], RT, dCg)
    dma("gpsimd", idnb[:, :], IDN, dCg)
    dma("sync", cosb[:, :], COS, dC)
    dma("sync", sinb[:, :], SIN, dC)
    dma("sync", gamb[:, :], GAMB, dC)
    dma("sync", betb[:, :], BETB, dC)
    boff = {}
    _o = 0
    for nm, ap_, w_ in (("bqkv2", bqkv2, 48), ("g0s", g0s, 48), ("u0s", u0s, 48),
                        ("d0s", d0s, 16), ("bo2", bo2, 16), ("d1s", d1s, 16),
                        ("g1s", g1s, 64), ("u1s", u1s, 64)):
        boff[nm] = _o
        dma("sync", bsc[:, _o:_o + w_], ap_, dC)
        _o += w_

    def bcol(nm, j):
        return bsc[:, boff[nm] + j:boff[nm] + j + 1]

    W("vector", dC)
    W("vector", dCg)
    O("vector", lambda e: e.memset(onesb[:, :], 1.0))
    O("vector", lambda e: e.memset(onesf[:, :], 1.0))
    O("vector", lambda e: e.memset(expsh[:, :], EXP_SHIFT))
    O("vector", lambda e: e.memset(onescol[:, :], 1.0))
    O("vector", lambda e: e.tensor_scalar_mul(
        bqs[:, :], bsc[:, boff["bqkv2"]:boff["bqkv2"] + 16], RS), vV, 1)
    b.barrier()

    def wload(WSRC, grp, wbuf, view=None):
        sem = WSEM[id(wbuf)]
        b.fwait("gpsimd", ("w", id(wbuf)))
        dst = view if view is not None else wbuf[:, :, :]
        return dma("gpsimd", dst, WSRC[grp].rearrange("c p t -> p c t"), sem), sem

    def phase_ln(L):
        XSRC = X if L == 0 else X1
        XR = XSRC.rearrange("(c p) t -> p c t", p=128)
        if L == 1:
            W("sync", dOUT)
        from contextlib import ExitStack
        _uid[0] += 1
        u = _uid[0]
        with ExitStack() as es:
            xsg = es.enter_context(nc.sbuf_tensor(f"xs{u}", [128, 2, 4, 1024], F32))
            xqg = es.enter_context(nc.sbuf_tensor(f"xq{u}", [128, 2, 1024], BF16))
            xqs = es.enter_context(nc.sbuf_tensor(f"xsq{u}", [128, 2, 1024], BF16))
            mrow = es.enter_context(nc.sbuf_tensor(f"mrow{u}", [1, 1024], F32))
            rrow = es.enter_context(nc.sbuf_tensor(f"rrow{u}", [1, 1024], F32))
            trow = es.enter_context(nc.sbuf_tensor(f"trow{u}", [1, 1024], F32))
            t2row = es.enter_context(nc.sbuf_tensor(f"t2row{u}", [1, 1024], F32))
            xs = xsg
            dma_gate("sync")
            accs = [(pC, "pC"), (pD, "pD"), (pE, "pE"), (pF, "pF")]
            for a, k in accs:
                b.bwait("tensor", k)
            for grp in range(4):
                p = grp % 2
                b.fwait("sync", ("xs", p))
                b.fwait("sync", ("xs", p, "v"))
                snap = dma("sync", xs[:, p, :, :], XR[:, grp * 4:grp * 4 + 4, :], dX[p])
                for j in range(4):
                    c = grp * 4 + j
                    src = xs[:, p, j, :]
                    W("vector", dX[p], snap)
                    b.fwait("vector", ("xq", c % 2))
                    O("vector", lambda e, o=xqg[:, c % 2, :], i=src: e.tensor_copy(o, i), vV, 1)
                    W("scalar", dX[p], snap)
                    b.fwait("scalar", ("xsq", c % 2))
                    O("scalar", lambda e, o=xqs[:, c % 2, :], i=src:
                      e.activation(o, i, AF.Square), sS, 1)
                    W("tensor", vV)
                    W("tensor", sS)
                    for half in range(2):
                        hs = slice(half * 512, half * 512 + 512)
                        O("tensor", lambda e, o=accs[half][0][0:1, :], l=onescol[:, :],
                          r=xqg[:, c % 2, hs], s=(c == 0), p_=(c == 15):
                          e.matmul(o, l, r, start=s, stop=p_), tT, 1)
                        O("tensor", lambda e, o=accs[2 + half][0][0:1, :], l=onescol[:, :],
                          r=xqs[:, c % 2, hs], s=(c == 0), p_=(c == 15):
                          e.matmul(o, l, r, start=s, stop=p_), tT, 1)
                    b.fread(("xq", c % 2), tT)
                    b.fread(("xsq", c % 2), tT)
                b.fread(("xs", p), sS)
                b.fread(("xs", p, "v"), vV)
            W("scalar", tT)
            for half in range(2):
                hs = slice(half * 512, half * 512 + 512)
                O("scalar", lambda e, o=trow[0:1, hs], i=accs[half][0][0:1, :]:
                  e.copy(o, i), sS, 1)
                O("scalar", lambda e, o=t2row[0:1, hs], i=accs[2 + half][0][0:1, :]:
                  e.copy(o, i), sS, 1)
            for a, k in accs:
                b.bread(k, sS)
            W("vector", sS)
            O("vector", lambda e: e.tensor_scalar_mul(mrow[:, :], trow[:, :], 1.0 / 2048.0), vV, 1)
            O("vector", lambda e: e.tensor_scalar(t2row[:, :], t2row[:, :], 1.0 / 2048.0, EPS_LN,
                                                  ALU.mult, ALU.add), vV, 1)
            O("vector", lambda e: e.scalar_tensor_tensor(trow[:, :], mrow[:, :], 1.0, mrow[:, :],
                                                         ALU.mult, ALU.mult), vV, 1)
            O("vector", lambda e: e.tensor_tensor(t2row[:, :], t2row[:, :], trow[:, :],
                                                  ALU.subtract), vV, 1)
            O("vector", lambda e: e.reciprocal(t2row[:, :], t2row[:, :]), vV, 1)
            W("scalar", vV)
            O("scalar", lambda e: e.activation(rrow[:, :], t2row[:, :], AF.Sqrt), sS, 1)
            for grp in range(4):
                p = grp % 2
                b.fwait("sync", ("xs", p))
                b.fwait("sync", ("xs", p, "v"))
                snap = dma("sync", xs[:, p, :, :], XR[:, grp * 4:grp * 4 + 4, :], dX[p])
                for half in range(2):
                    hs = slice(half * 512, half * 512 + 512)
                    pk, bank = PK[half]
                    pk2, bank2 = [(pE, "pE"), (pF, "pF")][half]
                    if grp == 0:
                        W("tensor", sS)
                        b.bwait("tensor", bank)
                        O("tensor", lambda e, o=pk[:, :], r=mrow[0:1, hs]:
                          e.matmul(o, onesf[0:1, :], r, start=True, stop=True), tT, 1)
                        b.bwait("tensor", bank2)
                        O("tensor", lambda e, o=pk2[:, :], r=rrow[0:1, hs]:
                          e.matmul(o, onesf[0:1, :], r, start=True, stop=True), tT, 1)
                    for j in range(4):
                        c = grp * 4 + j
                        W("vector", dX[p], snap)
                        W("vector", tT)
                        b.fwait("vector", ("lntmp",))
                        O("vector", lambda e, o=sc_a[:, 0:512], i=xs[:, p, j, hs], m=pk[:, :]:
                          e.tensor_tensor(o, i, m, ALU.subtract), vV, 1)
                        O("vector", lambda e, o=xn[:, c, hs], i=sc_a[:, 0:512], m=pk2[:, :]:
                          e.tensor_tensor(o, i, m, ALU.mult), vV, 1)
                        b.fread(("lntmp",), vV)
                b.fread(("xs", p), vV)
                b.fread(("xs", p, "v"), vV)
            b.bread("pA", vV)
            b.bread("pB", vV)
            b.bread("pE", vV)
            b.bread("pF", vV)
            b.barrier()

    def proj_rope(grps, kind, qdst=None, slot=None):
        pending = [None]

        def emit_fin():
            if pending[0] is not None:
                pending[0]()
                pending[0] = None

        for gi, grp in enumerate(grps):
            wbuf = (wXt, wYt)[gi % 2]
            snap, wsem = wload(WqkvT, grp, wbuf)
            for hh in range(4):
                h = gi * 4 + hh
                halves = range(2) if slot is None else [slot]
                for half in halves:
                    hs = slice(half * 512, half * 512 + 512)
                    pk, bank = PK[half % 2]
                    b.bwait("tensor", bank)
                    W("tensor", wsem, snap)
                    for kc in range(FC):
                        O("tensor", lambda e, o=pk[:, :], l=wbuf[:, kc, hh * 128:hh * 128 + 128],
                          r=xn[:, kc, hs], s=(kc == 0), p_=(kc == 15):
                          e.matmul(o, l, r, start=s, stop=p_), tT, 1)
                    b.fread(("w", id(wbuf)), tT)
                    emit_fin()
                    W("scalar", tT)
                    b.fwait("scalar", ("praw", half))
                    if kind == "q":
                        O("scalar", lambda e, o=sc_a[:, hs], i=pk[:, :], bb=bqs[:, h:h + 1]:
                          e.activation(o, i, AF.Identity, bias=bb, scale=RS), sS, 1)
                    else:
                        O("scalar", lambda e, o=sc_a[:, hs], i=pk[:, :],
                          bb=bcol("bqkv2", 16 + h): e.activation(o, i, AF.Identity, bias=bb), sS, 1)
                    b.bread(bank, sS)
                    W("vector", sS)
                    tile = (h * 2 + half) % 2
                    b.fwait("vector", ("pp", tile))
                    b.fwait("vector", ("pp2", tile))
                    O("vector", lambda e, o=pp[:, tile, :], i=sc_a[:, hs], m=sinb[:, hs]:
                      e.tensor_tensor(o, i, m, ALU.mult), vV, 1)
                    O("vector", lambda e, o=pm[:, tile, :], i=sc_a[:, hs], m=cosb[:, hs]:
                      e.tensor_tensor(o, i, m, ALU.mult), vV, 1)
                    b.fread(("praw", half), vV)
                    vsnap = vV.n

                    def fin(h=h, half=half, hs=hs, vsnap=vsnap, tile=tile):
                        b.bwait("tensor", "pF")
                        W("tensor", vV, vsnap)
                        O("tensor", lambda e, o=pF[:, :], l=rtb[:, :], r=pp[:, tile, :]:
                          e.matmul(o, l, r, start=True, stop=False), tT, 1)
                        O("tensor", lambda e, o=pF[:, :], l=idnb[:, :], r=pm[:, tile, :]:
                          e.matmul(o, l, r, start=False, stop=True), tT, 1)
                        b.fread(("pp", tile), tT)
                        b.fread(("pp2", tile), tT)
                        W("scalar", tT)
                        if kind == "q":
                            O("scalar", lambda e, o=qdst[:, h, 0:512], i=pF[:, :]: e.copy(o, i), sS, 1)
                            b.bread("pF", sS)
                        else:
                            b.fwait("scalar", ("kst",))
                            O("scalar", lambda e, o=rzt[:, :], i=pF[:, :]: e.copy(o, i), sS, 1)
                            b.bread("pF", sS)
                            W("sync", sS)
                            dma("sync", kvo_k(half, h), rzt[:, :], dKVW)
                            b.fread(("kst",), dKVW)
                    pending[0] = fin
        emit_fin()

    def proj_tm(WSRC, grps, consume, slot=None):
        tcs = range(8) if slot is None else range(slot * 4, slot * 4 + 4)
        pending = [None]
        for gi, grp in enumerate(grps):
            wbuf = (wXt, wYt)[gi % 2]
            snap, wsem = wload(WSRC, grp, wbuf)
            for tc in tcs:
                pk, bank = PK[tc % 2]
                b.bwait("tensor", bank)
                W("tensor", wsem, snap)
                for kc in range(FC):
                    O("tensor", lambda e, o=pk[:, :], l=xn[:, kc, tc * 128:tc * 128 + 128],
                      r=wbuf[:, kc, :], s=(kc == 0), p_=(kc == 15):
                      e.matmul(o, l, r, start=s, stop=p_), tT, 1)
                b.fread(("w", id(wbuf)), tT)
                if pending[0] is not None:
                    pending[0]()
                pending[0] = consume(gi, tc, pk, bank)
        if pending[0] is not None:
            pending[0]()

    def v_consume(gi, tc, pk, bank):
        half, tcq = tc // 4, tc % 4
        W("scalar", tT)
        b.fwait("scalar", ("vst",))
        O("scalar", lambda e, o=sc_d[:, 1, :], i=pk[:, :]: e.copy(o, i), sS, 1)
        b.bread(bank, sS)
        W("sync", sS)
        for hh in range(4):
            h = gi * 4 + hh
            dma("sync", kvo_v(half, h, tcq), sc_d[:, 1, hh * 128:hh * 128 + 128], dVW)
        b.fread(("vst",), dVW)
        return None

    def fm_consume_mk(is_q, qdst=None, pT=None):
        def consume(gi, tc, pk, bank):
            half, tcq = tc // 4, tc % 4
            W("vector", tT)
            b.fwait("vector", ("fmy",))
            O("vector", lambda e, o=fmy[:, :], i=pk[:, :], m=gamb[:, :]:
              e.tensor_tensor(o, i, m, ALU.mult), vV, 1)
            O("vector", lambda e, o=fmy[:, :], i=fmy[:, :], m=betb[:, :]:
              e.tensor_tensor(o, i, m, ALU.add), vV, 1)
            b.bread(bank, vV)
            O("vector", lambda e, o=st1[:, :],
              i=fmy.ap().rearrange("p (h d) -> p h d", d=128):
              e.reduce_sum(o, i, axis=AX.X), vV, 1)
            W("scalar", vV)
            b.fwait("scalar", ("fmsq",))
            O("scalar", lambda e, o=sc_b[:, 0:512], i=fmy[:, :]:
              e.activation(o, i, AF.Square), sS, 1)
            W("vector", sS)
            O("vector", lambda e, o=st2[:, :],
              i=sc_b.ap()[:, 0:512].rearrange("p (h d) -> p h d", d=128):
              e.reduce_sum(o, i, axis=AX.X), vV, 1)
            b.fread(("fmsq",), vV)
            O("vector", lambda e: e.tensor_scalar_mul(stm[:, :], st1[:, :], 1.0 / 128.0), vV, 1)
            O("vector", lambda e: e.tensor_scalar(str_[:, :], st2[:, :], 1.0 / 128.0, EPS_L1,
                                                  ALU.mult, ALU.add), vV, 1)
            O("vector", lambda e: e.scalar_tensor_tensor(st1[:, :], stm[:, :], 1.0, stm[:, :],
                                                         ALU.mult, ALU.mult), vV, 1)
            O("vector", lambda e: e.tensor_tensor(str_[:, :], str_[:, :], st1[:, :],
                                                  ALU.subtract), vV, 1)
            O("vector", lambda e: e.reciprocal(str_[:, :], str_[:, :]), vV, 1)
            W("scalar", vV)
            O("scalar", lambda e: e.activation(str_[:, :], str_[:, :], AF.Sqrt), sS, 1)
            if is_q:
                W("vector", sS)
                O("vector", lambda e: e.tensor_scalar_mul(str_[:, :], str_[:, :], RS), vV, 1)
            for hh in range(4):
                W("vector", sS)
                b.fwait("vector", ("fmo", hh))
                O("vector", lambda e, o=fmo[:, hh * 128:hh * 128 + 128],
                  i=fmy[:, hh * 128:hh * 128 + 128], m1=stm[:, hh:hh + 1],
                  m2=str_[:, hh:hh + 1]:
                  e.tensor_scalar(o, i, m1, m2, ALU.subtract, ALU.mult), vV, 1)
            b.fread(("fmy",), vV)
            vsnap = vV.n

            def fin(gi=gi, tc=tc, half=half, tcq=tcq, vsnap=vsnap):
                for hh in range(4):
                    h = gi * 4 + hh
                    b.bwait("tensor", "pT")
                    W("tensor", vV, vsnap)
                    O("tensor", lambda e, o=pT[:, 0:128], i=fmo[:, hh * 128:hh * 128 + 128]:
                      e.transpose(o, i, idnb[:, :]), tT, 1)
                    if hh == 3:
                        b.fread(("fmo", 0), tT)
                        b.fread(("fmo", 1), tT)
                        b.fread(("fmo", 2), tT)
                        b.fread(("fmo", 3), tT)
                    W("scalar", tT)
                    if is_q:
                        tl = tc % 4
                        O("scalar", lambda e, o=qdst[:, h, tl * 128:tl * 128 + 128],
                          i=pT[:, 0:128]: e.copy(o, i), sS, 1)
                        b.bread("pT", sS)
                    else:
                        b.fwait("scalar", ("kst",))
                        O("scalar", lambda e, o=rzt[:, 0:128], i=pT[:, 0:128]: e.copy(o, i), sS, 1)
                        b.bread("pT", sS)
                        W("sync", sS)
                        dma("sync",
                            kvo_k(half, h).rearrange("p (a t) -> p a t", a=4)[:, tcq, :],
                            rzt[:, 0:128], dKVW)
                        b.fread(("kst",), dKVW)
            return fin
        return consume

    def phase_ffn(L):
        if L == 0:
            NMG, NKD, WG, WU, WD = 12, 48, G0w, U0w, D0w
            gsn, usn, dsn = "g0s", "u0s", "d0s"
        else:
            NMG, NKD, WG, WU, WD = 16, 64, G1w, U1w, D1w
            gsn, usn, dsn = "g1s", "u1s", "d1s"
        nseg = NKD // 16
        GDr = GD.rearrange("(c p) t -> p c t", p=128)
        from contextlib import ExitStack
        _uid[0] += 1
        with ExitStack() as es:
            tokw = 512
            fuse = es.enter_context(nc.sbuf_tensor(f"fuse{_uid[0]}", [128, NKD, tokw], BF16))
            wZt = es.enter_context(nc.sbuf_tensor(f"wZ{_uid[0]}", [128, 16, 512], BF16))
            wWt = es.enter_context(nc.sbuf_tensor(f"wW{_uid[0]}", [128, 16, 512], BF16))
            WSEM[id(wZt)] = dWZ
            WSEM[id(wWt)] = dWW
            tiles = [wXt, wYt, wZt, wWt]
            if L == 0:
                views = [t.ap().bitcast(FP8)[:, :, 0:512] for t in tiles]
            else:
                views = [t[:, :, :] for t in tiles]

            def mk_loader(srcs):
                snaps = [None] * len(srcs)
                emitted = [0]

                nb = len(tiles)
                def ensure(i):
                    while emitted[0] <= min(i + (nb - 2), len(srcs) - 1):
                        j = emitted[0]
                        t = tiles[j % nb]
                        b.fwait("gpsimd", ("w", id(t)))
                        snaps[j] = dma("gpsimd", views[j % nb],
                                       srcs[j].rearrange("c p t -> p c t"), WSEM[id(t)])
                        emitted[0] += 1
                return snaps, ensure

            halves = [0, 1]
            for hpass in halves:
                srcs = []
                for mg in range(NMG):
                    srcs.append(WG[mg])
                    srcs.append(WU[mg])
                for fg in range(4):
                    WDr = WD[fg].rearrange("(s c) p t -> s c p t", c=16)
                    for seg in range(nseg):
                        srcs.append(WDr[seg])
                snaps, ensure = mk_loader(srcs)
                tok_halves = [hpass]
                li = 0
                for mg in range(NMG):
                    ensure(li + 1)
                    for mc in range(4):
                        m = mg * 4 + mc
                        for hi, half in enumerate(tok_halves):
                            hs = slice(half * 512, half * 512 + 512)
                            fs = slice(hi * 512, hi * 512 + 512)
                            slot = (m % 2) * 2 + hi
                            for (scn, func, banks, idx) in (
                                    (gsn, AF.Silu, ((pA, "pA"), (pB, "pB")), li),
                                    (usn, AF.Copy, ((pC, "pC"), (pD, "pD")), li + 1)):
                                wv = views[idx % len(tiles)]
                                wt = tiles[idx % len(tiles)]
                                pk, bank = banks[hi % len(banks)]
                                b.bwait("tensor", bank)
                                W("tensor", WSEM[id(wt)], snaps[idx])
                                for kc in range(FC):
                                    O("tensor", lambda e, o=pk[:, :],
                                      l=wv[:, kc, mc * 128:mc * 128 + 128],
                                      r=xn[:, kc, hs], s=(kc == 0), p_=(kc == 15):
                                      e.matmul(o, l, r, start=s, stop=p_), tT, 1)
                                b.fread(("w", id(wt)), tT)
                                W("scalar", tT)
                                if func is AF.Silu:
                                    b.fwait("scalar", ("gstg", slot))
                                    O("scalar", lambda e, o=gstg[:, slot, :], i=pk[:, :],
                                      sc=bcol(scn, m): e.activation(o, i, AF.Silu, scale=sc),
                                      sS, 1)
                                    b.bread(bank, sS)
                                else:
                                    b.fwait("scalar", ("gu", hi))
                                    O("scalar", lambda e, o=sc_d[:, hi, :], i=pk[:, :],
                                      sc=bcol(scn, m): e.activation(o, i, AF.Copy, scale=sc),
                                      sS, 1)
                                    b.bread(bank, sS)
                                    W("vector", sS)
                                    b.fwait("vector", ("fuse", m, hi))
                                    O("vector", lambda e, o=fuse[:, m, fs], a=gstg[:, slot, :],
                                      bb=sc_d[:, hi, :]: e.tensor_tensor(o, a, bb, ALU.mult),
                                      vV, 1)
                                    b.fread(("gstg", slot), vV)
                                    b.fread(("gu", hi), vV)
                    li += 2
                dbanks = [(pA, "pA"), (pB, "pB"), (pC, "pC"), (pD, "pD")]
                nunit = len(tok_halves)
                for fg in range(4):
                    for _, kk in dbanks:
                        b.bwait("tensor", kk)
                    for seg in range(nseg):
                        idx = li + seg
                        ensure(idx)
                        wv = views[idx % len(tiles)]
                        wt = tiles[idx % len(tiles)]
                        W("tensor", WSEM[id(wt)], snaps[idx])
                        if seg == 0:
                            W("tensor", vV)
                        for kc in range(16):
                            k = seg * 16 + kc
                            for fo4 in range(4):
                                for hi in range(nunit):
                                    fs = slice(hi * 512, hi * 512 + 512)
                                    dbk = dbanks[fo4 * nunit + hi][0]
                                    O("tensor", lambda e, o=dbk[:, :],
                                      l=wv[:, kc, fo4 * 128:fo4 * 128 + 128],
                                      r=fuse[:, k, fs],
                                      s=(k == 0), p_=(k == NKD - 1):
                                      e.matmul(o, l, r, start=s, stop=p_), tT, 1)
                        b.fread(("w", id(wt)), tT)
                    li += nseg
                    W("scalar", tT)
                    for fo4 in range(4):
                        fo = fg * 4 + fo4
                        for hi, half in enumerate(tok_halves):
                            hs = slice(half * 512, half * 512 + 512)
                            ui = (fo4 * nunit + hi) % 2
                            b.fwait("scalar", ("pp", ui))
                            O("scalar", lambda e, o=pp[:, ui, :],
                              i=dbanks[fo4 * nunit + hi][0][:, :],
                              sc=bcol(dsn, fo): e.activation(o, i, AF.Silu, scale=sc), sS, 1)
                            W("sync", sS)
                            dma("sync", GDr[:, fo, hs], pp[:, ui, :], dGW[ui])
                            b.fread(("pp", ui), dGW[ui])
                    for _, kk in dbanks:
                        b.bread(kk, sS)
                for m in range(NKD):
                    for hi in range(nunit):
                        b.fread(("fuse", m, hi), tT)
            b.barrier()

    def phase_attn(L, qbuf, obuf, slot, mskb, kbufs, vbufs):
        nch = 8 if slot == 0 else 16
        qs = slice(slot * 512, slot * 512 + 512)
        quarters = [0, 1] if slot == 0 else [0, 1, 2, 3]
        rot = [(pB, "pB"), (pC, "pC"), (pD, "pD")]
        rot_i = [0]

        def next_rot():
            r = rot[rot_i[0] % 3]
            rot_i[0] += 1
            return r

        def issue_kv(h):
            p = h % 2
            b.fwait("sync", ("kv", p))
            for ji, gq in enumerate(quarters):
                dma("sync", kbufs[p][:, ji, :], kvg_k(gq, h), dKV[p])
                dma("sync", vbufs[p][:, ji * 4:ji * 4 + 4, :], kvg_v(gq, h), dKV[p])
            return dKV[p].n

        pend_norm = [None]

        def emit_norm():
            if pend_norm[0] is not None:
                pend_norm[0]()
                pend_norm[0] = None

        kvsnap = issue_kv(0)
        for h in range(H):
            p = h % 2
            mysnap = kvsnap
            if h + 1 < H:
                kvsnap = issue_kv(h + 1)
            ob, obk = (pE, "pE") if h % 2 == 0 else (pA, "pA")
            zb, zbk = (pZ, "pZ") if h % 2 == 0 else (pF, "pF")
            W("tensor", dKV[p], mysnap)
            stage2 = []

            def pop_one():
                c2, pS2, bankS2, tsnap = stage2.pop(0)
                ppj = c2 % 3
                masked = True if slot == 0 else (c2 >= 8)
                mski = slot * 8 + (c2 if slot == 0 else c2 - 8)
                W("scalar", tT, tsnap)
                b.fwait("scalar", ("pp", ppj))
                if L == 0:
                    O("scalar", lambda e, o=pp[:, ppj, :], i=pS2[:, :]:
                      e.activation(o, i, AF.Exp, bias=expsh[:, 0:1]), sS, 1)
                else:
                    O("scalar", lambda e, o=pp[:, ppj, :], i=pS2[:, :]:
                      e.activation(o, i, AF.Square), sS, 1)
                b.bread(bankS2, sS)
                ssnap = sS.n
                if masked:
                    W("gpsimd", sS, ssnap)
                    b.fwait("gpsimd", ("pmm", c2 % 2))
                    O("gpsimd", lambda e, o=pm[:, c2 % 2, :], i=pp[:, ppj, :],
                      m=mskb[:, mski, :]: e.tensor_tensor(o, i, m, ALU.mult), gG, 1)
                    b.fread(("pp", ppj), gG)
                    src_ = pm[:, c2 % 2, :]
                    W("tensor", gG)
                else:
                    src_ = pp[:, ppj, :]
                    W("tensor", sS, ssnap)
                if c2 == 0:
                    b.bwait("tensor", obk)
                    b.bwait("tensor", zbk)
                O("tensor", lambda e, o=ob[:, :], l=vbufs[p][:, c2, :], r=src_,
                  s=(c2 == 0), p_=(c2 == nch - 1):
                  e.matmul(o, l, r, start=s, stop=p_), tT, 1)
                O("tensor", lambda e, o=zb[0:1, :], l=onescol[:, :], r=src_,
                  s=(c2 == 0), p_=(c2 == nch - 1):
                  e.matmul(o, l, r, start=s, stop=p_), tT, 1)
                if masked:
                    b.fread(("pmm", c2 % 2), tT)
                else:
                    b.fread(("pp", ppj), tT)

            for c in range(nch + 2):
                if stage2 and (len(stage2) == 2 or c >= nch):
                    pop_one()
                if c == 2:
                    emit_norm()
                if c < nch:
                    pS, bankS = next_rot()
                    b.bwait("tensor", bankS)
                    O("tensor", lambda e, o=pS[:, :],
                      l=kbufs[p][:, c // 4, (c % 4) * 128:(c % 4) * 128 + 128],
                      r=qbuf[:, h, 0:512]: e.matmul(o, l, r, start=True, stop=True), tT, 1)
                    stage2.append((c, pS, bankS, tT.n))
            b.fread(("kv", p), tT)
            tchunks = tT.n

            def norm(h=h, ob=ob, obk=obk, zb=zb, zbk=zbk, tchunks=tchunks):
                W("vector", tT, tchunks)
                b.fwait("vector", ("rz",))
                if L == 0:
                    O("vector", lambda e: e.reciprocal(rz[:, :], zb[0:1, :]), vV, 1)
                else:
                    O("vector", lambda e: e.tensor_scalar_add(rz[:, :], zb[0:1, :], EPS_L1), vV, 1)
                    O("vector", lambda e: e.reciprocal(rz[:, :], rz[:, :]), vV, 1)
                O("vector", lambda e: e.tensor_copy(rzb[:, :], rz[:, :]), vV, 1)
                b.bread(zbk, vV)
                po, pok = next_rot()
                b.bwait("tensor", pok)
                W("tensor", vV)
                O("tensor", lambda e, o=po[:, :]: e.matmul(o, onesb[:, :], rzb[:, :],
                                                           start=True, stop=True), tT, 1)
                W("scalar", tT)
                b.fwait("scalar", ("kst",))
                O("scalar", lambda e, o=rzt[:, :], i=po[:, :]: e.copy(o, i), sS, 1)
                b.bread(pok, sS)
                W("vector", sS)
                if L == 0:
                    b.fwait("vector", ("onrm",))
                    O("vector", lambda e, o=sc_c[:, 0, :], a=ob[:, :], bb=rzt[:, :]:
                      e.tensor_tensor(o, a, bb, ALU.mult), vV, 1)
                    O("vector", lambda e, o=obuf[:, h, qs], i=sc_c[:, 0, :],
                      bb=bcol("bqkv2", 32 + h): e.tensor_scalar_add(o, i, bb), vV, 1)
                    b.fread(("onrm",), vV)
                else:
                    O("vector", lambda e, o=obuf[:, h, qs], a=ob[:, :], bb=rzt[:, :]:
                      e.tensor_tensor(o, a, bb, ALU.mult), vV, 1)
                b.bread(obk, vV)
                b.fread(("rz",), vV)
                b.fread(("kst",), vV)
            pend_norm[0] = norm
        emit_norm()

    def phase_wo(L, obuf):
        XSRC = X if L == 0 else X1
        XR = XSRC.rearrange("(c p) t -> p c t", p=128)
        GDr = GD.rearrange("(c p) t -> p c t", p=128)
        XD = (X1 if L == 0 else OUT).rearrange("(c p) t -> p c t", p=128)
        WoT = Wo0T if L == 0 else Wo1T
        from contextlib import ExitStack
        _uid[0] += 1
        with ExitStack() as es:
            xf = es.enter_context(nc.sbuf_tensor(f"xf{_uid[0]}", [128, 2, 1024], F32))
            gf = es.enter_context(nc.sbuf_tensor(f"gf{_uid[0]}", [128, 2, 1024], BF16))
            dma_gate("sync")
            W("sync", dGW[0])
            W("sync", dGW[1])
            for grp in range(4):
                wbuf = (wXt, wYt)[grp % 2]
                snap, wsem = wload(WoT, grp, wbuf)
                for hh in range(4):
                    fo = grp * 4 + hh
                    p = fo % 2
                    b.fwait("sync", ("xf", p))
                    xsnap = dma("sync", xf[:, p, :], XR[:, fo, :], dX[p])
                    dma("sync", gf[:, p, :], GDr[:, fo, :], dX[p])
                    xsnap = dX[p].n
                    for half in range(2):
                        hs = slice(half * 512, half * 512 + 512)
                        pk, bank = PK[half]
                        b.bwait("tensor", bank)
                        W("tensor", wsem, snap)
                        for kc in range(FC):
                            O("tensor", lambda e, o=pk[:, :],
                              l=wbuf[:, kc, hh * 128:hh * 128 + 128],
                              r=obuf[:, kc, hs], s=(kc == 0), p_=(kc == 15):
                              e.matmul(o, l, r, start=s, stop=p_), tT, 1)
                        b.fread(("w", id(wbuf)), tT)
                        W("scalar", tT)
                        b.fwait("scalar", ("wos", half))
                        if L == 0:
                            O("scalar", lambda e, o=sc_c[:, half, :], i=pk[:, :],
                              bb=bcol("bo2", fo): e.activation(o, i, AF.Identity, bias=bb), sS, 1)
                        else:
                            O("scalar", lambda e, o=sc_c[:, half, :], i=pk[:, :]:
                              e.copy(o, i), sS, 1)
                        b.bread(bank, sS)
                        W("vector", sS)
                        W("vector", dX[p], xsnap)
                        b.fwait("vector", ("cmb",))
                        O("vector", lambda e, o=sc_b[:, :], i=sc_c[:, half, :],
                          m=gf[:, p, hs]: e.tensor_tensor(o, i, m, ALU.mult), vV, 1)
                        b.fread(("wos", half), vV)
                        b.fwait("vector", ("xout",))
                        O("vector", lambda e, o=xout[:, :], i=sc_b[:, :], m=xf[:, p, hs]:
                          e.tensor_tensor(o, i, m, ALU.add), vV, 1)
                        b.fread(("cmb",), vV)
                        W("sync", vV)
                        dma("sync", XD[:, fo, hs], xout[:, :], dOUT)
                        b.fread(("xout",), dOUT)
                    b.fread(("xf", p), vV)
            b.barrier()

    from contextlib import ExitStack
    for L in range(2):
        esL = ExitStack()
        pT = None
        if L == 1:
            pT = esL.enter_context(nc.psum_tensor("pT1", [128, 1024], BF16))
        phase_ln(L)
        if L == 0:
            proj_rope([4, 5, 6, 7], "k")
            proj_tm(WqkvT, [8, 9, 10, 11], v_consume)
        else:
            proj_tm(Wk1T, [0, 1, 2, 3], fm_consume_mk(False, pT=pT))
            proj_tm(Wv1T, [0, 1, 2, 3], v_consume)
        W("gpsimd", dKVW)
        W("gpsimd", dVW)
        O("gpsimd", lambda e: e.collective_compute(
            "AllGather", ALU.bypass,
            replica_groups=[[0, 1], [2, 3], [4, 5], [6, 7]],
            ins=[KVO.ap().opt()], outs=[KVG.ap().opt()]), cc, 1)
        phase_ffn(L)
        with ExitStack() as es:
            obuf = es.enter_context(nc.sbuf_tensor(f"obuf{L}", [128, FC, NTOK], BF16))
            with ExitStack() as es2:
                qbuf = es2.enter_context(nc.sbuf_tensor(f"qbuf{L}", [128, FC, 512], BF16))
                mskb = es2.enter_context(nc.sbuf_tensor(f"mskb{L}", [128, 16, 512], BF16))
                kbufs = [es2.enter_context(nc.sbuf_tensor(f"kb{L}{p}", [128, 4, 512], BF16))
                         for p in range(2)]
                vbufs = [es2.enter_context(nc.sbuf_tensor(f"vb{L}{p}", [128, 16, 128], BF16))
                         for p in range(2)]
                dma_gate("gpsimd", "sync")
                dma("gpsimd", mskb[:, :, :], MSK.rearrange("c p t -> p c t"), dCg)
                W("sync", cc)
                W("vector", dCg)
                for slot in range(2):
                    if L == 0:
                        proj_rope([0, 1, 2, 3], "q", qbuf, slot)
                    else:
                        proj_tm(Wq1T, [0, 1, 2, 3], fm_consume_mk(True, qbuf, pT=pT), slot)
                    b.barrier()
                    phase_attn(L, qbuf, obuf, slot, mskb, kbufs, vbufs)
                    b.barrier()
            phase_wo(L, obuf)
        esL.close()

    W("gpsimd", dOUT)
    W("gpsimd", dGW[0])
    W("gpsimd", dGW[1])
    W("gpsimd", dKVW)
    W("gpsimd", dVW)
    W("gpsimd", dC)
    W("gpsimd", dCg)
    for s in dX + dKV + [dWX, dWY, dWZ, dWW]:
        W("gpsimd", s)

    with nc.Block() as block:
        @block.tensor
        def _(e):
            for f in b.q["tensor"]:
                f(e)

        @block.vector
        def _(e):
            for f in b.q["vector"]:
                f(e)

        @block.scalar
        def _(e):
            for f in b.q["scalar"]:
                f(e)

        @block.gpsimd
        def _(e):
            for f in b.q["gpsimd"]:
                f(e)

        @block.sync
        def _(e):
            for f in b.q["sync"]:
                f(e)

    b.stack.close()
    return nc


def _blocked(w2d):
    K, M = w2d.shape
    return np.ascontiguousarray(
        w2d.reshape(K // 128, 128, M // 512, 512).transpose(2, 0, 1, 3))


def _chunk_cols(v):
    return np.ascontiguousarray(v.reshape(-1, 128).T)


def _rope_tables(pos):
    p = np.arange(64, dtype=np.float32)
    theta = np.power(np.float32(10000.0),
                     (-(np.float32(2.0) ** p) / np.float32(128.0)).astype(np.float32))
    ang = pos.astype(np.float32)[None, :] * theta[:, None]
    ang = np.concatenate([ang, ang], axis=0)
    return np.cos(ang).astype(np.float32), np.sin(ang).astype(np.float32)


def _rot_matrix():
    R = np.zeros((128, 128), np.float32)
    for i in range(64):
        R[i, i + 64] = -1.0
        R[i + 64, i] = 1.0
    return np.ascontiguousarray(R.T)


def prepare_in_maps(ii):
    x = np.asarray(ii["L0_attn_x"], np.float32)
    shared = {
        "WqkvT": _blocked(np.ascontiguousarray(np.asarray(ii["L0_attn_Wqkv"], np.float32).T)),
        "bqkv2": _chunk_cols(np.asarray(ii["L0_attn_bqkv"], np.float32)),
        "Wo0T": _blocked(np.ascontiguousarray(np.asarray(ii["L0_attn_Wo"], np.float32).T)),
        "bo2": _chunk_cols(np.asarray(ii["L0_attn_bo"], np.float32)),
        "G0w": _blocked(np.asarray(ii["L0_ffn_gate_w"], np.float32)),
        "U0w": _blocked(np.asarray(ii["L0_ffn_up_w"], np.float32)),
        "D0w": _blocked(np.asarray(ii["L0_ffn_down_w"], np.float32)),
        "g0s": _chunk_cols(np.asarray(ii["L0_ffn_gate_scale"], np.float32)),
        "u0s": _chunk_cols(np.asarray(ii["L0_ffn_up_scale"], np.float32)),
        "d0s": _chunk_cols(np.asarray(ii["L0_ffn_down_scale"], np.float32)),
        "Wq1T": _blocked(np.ascontiguousarray(np.asarray(ii["L1_attn_Wq"], np.float32).T)),
        "Wk1T": _blocked(np.ascontiguousarray(np.asarray(ii["L1_attn_Wk"], np.float32).T)),
        "Wv1T": _blocked(np.ascontiguousarray(np.asarray(ii["L1_attn_Wv"], np.float32).T)),
        "Wo1T": _blocked(np.ascontiguousarray(np.asarray(ii["L1_attn_Wo"], np.float32).T)),
        "GAMB": np.ascontiguousarray(
            np.tile(np.asarray(ii["L1_attn_gamma"], np.float32)[None, :], (128, 4))),
        "BETB": np.ascontiguousarray(
            np.tile(np.asarray(ii["L1_attn_beta"], np.float32)[None, :], (128, 4))),
        "G1w": _blocked(np.asarray(ii["L1_ffn_w_gate"])),
        "U1w": _blocked(np.asarray(ii["L1_ffn_w_up"])),
        "D1w": _blocked(np.asarray(ii["L1_ffn_w_down"])),
        "g1s": _chunk_cols(np.asarray(ii["L1_ffn_s_gate"], np.float32)),
        "u1s": _chunk_cols(np.asarray(ii["L1_ffn_s_up"], np.float32)),
        "d1s": _chunk_cols(np.asarray(ii["L1_ffn_s_down"], np.float32)),
        "RT": _rot_matrix(),
        "IDN": np.eye(128, dtype=np.float32),
    }

    in_maps = []
    for c in range(NCORES):
        s, par = c // 2, c % 2
        qE, qL = (0, 3) if par == 0 else (1, 2)
        xc = np.concatenate([x[s, qE * 512:qE * 512 + 512, :],
                             x[s, qL * 512:qL * 512 + 512, :]], axis=0)
        pos = np.concatenate([np.arange(qE * 512, qE * 512 + 512),
                              np.arange(qL * 512, qL * 512 + 512)]) + 1
        cosc, sinc = _rope_tables(pos)
        mskc = np.zeros((16, 128, 512), np.float32)
        for slot in range(2):
            tqb = (qE if slot == 0 else qL) * 512
            for j in range(8):
                cglob = j if slot == 0 else 8 + j
                tk = cglob * 128 + np.arange(128)[:, None]
                tq = tqb + np.arange(512)[None, :]
                mskc[slot * 8 + j] = (tk <= tq).astype(np.float32)
        m = dict(shared)
        m["X"] = np.ascontiguousarray(xc.T)
        m["COS"] = cosc
        m["SIN"] = sinc
        m["MSK"] = mskc
        in_maps.append(m)
    return in_maps


def assemble_out(core_outs):
    out = np.empty((B, T, C), np.float32)
    for c in range(NCORES):
        s, par = c // 2, c % 2
        qE, qL = (0, 3) if par == 0 else (1, 2)
        oc = np.asarray(core_outs[c], np.float32).T
        out[s, qE * 512:qE * 512 + 512, :] = oc[:512]
        out[s, qL * 512:qL * 512 + 512, :] = oc[512:]
    return out


def kernel(**ii):
    in_maps = prepare_in_maps(ii)
    nc = build_program()
    res = run_bass_kernel_spmd(nc, in_maps, core_ids=list(range(NCORES)))
    return assemble_out([res.results[c]["OUT"] for c in range(NCORES)])


# version 06
# speedup vs baseline: 1.0973x; 1.0582x over previous
import numpy as np
import concourse.bass as bass
import concourse.mybir as mybir
from concourse.bass_utils import run_bass_kernel_spmd

F32 = mybir.dt.float32
BF16 = mybir.dt.bfloat16
FP8 = mybir.dt.float8e4
I8 = mybir.dt.int8
AF = mybir.ActivationFunctionType
ALU = mybir.AluOpType
AX = mybir.AxisListType

B, T, C, H = 4, 2048, 2048, 16
FC = C // 128
NTOK = 1024
RS = float(1.0 / np.sqrt(128.0))
EXP_SHIFT = -12.0
EPS_LN = 1e-5
EPS_L1 = 1e-5
NCORES = 8

GQ = [(0, 0), (1, 0), (1, 1), (0, 1)]
KV_KQ = 16 * 128 * 512
KV_VQ = 16 * 4 * 128 * 128
KV_Q = KV_KQ + KV_VQ
KV_BLK = 2 * KV_Q


class Sem:
    def __init__(self, nc, stack, name):
        self.h = stack.enter_context(nc.semaphore(name))
        self.n = 0


class Builder:
    def __init__(self):
        from contextlib import ExitStack
        self.nc = bass.Bass("TRN2", target_bir_lowering=False, debug=False)
        self.stack = ExitStack()
        self.q = {e: [] for e in ("tensor", "vector", "scalar", "gpsimd", "sync")}
        self.sems = []
        self.bank_free = {}
        self.buf_free = {}

    def sem(self, name):
        s = Sem(self.nc, self.stack, name)
        self.sems.append(s)
        return s

    def op(self, eng, f, inc=None, k=1):
        ss = getattr(self, "self_serial", {}).get(eng)
        if ss is not None and ss.n > 0:
            self.wait(eng, ss, ss.n)
        if inc is not None:
            inc.n += k
            self.q[eng].append(lambda e, f=f, s=inc.h, k=k: f(e).then_inc(s, k))
        else:
            self.q[eng].append(lambda e, f=f: f(e))

    def wait(self, eng, sem, val=None):
        v = sem.n if val is None else val
        if v <= 0:
            return
        self.q[eng].append(lambda e, s=sem.h, v=v: e.wait_ge(s, v))

    def barrier(self):
        snaps = [(s, s.n) for s in self.sems]
        for e in ("tensor", "vector", "scalar"):
            for s, v in snaps:
                if v > 0:
                    self.wait(e, s, v)

    def bwait(self, eng, key):
        if key in self.bank_free:
            s, v = self.bank_free[key]
            self.wait(eng, s, v)

    def bread(self, key, sem):
        self.bank_free[key] = (sem, sem.n)

    def fwait(self, eng, key):
        if key in self.buf_free:
            s, v = self.buf_free[key]
            self.wait(eng, s, v)

    def fread(self, key, sem):
        self.buf_free[key] = (sem, sem.n)


def build_program():
    _uid = [0]
    b = Builder()
    nc = b.nc
    O, W = b.op, b.wait

    din = lambda n, sh, dt: nc.dram_tensor(n, sh, dt, kind="ExternalInput").ap()
    X = din("X", [C, NTOK], F32)
    WqkvT = din("WqkvT", [12, 16, 128, 512], F32)
    bqkv2 = din("bqkv2", [128, 48], F32)
    Wo0T = din("Wo0T", [4, 16, 128, 512], F32)
    bo2 = din("bo2", [128, 16], F32)
    G0w = din("G0w", [12, 16, 128, 512], F32)
    U0w = din("U0w", [12, 16, 128, 512], F32)
    D0w = din("D0w", [4, 48, 128, 512], F32)
    g0s = din("g0s", [128, 48], F32)
    u0s = din("u0s", [128, 48], F32)
    d0s = din("d0s", [128, 16], F32)
    Wq1T = din("Wq1T", [4, 16, 128, 512], F32)
    Wk1T = din("Wk1T", [4, 16, 128, 512], F32)
    Wv1T = din("Wv1T", [4, 16, 128, 512], F32)
    Wo1T = din("Wo1T", [4, 16, 128, 512], F32)
    GAMB = din("GAMB", [128, 512], F32)
    BETB = din("BETB", [128, 512], F32)
    G1w = din("G1w", [16, 16, 128, 512], I8)
    U1w = din("U1w", [16, 16, 128, 512], I8)
    D1w = din("D1w", [4, 64, 128, 512], I8)
    g1s = din("g1s", [128, 64], F32)
    u1s = din("u1s", [128, 64], F32)
    d1s = din("d1s", [128, 16], F32)
    COS = din("COS", [128, NTOK], F32)
    SIN = din("SIN", [128, NTOK], F32)
    MSK = din("MSK", [16, 128, 512], F32)
    RT = din("RT", [128, 128], F32)
    IDN = din("IDN", [128, 128], F32)
    OUT = nc.dram_tensor("OUT", [C, NTOK], F32, kind="ExternalOutput").ap()
    X1 = nc.dram_tensor("X1", [C, NTOK], F32).ap()
    GD = nc.dram_tensor("GD", [C, NTOK], BF16).ap()
    KVO = nc.dram_tensor("KVO", [KV_BLK], BF16)
    KVG = nc.dram_tensor("KVG", [2 * KV_BLK], BF16)

    def kvo_k(q, h):
        off = q * KV_Q + h * (128 * 512)
        return KVO.ap()[off:off + 128 * 512].rearrange("(p t) -> p t", p=128)

    def kvo_v(q, h, tc):
        off = q * KV_Q + KV_KQ + h * (4 * 128 * 128) + tc * (128 * 128)
        return KVO.ap()[off:off + 128 * 128].rearrange("(p t) -> p t", p=128)

    def kvg_k(gq, h):
        r, qq = GQ[gq]
        off = r * KV_BLK + qq * KV_Q + h * (128 * 512)
        return KVG.ap()[off:off + 128 * 512].rearrange("(p t) -> p t", p=128)

    def kvg_v(gq, h):
        r, qq = GQ[gq]
        off = r * KV_BLK + qq * KV_Q + KV_KQ + h * (4 * 128 * 128)
        return KVG.ap()[off:off + 4 * 128 * 128].rearrange("(tc p t) -> p tc t", p=128, tc=4)

    sb = nc.alloc_sbuf_tensor
    xn = sb("xn", [128, FC, NTOK], BF16)
    wXt = sb("wX", [128, 16, 512], BF16)
    wYt = sb("wY", [128, 16, 512], BF16)
    cosb = sb("cosb", [128, NTOK], F32)
    sinb = sb("sinb", [128, NTOK], F32)
    rtb = sb("rtb", [128, 128], BF16)
    idnb = sb("idnb", [128, 128], BF16)
    onesb = sb("onesb", [1, 128], BF16)
    onesf = sb("onesf", [1, 128], F32)
    onescol = sb("onescol", [128, 1], BF16)
    sc_a = sb("sc_a", [128, 1024], F32)
    sc_b = sb("sc_b", [128, 512], F32)
    sc_c = sb("sc_c", [128, 2, 512], BF16)
    gstg = sb("gstg", [128, 4, 512], BF16)
    sc_d = sb("sc_d", [128, 2, 512], BF16)
    pp = sb("pp", [128, 3, 512], BF16)
    pm = sb("pm", [128, 2, 512], BF16)
    rz = sb("rz", [1, 512], F32)
    rzb = sb("rzb", [1, 512], BF16)
    rzt = sb("rzt", [128, 512], BF16)
    bqs = sb("bqs", [128, 16], F32)
    expsh = sb("expsh", [128, 1], F32)
    gamb = sb("gamb", [128, 512], F32)
    betb = sb("betb", [128, 512], F32)
    st1 = sb("st1", [128, 4], F32)
    st2 = sb("st2", [128, 4], F32)
    stm = sb("stm", [128, 4], F32)
    str_ = sb("str", [128, 4], F32)
    fmy = sb("fmy", [128, 512], F32)
    fmo = sb("fmo", [128, 512], BF16)
    xout = sb("xout", [128, 512], F32)
    bsc = sb("bsc", [128, 48 * 3 + 16 * 3 + 64 * 2], F32)

    ps = nc.alloc_psum_tensor
    pA = ps("pA", [128, 512], F32)
    pB = ps("pB", [128, 512], F32)
    pC = ps("pC", [128, 512], F32)
    pD = ps("pD", [128, 512], F32)
    pE = ps("pE", [128, 512], F32)
    pF = ps("pF", [128, 512], F32)
    pZ = ps("pZ", [128, 512], F32)
    PK = {0: (pA, "pA"), 1: (pB, "pB")}

    dC = b.sem("dC")
    dCg = b.sem("dCg")
    dX = [b.sem("dX0"), b.sem("dX1")]
    dWX, dWY = b.sem("dWX"), b.sem("dWY")
    dWZ, dWW = b.sem("dWZ"), b.sem("dWW")
    dKV = [b.sem("dKV0"), b.sem("dKV1")]
    dKVW = b.sem("dKVW")
    dVW = b.sem("dVW")
    dOUT = b.sem("dOUT")
    dGW = [b.sem("dGW0"), b.sem("dGW1")]
    tT = b.sem("tT")
    vV = b.sem("vV")
    sS = b.sem("sS")
    cc = b.sem("cc")
    gG = b.sem("gG")

    def VCH():
        W("vector", vV)
    WSEM = {id(wXt): dWX, id(wYt): dWY}

    def dma_gate(*engs):
        for eng in engs:
            for s in (tT, vV, sS):
                W(eng, s)

    def dma(eng, out, in_, sem):
        O(eng, lambda e, o=out, i=in_: e.dma_start(out=o, in_=i), sem, 16)
        return sem.n

    dma("gpsimd", rtb[:, :], RT, dCg)
    dma("gpsimd", idnb[:, :], IDN, dCg)
    dma("sync", cosb[:, :], COS, dC)
    dma("sync", sinb[:, :], SIN, dC)
    dma("sync", gamb[:, :], GAMB, dC)
    dma("sync", betb[:, :], BETB, dC)
    boff = {}
    _o = 0
    for nm, ap_, w_ in (("bqkv2", bqkv2, 48), ("g0s", g0s, 48), ("u0s", u0s, 48),
                        ("d0s", d0s, 16), ("bo2", bo2, 16), ("d1s", d1s, 16),
                        ("g1s", g1s, 64), ("u1s", u1s, 64)):
        boff[nm] = _o
        dma("sync", bsc[:, _o:_o + w_], ap_, dC)
        _o += w_

    def bcol(nm, j):
        return bsc[:, boff[nm] + j:boff[nm] + j + 1]

    W("vector", dC)
    W("vector", dCg)
    O("vector", lambda e: e.memset(onesb[:, :], 1.0))
    O("vector", lambda e: e.memset(onesf[:, :], 1.0))
    O("vector", lambda e: e.memset(expsh[:, :], EXP_SHIFT))
    O("vector", lambda e: e.memset(onescol[:, :], 1.0))
    O("vector", lambda e: e.tensor_scalar_mul(
        bqs[:, :], bsc[:, boff["bqkv2"]:boff["bqkv2"] + 16], RS), vV, 1)
    b.barrier()

    def wload(WSRC, grp, wbuf, view=None):
        sem = WSEM[id(wbuf)]
        b.fwait("gpsimd", ("w", id(wbuf)))
        dst = view if view is not None else wbuf[:, :, :]
        return dma("gpsimd", dst, WSRC[grp].rearrange("c p t -> p c t"), sem), sem

    def phase_ln(L):
        XSRC = X if L == 0 else X1
        XR = XSRC.rearrange("(c p) t -> p c t", p=128)
        if L == 1:
            W("sync", dOUT)
        from contextlib import ExitStack
        _uid[0] += 1
        u = _uid[0]
        with ExitStack() as es:
            xsg = es.enter_context(nc.sbuf_tensor(f"xs{u}", [128, 2, 4, 1024], F32))
            xqg = es.enter_context(nc.sbuf_tensor(f"xq{u}", [128, 2, 1024], BF16))
            xqs = es.enter_context(nc.sbuf_tensor(f"xsq{u}", [128, 2, 1024], BF16))
            mrow = es.enter_context(nc.sbuf_tensor(f"mrow{u}", [1, 1024], F32))
            rrow = es.enter_context(nc.sbuf_tensor(f"rrow{u}", [1, 1024], F32))
            trow = es.enter_context(nc.sbuf_tensor(f"trow{u}", [1, 1024], F32))
            t2row = es.enter_context(nc.sbuf_tensor(f"t2row{u}", [1, 1024], F32))
            xs = xsg
            dma_gate("sync")
            accs = [(pC, "pC"), (pD, "pD"), (pE, "pE"), (pF, "pF")]
            for a, k in accs:
                b.bwait("tensor", k)
            for grp in range(4):
                p = grp % 2
                b.fwait("sync", ("xs", p))
                b.fwait("sync", ("xs", p, "v"))
                snap = dma("sync", xs[:, p, :, :], XR[:, grp * 4:grp * 4 + 4, :], dX[p])
                for j in range(4):
                    c = grp * 4 + j
                    src = xs[:, p, j, :]
                    W("vector", dX[p], snap)
                    b.fwait("vector", ("xq", c % 2))
                    O("vector", lambda e, o=xqg[:, c % 2, :], i=src: e.tensor_copy(o, i), vV, 1)
                    W("scalar", dX[p], snap)
                    b.fwait("scalar", ("xsq", c % 2))
                    O("scalar", lambda e, o=xqs[:, c % 2, :], i=src:
                      e.activation(o, i, AF.Square), sS, 1)
                    W("tensor", vV)
                    W("tensor", sS)
                    for half in range(2):
                        hs = slice(half * 512, half * 512 + 512)
                        O("tensor", lambda e, o=accs[half][0][0:1, :], l=onescol[:, :],
                          r=xqg[:, c % 2, hs], s=(c == 0), p_=(c == 15):
                          e.matmul(o, l, r, start=s, stop=p_), tT, 1)
                        O("tensor", lambda e, o=accs[2 + half][0][0:1, :], l=onescol[:, :],
                          r=xqs[:, c % 2, hs], s=(c == 0), p_=(c == 15):
                          e.matmul(o, l, r, start=s, stop=p_), tT, 1)
                    b.fread(("xq", c % 2), tT)
                    b.fread(("xsq", c % 2), tT)
                b.fread(("xs", p), sS)
                b.fread(("xs", p, "v"), vV)
            W("scalar", tT)
            for half in range(2):
                hs = slice(half * 512, half * 512 + 512)
                O("scalar", lambda e, o=trow[0:1, hs], i=accs[half][0][0:1, :]:
                  e.copy(o, i), sS, 1)
                O("scalar", lambda e, o=t2row[0:1, hs], i=accs[2 + half][0][0:1, :]:
                  e.copy(o, i), sS, 1)
            for a, k in accs:
                b.bread(k, sS)
            W("vector", sS)
            O("vector", lambda e: e.tensor_scalar_mul(mrow[:, :], trow[:, :], 1.0 / 2048.0), vV, 1)
            O("vector", lambda e: e.tensor_scalar(t2row[:, :], t2row[:, :], 1.0 / 2048.0, EPS_LN,
                                                  ALU.mult, ALU.add), vV, 1)
            VCH()
            O("vector", lambda e: e.scalar_tensor_tensor(trow[:, :], mrow[:, :], 1.0, mrow[:, :],
                                                         ALU.mult, ALU.mult), vV, 1)
            VCH()
            O("vector", lambda e: e.tensor_tensor(t2row[:, :], t2row[:, :], trow[:, :],
                                                  ALU.subtract), vV, 1)
            VCH()
            O("vector", lambda e: e.reciprocal(t2row[:, :], t2row[:, :]), vV, 1)
            W("scalar", vV)
            O("scalar", lambda e: e.activation(rrow[:, :], t2row[:, :], AF.Sqrt), sS, 1)
            for grp in range(4):
                p = grp % 2
                b.fwait("sync", ("xs", p))
                b.fwait("sync", ("xs", p, "v"))
                snap = dma("sync", xs[:, p, :, :], XR[:, grp * 4:grp * 4 + 4, :], dX[p])
                for half in range(2):
                    hs = slice(half * 512, half * 512 + 512)
                    pk, bank = PK[half]
                    pk2, bank2 = [(pE, "pE"), (pF, "pF")][half]
                    if grp == 0:
                        W("tensor", sS)
                        b.bwait("tensor", bank)
                        O("tensor", lambda e, o=pk[:, :], r=mrow[0:1, hs]:
                          e.matmul(o, onesf[0:1, :], r, start=True, stop=True), tT, 1)
                        b.bwait("tensor", bank2)
                        O("tensor", lambda e, o=pk2[:, :], r=rrow[0:1, hs]:
                          e.matmul(o, onesf[0:1, :], r, start=True, stop=True), tT, 1)
                    for j in range(4):
                        c = grp * 4 + j
                        W("vector", dX[p], snap)
                        W("vector", tT)
                        b.fwait("vector", ("lntmp",))
                        O("vector", lambda e, o=sc_a[:, 0:512], i=xs[:, p, j, hs], m=pk[:, :]:
                          e.tensor_tensor(o, i, m, ALU.subtract), vV, 1)
                        VCH()
                        O("vector", lambda e, o=xn[:, c, hs], i=sc_a[:, 0:512], m=pk2[:, :]:
                          e.tensor_tensor(o, i, m, ALU.mult), vV, 1)
                        b.fread(("lntmp",), vV)
                b.fread(("xs", p), vV)
                b.fread(("xs", p, "v"), vV)
            b.bread("pA", vV)
            b.bread("pB", vV)
            b.bread("pE", vV)
            b.bread("pF", vV)
            b.barrier()

    def proj_rope(grps, kind, qdst=None, slot=None):
        pending = [None]

        def emit_fin():
            if pending[0] is not None:
                pending[0]()
                pending[0] = None

        for gi, grp in enumerate(grps):
            wbuf = (wXt, wYt)[gi % 2]
            snap, wsem = wload(WqkvT, grp, wbuf)
            for hh in range(4):
                h = gi * 4 + hh
                halves = range(2) if slot is None else [slot]
                for half in halves:
                    hs = slice(half * 512, half * 512 + 512)
                    pk, bank = PK[half % 2]
                    b.bwait("tensor", bank)
                    W("tensor", wsem, snap)
                    for kc in range(FC):
                        O("tensor", lambda e, o=pk[:, :], l=wbuf[:, kc, hh * 128:hh * 128 + 128],
                          r=xn[:, kc, hs], s=(kc == 0), p_=(kc == 15):
                          e.matmul(o, l, r, start=s, stop=p_), tT, 1)
                    b.fread(("w", id(wbuf)), tT)
                    emit_fin()
                    W("scalar", tT)
                    b.fwait("scalar", ("praw", half))
                    if kind == "q":
                        O("scalar", lambda e, o=sc_a[:, hs], i=pk[:, :], bb=bqs[:, h:h + 1]:
                          e.activation(o, i, AF.Identity, bias=bb, scale=RS), sS, 1)
                    else:
                        O("scalar", lambda e, o=sc_a[:, hs], i=pk[:, :],
                          bb=bcol("bqkv2", 16 + h): e.activation(o, i, AF.Identity, bias=bb), sS, 1)
                    b.bread(bank, sS)
                    W("vector", sS)
                    tile = (h * 2 + half) % 2
                    b.fwait("vector", ("pp", tile))
                    b.fwait("vector", ("pp2", tile))
                    O("vector", lambda e, o=pp[:, tile, :], i=sc_a[:, hs], m=sinb[:, hs]:
                      e.tensor_tensor(o, i, m, ALU.mult), vV, 1)
                    O("vector", lambda e, o=pm[:, tile, :], i=sc_a[:, hs], m=cosb[:, hs]:
                      e.tensor_tensor(o, i, m, ALU.mult), vV, 1)
                    b.fread(("praw", half), vV)
                    vsnap = vV.n

                    def fin(h=h, half=half, hs=hs, vsnap=vsnap, tile=tile):
                        b.bwait("tensor", "pF")
                        W("tensor", vV, vsnap)
                        O("tensor", lambda e, o=pF[:, :], l=rtb[:, :], r=pp[:, tile, :]:
                          e.matmul(o, l, r, start=True, stop=False), tT, 1)
                        O("tensor", lambda e, o=pF[:, :], l=idnb[:, :], r=pm[:, tile, :]:
                          e.matmul(o, l, r, start=False, stop=True), tT, 1)
                        b.fread(("pp", tile), tT)
                        b.fread(("pp2", tile), tT)
                        W("scalar", tT)
                        if kind == "q":
                            O("scalar", lambda e, o=qdst[:, h, 0:512], i=pF[:, :]: e.copy(o, i), sS, 1)
                            b.bread("pF", sS)
                        else:
                            b.fwait("scalar", ("kst",))
                            O("scalar", lambda e, o=rzt[:, :], i=pF[:, :]: e.copy(o, i), sS, 1)
                            b.bread("pF", sS)
                            W("sync", sS)
                            dma("sync", kvo_k(half, h), rzt[:, :], dKVW)
                            b.fread(("kst",), dKVW)
                    pending[0] = fin
        emit_fin()

    def proj_tm(WSRC, grps, consume, slot=None):
        tcs = range(8) if slot is None else range(slot * 4, slot * 4 + 4)
        pending = [None]
        for gi, grp in enumerate(grps):
            wbuf = (wXt, wYt)[gi % 2]
            snap, wsem = wload(WSRC, grp, wbuf)
            for tc in tcs:
                pk, bank = PK[tc % 2]
                b.bwait("tensor", bank)
                W("tensor", wsem, snap)
                for kc in range(FC):
                    O("tensor", lambda e, o=pk[:, :], l=xn[:, kc, tc * 128:tc * 128 + 128],
                      r=wbuf[:, kc, :], s=(kc == 0), p_=(kc == 15):
                      e.matmul(o, l, r, start=s, stop=p_), tT, 1)
                b.fread(("w", id(wbuf)), tT)
                if pending[0] is not None:
                    pending[0]()
                pending[0] = consume(gi, tc, pk, bank)
        if pending[0] is not None:
            pending[0]()

    def v_consume(gi, tc, pk, bank):
        half, tcq = tc // 4, tc % 4
        W("scalar", tT)
        b.fwait("scalar", ("vst",))
        O("scalar", lambda e, o=sc_d[:, 1, :], i=pk[:, :]: e.copy(o, i), sS, 1)
        b.bread(bank, sS)
        W("sync", sS)
        for hh in range(4):
            h = gi * 4 + hh
            dma("sync", kvo_v(half, h, tcq), sc_d[:, 1, hh * 128:hh * 128 + 128], dVW)
        b.fread(("vst",), dVW)
        return None

    def fm_consume_mk(is_q, qdst=None, pT=None):
        def consume(gi, tc, pk, bank):
            half, tcq = tc // 4, tc % 4
            W("vector", tT)
            b.fwait("vector", ("fmy",))
            O("vector", lambda e, o=fmy[:, :], i=pk[:, :], m=gamb[:, :]:
              e.tensor_tensor(o, i, m, ALU.mult), vV, 1)
            VCH()
            O("vector", lambda e, o=fmy[:, :], i=fmy[:, :], m=betb[:, :]:
              e.tensor_tensor(o, i, m, ALU.add), vV, 1)
            b.bread(bank, vV)
            VCH()
            O("vector", lambda e, o=st1[:, :],
              i=fmy.ap().rearrange("p (h d) -> p h d", d=128):
              e.reduce_sum(o, i, axis=AX.X), vV, 1)
            W("scalar", vV)
            b.fwait("scalar", ("fmsq",))
            O("scalar", lambda e, o=sc_b[:, 0:512], i=fmy[:, :]:
              e.activation(o, i, AF.Square), sS, 1)
            W("vector", sS)
            O("vector", lambda e, o=st2[:, :],
              i=sc_b.ap()[:, 0:512].rearrange("p (h d) -> p h d", d=128):
              e.reduce_sum(o, i, axis=AX.X), vV, 1)
            b.fread(("fmsq",), vV)
            VCH()
            O("vector", lambda e: e.tensor_scalar_mul(stm[:, :], st1[:, :], 1.0 / 128.0), vV, 1)
            O("vector", lambda e: e.tensor_scalar(str_[:, :], st2[:, :], 1.0 / 128.0, EPS_L1,
                                                  ALU.mult, ALU.add), vV, 1)
            VCH()
            O("vector", lambda e: e.scalar_tensor_tensor(st1[:, :], stm[:, :], 1.0, stm[:, :],
                                                         ALU.mult, ALU.mult), vV, 1)
            VCH()
            O("vector", lambda e: e.tensor_tensor(str_[:, :], str_[:, :], st1[:, :],
                                                  ALU.subtract), vV, 1)
            VCH()
            O("vector", lambda e: e.reciprocal(str_[:, :], str_[:, :]), vV, 1)
            W("scalar", vV)
            O("scalar", lambda e: e.activation(str_[:, :], str_[:, :], AF.Sqrt), sS, 1)
            if is_q:
                W("vector", sS)
                VCH()
                O("vector", lambda e: e.tensor_scalar_mul(str_[:, :], str_[:, :], RS), vV, 1)
            for hh in range(4):
                W("vector", sS)
                VCH()
                b.fwait("vector", ("fmo", hh))
                O("vector", lambda e, o=fmo[:, hh * 128:hh * 128 + 128],
                  i=fmy[:, hh * 128:hh * 128 + 128], m1=stm[:, hh:hh + 1],
                  m2=str_[:, hh:hh + 1]:
                  e.tensor_scalar(o, i, m1, m2, ALU.subtract, ALU.mult), vV, 1)
            b.fread(("fmy",), vV)
            vsnap = vV.n

            def fin(gi=gi, tc=tc, half=half, tcq=tcq, vsnap=vsnap):
                for hh in range(4):
                    h = gi * 4 + hh
                    b.bwait("tensor", "pT")
                    W("tensor", vV, vsnap)
                    O("tensor", lambda e, o=pT[:, 0:128], i=fmo[:, hh * 128:hh * 128 + 128]:
                      e.transpose(o, i, idnb[:, :]), tT, 1)
                    if hh == 3:
                        b.fread(("fmo", 0), tT)
                        b.fread(("fmo", 1), tT)
                        b.fread(("fmo", 2), tT)
                        b.fread(("fmo", 3), tT)
                    W("scalar", tT)
                    if is_q:
                        tl = tc % 4
                        O("scalar", lambda e, o=qdst[:, h, tl * 128:tl * 128 + 128],
                          i=pT[:, 0:128]: e.copy(o, i), sS, 1)
                        b.bread("pT", sS)
                    else:
                        b.fwait("scalar", ("kst",))
                        O("scalar", lambda e, o=rzt[:, 0:128], i=pT[:, 0:128]: e.copy(o, i), sS, 1)
                        b.bread("pT", sS)
                        W("sync", sS)
                        dma("sync",
                            kvo_k(half, h).rearrange("p (a t) -> p a t", a=4)[:, tcq, :],
                            rzt[:, 0:128], dKVW)
                        b.fread(("kst",), dKVW)
            return fin
        return consume

    def phase_ffn(L):
        if L == 0:
            NMG, NKD, WG, WU, WD = 12, 48, G0w, U0w, D0w
            gsn, usn, dsn = "g0s", "u0s", "d0s"
        else:
            NMG, NKD, WG, WU, WD = 16, 64, G1w, U1w, D1w
            gsn, usn, dsn = "g1s", "u1s", "d1s"
        nseg = NKD // 16
        GDr = GD.rearrange("(c p) t -> p c t", p=128)
        from contextlib import ExitStack
        _uid[0] += 1
        with ExitStack() as es:
            tokw = 512
            fuse = es.enter_context(nc.sbuf_tensor(f"fuse{_uid[0]}", [128, NKD, tokw], BF16))
            wZt = es.enter_context(nc.sbuf_tensor(f"wZ{_uid[0]}", [128, 16, 512], BF16))
            wWt = es.enter_context(nc.sbuf_tensor(f"wW{_uid[0]}", [128, 16, 512], BF16))
            WSEM[id(wZt)] = dWZ
            WSEM[id(wWt)] = dWW
            tiles = [wXt, wYt, wZt, wWt]
            if L == 0:
                views = [t.ap().bitcast(FP8)[:, :, 0:512] for t in tiles]
            else:
                views = [t[:, :, :] for t in tiles]

            def mk_loader(srcs):
                snaps = [None] * len(srcs)
                emitted = [0]

                nb = len(tiles)
                def ensure(i):
                    while emitted[0] <= min(i + (nb - 2), len(srcs) - 1):
                        j = emitted[0]
                        t = tiles[j % nb]
                        b.fwait("gpsimd", ("w", id(t)))
                        snaps[j] = dma("gpsimd", views[j % nb],
                                       srcs[j].rearrange("c p t -> p c t"), WSEM[id(t)])
                        emitted[0] += 1
                return snaps, ensure

            halves = [0, 1]
            for hpass in halves:
                srcs = []
                for mg in range(NMG):
                    srcs.append(WG[mg])
                    srcs.append(WU[mg])
                for fg in range(4):
                    WDr = WD[fg].rearrange("(s c) p t -> s c p t", c=16)
                    for seg in range(nseg):
                        srcs.append(WDr[seg])
                snaps, ensure = mk_loader(srcs)
                tok_halves = [hpass]
                li = 0
                for mg in range(NMG):
                    ensure(li + 1)
                    for mc in range(4):
                        m = mg * 4 + mc
                        for hi, half in enumerate(tok_halves):
                            hs = slice(half * 512, half * 512 + 512)
                            fs = slice(hi * 512, hi * 512 + 512)
                            slot = (m % 2) * 2 + hi
                            for (scn, func, banks, idx) in (
                                    (gsn, AF.Silu, ((pA, "pA"), (pB, "pB")), li),
                                    (usn, AF.Copy, ((pC, "pC"), (pD, "pD")), li + 1)):
                                wv = views[idx % len(tiles)]
                                wt = tiles[idx % len(tiles)]
                                pk, bank = banks[hi % len(banks)]
                                b.bwait("tensor", bank)
                                W("tensor", WSEM[id(wt)], snaps[idx])
                                for kc in range(FC):
                                    O("tensor", lambda e, o=pk[:, :],
                                      l=wv[:, kc, mc * 128:mc * 128 + 128],
                                      r=xn[:, kc, hs], s=(kc == 0), p_=(kc == 15):
                                      e.matmul(o, l, r, start=s, stop=p_), tT, 1)
                                b.fread(("w", id(wt)), tT)
                                W("scalar", tT)
                                if func is AF.Silu:
                                    b.fwait("scalar", ("gstg", slot))
                                    O("scalar", lambda e, o=gstg[:, slot, :], i=pk[:, :],
                                      sc=bcol(scn, m): e.activation(o, i, AF.Silu, scale=sc),
                                      sS, 1)
                                    b.bread(bank, sS)
                                else:
                                    b.fwait("scalar", ("gu", hi))
                                    O("scalar", lambda e, o=sc_d[:, hi, :], i=pk[:, :],
                                      sc=bcol(scn, m): e.activation(o, i, AF.Copy, scale=sc),
                                      sS, 1)
                                    b.bread(bank, sS)
                                    W("vector", sS)
                                    b.fwait("vector", ("fuse", m, hi))
                                    O("vector", lambda e, o=fuse[:, m, fs], a=gstg[:, slot, :],
                                      bb=sc_d[:, hi, :]: e.tensor_tensor(o, a, bb, ALU.mult),
                                      vV, 1)
                                    b.fread(("gstg", slot), vV)
                                    b.fread(("gu", hi), vV)
                    li += 2
                dbanks = [(pA, "pA"), (pB, "pB"), (pC, "pC"), (pD, "pD")]
                nunit = len(tok_halves)
                for fg in range(4):
                    for _, kk in dbanks:
                        b.bwait("tensor", kk)
                    for seg in range(nseg):
                        idx = li + seg
                        ensure(idx)
                        wv = views[idx % len(tiles)]
                        wt = tiles[idx % len(tiles)]
                        W("tensor", WSEM[id(wt)], snaps[idx])
                        if seg == 0:
                            W("tensor", vV)
                        for kc in range(16):
                            k = seg * 16 + kc
                            for fo4 in range(4):
                                for hi in range(nunit):
                                    fs = slice(hi * 512, hi * 512 + 512)
                                    dbk = dbanks[fo4 * nunit + hi][0]
                                    O("tensor", lambda e, o=dbk[:, :],
                                      l=wv[:, kc, fo4 * 128:fo4 * 128 + 128],
                                      r=fuse[:, k, fs],
                                      s=(k == 0), p_=(k == NKD - 1):
                                      e.matmul(o, l, r, start=s, stop=p_), tT, 1)
                        b.fread(("w", id(wt)), tT)
                    li += nseg
                    W("scalar", tT)
                    for fo4 in range(4):
                        fo = fg * 4 + fo4
                        for hi, half in enumerate(tok_halves):
                            hs = slice(half * 512, half * 512 + 512)
                            ui = (fo4 * nunit + hi) % 2
                            b.fwait("scalar", ("pp", ui))
                            O("scalar", lambda e, o=pp[:, ui, :],
                              i=dbanks[fo4 * nunit + hi][0][:, :],
                              sc=bcol(dsn, fo): e.activation(o, i, AF.Silu, scale=sc), sS, 1)
                            W("sync", sS)
                            dma("sync", GDr[:, fo, hs], pp[:, ui, :], dGW[ui])
                            b.fread(("pp", ui), dGW[ui])
                    for _, kk in dbanks:
                        b.bread(kk, sS)
                for m in range(NKD):
                    for hi in range(nunit):
                        b.fread(("fuse", m, hi), tT)
            b.barrier()

    def phase_attn(L, qbuf, obuf, slot, mskb, kbufs, vbufs):
        nch = 8 if slot == 0 else 16
        qs = slice(slot * 512, slot * 512 + 512)
        quarters = [0, 1] if slot == 0 else [0, 1, 2, 3]
        rot = [(pB, "pB"), (pC, "pC"), (pD, "pD")]
        rot_i = [0]

        def next_rot():
            r = rot[rot_i[0] % 3]
            rot_i[0] += 1
            return r

        def issue_kv(h):
            p = h % 2
            b.fwait("sync", ("kv", p))
            for ji, gq in enumerate(quarters):
                dma("sync", kbufs[p][:, ji, :], kvg_k(gq, h), dKV[p])
                dma("sync", vbufs[p][:, ji * 4:ji * 4 + 4, :], kvg_v(gq, h), dKV[p])
            return dKV[p].n

        pend_norm = [None]

        def emit_norm():
            if pend_norm[0] is not None:
                pend_norm[0]()
                pend_norm[0] = None

        kvsnap = issue_kv(0)
        for h in range(H):
            p = h % 2
            mysnap = kvsnap
            if h + 1 < H:
                kvsnap = issue_kv(h + 1)
            ob, obk = (pE, "pE") if h % 2 == 0 else (pA, "pA")
            zb, zbk = (pZ, "pZ") if h % 2 == 0 else (pF, "pF")
            W("tensor", dKV[p], mysnap)
            stage2 = []

            def pop_one():
                c2, pS2, bankS2, tsnap = stage2.pop(0)
                ppj = c2 % 3
                masked = True if slot == 0 else (c2 >= 8)
                mski = slot * 8 + (c2 if slot == 0 else c2 - 8)
                W("scalar", tT, tsnap)
                b.fwait("scalar", ("pp", ppj))
                if L == 0:
                    O("scalar", lambda e, o=pp[:, ppj, :], i=pS2[:, :]:
                      e.activation(o, i, AF.Exp, bias=expsh[:, 0:1]), sS, 1)
                else:
                    O("scalar", lambda e, o=pp[:, ppj, :], i=pS2[:, :]:
                      e.activation(o, i, AF.Square), sS, 1)
                b.bread(bankS2, sS)
                ssnap = sS.n
                if masked:
                    W("gpsimd", sS, ssnap)
                    b.fwait("gpsimd", ("pmm", c2 % 2))
                    O("gpsimd", lambda e, o=pm[:, c2 % 2, :], i=pp[:, ppj, :],
                      m=mskb[:, mski, :]: e.tensor_tensor(o, i, m, ALU.mult), gG, 1)
                    b.fread(("pp", ppj), gG)
                    src_ = pm[:, c2 % 2, :]
                    W("tensor", gG)
                else:
                    src_ = pp[:, ppj, :]
                    W("tensor", sS, ssnap)
                if c2 == 0:
                    b.bwait("tensor", obk)
                    b.bwait("tensor", zbk)
                O("tensor", lambda e, o=ob[:, :], l=vbufs[p][:, c2, :], r=src_,
                  s=(c2 == 0), p_=(c2 == nch - 1):
                  e.matmul(o, l, r, start=s, stop=p_), tT, 1)
                O("tensor", lambda e, o=zb[0:1, :], l=onescol[:, :], r=src_,
                  s=(c2 == 0), p_=(c2 == nch - 1):
                  e.matmul(o, l, r, start=s, stop=p_), tT, 1)
                if masked:
                    b.fread(("pmm", c2 % 2), tT)
                else:
                    b.fread(("pp", ppj), tT)

            for c in range(nch + 2):
                if stage2 and (len(stage2) == 2 or c >= nch):
                    pop_one()
                if c == 2:
                    emit_norm()
                if c < nch:
                    pS, bankS = next_rot()
                    b.bwait("tensor", bankS)
                    O("tensor", lambda e, o=pS[:, :],
                      l=kbufs[p][:, c // 4, (c % 4) * 128:(c % 4) * 128 + 128],
                      r=qbuf[:, h, 0:512]: e.matmul(o, l, r, start=True, stop=True), tT, 1)
                    stage2.append((c, pS, bankS, tT.n))
            b.fread(("kv", p), tT)
            tchunks = tT.n

            def norm(h=h, ob=ob, obk=obk, zb=zb, zbk=zbk, tchunks=tchunks):
                W("vector", tT, tchunks)
                b.fwait("vector", ("rz",))
                if L == 0:
                    O("vector", lambda e: e.reciprocal(rz[:, :], zb[0:1, :]), vV, 1)
                else:
                    O("vector", lambda e: e.tensor_scalar_add(rz[:, :], zb[0:1, :], EPS_L1), vV, 1)
                    VCH()
                    O("vector", lambda e: e.reciprocal(rz[:, :], rz[:, :]), vV, 1)
                VCH()
                O("vector", lambda e: e.tensor_copy(rzb[:, :], rz[:, :]), vV, 1)
                b.bread(zbk, vV)
                po, pok = next_rot()
                b.bwait("tensor", pok)
                W("tensor", vV)
                O("tensor", lambda e, o=po[:, :]: e.matmul(o, onesb[:, :], rzb[:, :],
                                                           start=True, stop=True), tT, 1)
                W("scalar", tT)
                b.fwait("scalar", ("kst",))
                O("scalar", lambda e, o=rzt[:, :], i=po[:, :]: e.copy(o, i), sS, 1)
                b.bread(pok, sS)
                W("vector", sS)
                if L == 0:
                    b.fwait("vector", ("onrm",))
                    O("vector", lambda e, o=sc_c[:, 0, :], a=ob[:, :], bb=rzt[:, :]:
                      e.tensor_tensor(o, a, bb, ALU.mult), vV, 1)
                    VCH()
                    O("vector", lambda e, o=obuf[:, h, qs], i=sc_c[:, 0, :],
                      bb=bcol("bqkv2", 32 + h): e.tensor_scalar_add(o, i, bb), vV, 1)
                    b.fread(("onrm",), vV)
                else:
                    O("vector", lambda e, o=obuf[:, h, qs], a=ob[:, :], bb=rzt[:, :]:
                      e.tensor_tensor(o, a, bb, ALU.mult), vV, 1)
                b.bread(obk, vV)
                b.fread(("rz",), vV)
                b.fread(("kst",), vV)
            pend_norm[0] = norm
        emit_norm()

    def phase_wo(L, obuf):
        XSRC = X if L == 0 else X1
        XR = XSRC.rearrange("(c p) t -> p c t", p=128)
        GDr = GD.rearrange("(c p) t -> p c t", p=128)
        XD = (X1 if L == 0 else OUT).rearrange("(c p) t -> p c t", p=128)
        WoT = Wo0T if L == 0 else Wo1T
        from contextlib import ExitStack
        _uid[0] += 1
        with ExitStack() as es:
            xf = es.enter_context(nc.sbuf_tensor(f"xf{_uid[0]}", [128, 2, 1024], F32))
            gf = es.enter_context(nc.sbuf_tensor(f"gf{_uid[0]}", [128, 2, 1024], BF16))
            dma_gate("sync")
            W("sync", dGW[0])
            W("sync", dGW[1])
            for grp in range(4):
                wbuf = (wXt, wYt)[grp % 2]
                snap, wsem = wload(WoT, grp, wbuf)
                for hh in range(4):
                    fo = grp * 4 + hh
                    p = fo % 2
                    b.fwait("sync", ("xf", p))
                    xsnap = dma("sync", xf[:, p, :], XR[:, fo, :], dX[p])
                    dma("sync", gf[:, p, :], GDr[:, fo, :], dX[p])
                    xsnap = dX[p].n
                    for half in range(2):
                        hs = slice(half * 512, half * 512 + 512)
                        pk, bank = PK[half]
                        b.bwait("tensor", bank)
                        W("tensor", wsem, snap)
                        for kc in range(FC):
                            O("tensor", lambda e, o=pk[:, :],
                              l=wbuf[:, kc, hh * 128:hh * 128 + 128],
                              r=obuf[:, kc, hs], s=(kc == 0), p_=(kc == 15):
                              e.matmul(o, l, r, start=s, stop=p_), tT, 1)
                        b.fread(("w", id(wbuf)), tT)
                        W("scalar", tT)
                        b.fwait("scalar", ("wos", half))
                        if L == 0:
                            O("scalar", lambda e, o=sc_c[:, half, :], i=pk[:, :],
                              bb=bcol("bo2", fo): e.activation(o, i, AF.Identity, bias=bb), sS, 1)
                        else:
                            O("scalar", lambda e, o=sc_c[:, half, :], i=pk[:, :]:
                              e.copy(o, i), sS, 1)
                        b.bread(bank, sS)
                        W("vector", sS)
                        W("vector", dX[p], xsnap)
                        b.fwait("vector", ("cmb",))
                        O("vector", lambda e, o=sc_b[:, :], i=sc_c[:, half, :],
                          m=gf[:, p, hs]: e.tensor_tensor(o, i, m, ALU.mult), vV, 1)
                        b.fread(("wos", half), vV)
                        b.fwait("vector", ("xout",))
                        VCH()
                        O("vector", lambda e, o=xout[:, :], i=sc_b[:, :], m=xf[:, p, hs]:
                          e.tensor_tensor(o, i, m, ALU.add), vV, 1)
                        b.fread(("cmb",), vV)
                        W("sync", vV)
                        dma("sync", XD[:, fo, hs], xout[:, :], dOUT)
                        b.fread(("xout",), dOUT)
                    b.fread(("xf", p), vV)
            b.barrier()

    from contextlib import ExitStack
    for L in range(2):
        esL = ExitStack()
        pT = None
        if L == 1:
            pT = esL.enter_context(nc.psum_tensor("pT1", [128, 1024], BF16))
        phase_ln(L)
        if L == 0:
            proj_rope([4, 5, 6, 7], "k")
            proj_tm(WqkvT, [8, 9, 10, 11], v_consume)
        else:
            proj_tm(Wk1T, [0, 1, 2, 3], fm_consume_mk(False, pT=pT))
            proj_tm(Wv1T, [0, 1, 2, 3], v_consume)
        W("gpsimd", dKVW)
        W("gpsimd", dVW)
        O("gpsimd", lambda e: e.collective_compute(
            "AllGather", ALU.bypass,
            replica_groups=[[0, 1], [2, 3], [4, 5], [6, 7]],
            ins=[KVO.ap().opt()], outs=[KVG.ap().opt()]), cc, 1)
        phase_ffn(L)
        with ExitStack() as es:
            obuf = es.enter_context(nc.sbuf_tensor(f"obuf{L}", [128, FC, NTOK], BF16))
            with ExitStack() as es2:
                qbuf = es2.enter_context(nc.sbuf_tensor(f"qbuf{L}", [128, FC, 512], BF16))
                mskb = es2.enter_context(nc.sbuf_tensor(f"mskb{L}", [128, 16, 512], BF16))
                kbufs = [es2.enter_context(nc.sbuf_tensor(f"kb{L}{p}", [128, 4, 512], BF16))
                         for p in range(2)]
                vbufs = [es2.enter_context(nc.sbuf_tensor(f"vb{L}{p}", [128, 16, 128], BF16))
                         for p in range(2)]
                dma_gate("gpsimd", "sync")
                dma("gpsimd", mskb[:, :, :], MSK.rearrange("c p t -> p c t"), dCg)
                W("sync", cc)
                W("vector", dCg)
                for slot in range(2):
                    if L == 0:
                        proj_rope([0, 1, 2, 3], "q", qbuf, slot)
                    else:
                        proj_tm(Wq1T, [0, 1, 2, 3], fm_consume_mk(True, qbuf, pT=pT), slot)
                    b.barrier()
                    phase_attn(L, qbuf, obuf, slot, mskb, kbufs, vbufs)
                    b.barrier()
            phase_wo(L, obuf)
        esL.close()

    W("gpsimd", dOUT)
    W("gpsimd", dGW[0])
    W("gpsimd", dGW[1])
    W("gpsimd", dKVW)
    W("gpsimd", dVW)
    W("gpsimd", dC)
    W("gpsimd", dCg)
    for s in dX + dKV + [dWX, dWY, dWZ, dWW]:
        W("gpsimd", s)

    with nc.Block() as block:
        @block.tensor
        def _(e):
            for f in b.q["tensor"]:
                f(e)

        @block.vector
        def _(e):
            for f in b.q["vector"]:
                f(e)

        @block.scalar
        def _(e):
            for f in b.q["scalar"]:
                f(e)

        @block.gpsimd
        def _(e):
            for f in b.q["gpsimd"]:
                f(e)

        @block.sync
        def _(e):
            for f in b.q["sync"]:
                f(e)

    b.stack.close()
    return nc


def _blocked(w2d):
    K, M = w2d.shape
    return np.ascontiguousarray(
        w2d.reshape(K // 128, 128, M // 512, 512).transpose(2, 0, 1, 3))


def _chunk_cols(v):
    return np.ascontiguousarray(v.reshape(-1, 128).T)


def _rope_tables(pos):
    p = np.arange(64, dtype=np.float32)
    theta = np.power(np.float32(10000.0),
                     (-(np.float32(2.0) ** p) / np.float32(128.0)).astype(np.float32))
    ang = pos.astype(np.float32)[None, :] * theta[:, None]
    ang = np.concatenate([ang, ang], axis=0)
    return np.cos(ang).astype(np.float32), np.sin(ang).astype(np.float32)


def _rot_matrix():
    R = np.zeros((128, 128), np.float32)
    for i in range(64):
        R[i, i + 64] = -1.0
        R[i + 64, i] = 1.0
    return np.ascontiguousarray(R.T)


def prepare_in_maps(ii):
    x = np.asarray(ii["L0_attn_x"], np.float32)
    shared = {
        "WqkvT": _blocked(np.ascontiguousarray(np.asarray(ii["L0_attn_Wqkv"], np.float32).T)),
        "bqkv2": _chunk_cols(np.asarray(ii["L0_attn_bqkv"], np.float32)),
        "Wo0T": _blocked(np.ascontiguousarray(np.asarray(ii["L0_attn_Wo"], np.float32).T)),
        "bo2": _chunk_cols(np.asarray(ii["L0_attn_bo"], np.float32)),
        "G0w": _blocked(np.asarray(ii["L0_ffn_gate_w"], np.float32)),
        "U0w": _blocked(np.asarray(ii["L0_ffn_up_w"], np.float32)),
        "D0w": _blocked(np.asarray(ii["L0_ffn_down_w"], np.float32)),
        "g0s": _chunk_cols(np.asarray(ii["L0_ffn_gate_scale"], np.float32)),
        "u0s": _chunk_cols(np.asarray(ii["L0_ffn_up_scale"], np.float32)),
        "d0s": _chunk_cols(np.asarray(ii["L0_ffn_down_scale"], np.float32)),
        "Wq1T": _blocked(np.ascontiguousarray(np.asarray(ii["L1_attn_Wq"], np.float32).T)),
        "Wk1T": _blocked(np.ascontiguousarray(np.asarray(ii["L1_attn_Wk"], np.float32).T)),
        "Wv1T": _blocked(np.ascontiguousarray(np.asarray(ii["L1_attn_Wv"], np.float32).T)),
        "Wo1T": _blocked(np.ascontiguousarray(np.asarray(ii["L1_attn_Wo"], np.float32).T)),
        "GAMB": np.ascontiguousarray(
            np.tile(np.asarray(ii["L1_attn_gamma"], np.float32)[None, :], (128, 4))),
        "BETB": np.ascontiguousarray(
            np.tile(np.asarray(ii["L1_attn_beta"], np.float32)[None, :], (128, 4))),
        "G1w": _blocked(np.asarray(ii["L1_ffn_w_gate"])),
        "U1w": _blocked(np.asarray(ii["L1_ffn_w_up"])),
        "D1w": _blocked(np.asarray(ii["L1_ffn_w_down"])),
        "g1s": _chunk_cols(np.asarray(ii["L1_ffn_s_gate"], np.float32)),
        "u1s": _chunk_cols(np.asarray(ii["L1_ffn_s_up"], np.float32)),
        "d1s": _chunk_cols(np.asarray(ii["L1_ffn_s_down"], np.float32)),
        "RT": _rot_matrix(),
        "IDN": np.eye(128, dtype=np.float32),
    }

    in_maps = []
    for c in range(NCORES):
        s, par = c // 2, c % 2
        qE, qL = (0, 3) if par == 0 else (1, 2)
        xc = np.concatenate([x[s, qE * 512:qE * 512 + 512, :],
                             x[s, qL * 512:qL * 512 + 512, :]], axis=0)
        pos = np.concatenate([np.arange(qE * 512, qE * 512 + 512),
                              np.arange(qL * 512, qL * 512 + 512)]) + 1
        cosc, sinc = _rope_tables(pos)
        mskc = np.zeros((16, 128, 512), np.float32)
        for slot in range(2):
            tqb = (qE if slot == 0 else qL) * 512
            for j in range(8):
                cglob = j if slot == 0 else 8 + j
                tk = cglob * 128 + np.arange(128)[:, None]
                tq = tqb + np.arange(512)[None, :]
                mskc[slot * 8 + j] = (tk <= tq).astype(np.float32)
        m = dict(shared)
        m["X"] = np.ascontiguousarray(xc.T)
        m["COS"] = cosc
        m["SIN"] = sinc
        m["MSK"] = mskc
        in_maps.append(m)
    return in_maps


def assemble_out(core_outs):
    out = np.empty((B, T, C), np.float32)
    for c in range(NCORES):
        s, par = c // 2, c % 2
        qE, qL = (0, 3) if par == 0 else (1, 2)
        oc = np.asarray(core_outs[c], np.float32).T
        out[s, qE * 512:qE * 512 + 512, :] = oc[:512]
        out[s, qL * 512:qL * 512 + 512, :] = oc[512:]
    return out


def kernel(**ii):
    in_maps = prepare_in_maps(ii)
    nc = build_program()
    res = run_bass_kernel_spmd(nc, in_maps, core_ids=list(range(NCORES)))
    return assemble_out([res.results[c]["OUT"] for c in range(NCORES)])


# version 07
# speedup vs baseline: 1.1310x; 1.0307x over previous
import numpy as np
import concourse.bass as bass
import concourse.mybir as mybir
from concourse.bass_utils import run_bass_kernel_spmd

F32 = mybir.dt.float32
BF16 = mybir.dt.bfloat16
FP8 = mybir.dt.float8e4
I8 = mybir.dt.int8
AF = mybir.ActivationFunctionType
ALU = mybir.AluOpType
AX = mybir.AxisListType

B, T, C, H = 4, 2048, 2048, 16
FC = C // 128
NTOK = 1024
RS = float(1.0 / np.sqrt(128.0))
EXP_SHIFT = -12.0
EPS_LN = 1e-5
EPS_L1 = 1e-5
NCORES = 8

GQ = [(0, 0), (1, 0), (1, 1), (0, 1)]
KV_KQ = 16 * 128 * 512
KV_VQ = 16 * 4 * 128 * 128
KV_Q = KV_KQ + KV_VQ
KV_BLK = 2 * KV_Q


class Sem:
    def __init__(self, nc, stack, name):
        self.h = stack.enter_context(nc.semaphore(name))
        self.n = 0


class Builder:
    def __init__(self):
        from contextlib import ExitStack
        self.nc = bass.Bass("TRN2", target_bir_lowering=False, debug=False)
        self.stack = ExitStack()
        self.q = {e: [] for e in ("tensor", "vector", "scalar", "gpsimd", "sync")}
        self.sems = []
        self.bank_free = {}
        self.buf_free = {}

    def sem(self, name):
        s = Sem(self.nc, self.stack, name)
        self.sems.append(s)
        return s

    def op(self, eng, f, inc=None, k=1):
        ss = getattr(self, "self_serial", {}).get(eng)
        if ss is not None and ss.n > 0:
            self.wait(eng, ss, ss.n)
        if inc is not None:
            inc.n += k
            self.q[eng].append(lambda e, f=f, s=inc.h, k=k: f(e).then_inc(s, k))
        else:
            self.q[eng].append(lambda e, f=f: f(e))

    def wait(self, eng, sem, val=None):
        v = sem.n if val is None else val
        if v <= 0:
            return
        self.q[eng].append(lambda e, s=sem.h, v=v: e.wait_ge(s, v))

    def barrier(self):
        snaps = [(s, s.n) for s in self.sems]
        for e in ("tensor", "vector", "scalar"):
            for s, v in snaps:
                if v > 0:
                    self.wait(e, s, v)

    def bwait(self, eng, key):
        if key in self.bank_free:
            s, v = self.bank_free[key]
            self.wait(eng, s, v)

    def bread(self, key, sem):
        self.bank_free[key] = (sem, sem.n)

    def fwait(self, eng, key):
        if key in self.buf_free:
            s, v = self.buf_free[key]
            self.wait(eng, s, v)

    def fread(self, key, sem):
        self.buf_free[key] = (sem, sem.n)


def build_program():
    _uid = [0]
    b = Builder()
    nc = b.nc
    O, W = b.op, b.wait

    din = lambda n, sh, dt: nc.dram_tensor(n, sh, dt, kind="ExternalInput").ap()
    X = din("X", [C, NTOK], F32)
    WqkvT = din("WqkvT", [12, 16, 128, 512], F32)
    bqkv2 = din("bqkv2", [128, 48], F32)
    Wo0T = din("Wo0T", [4, 16, 128, 512], F32)
    bo2 = din("bo2", [128, 16], F32)
    G0w = din("G0w", [12, 16, 128, 512], F32)
    U0w = din("U0w", [12, 16, 128, 512], F32)
    D0w = din("D0w", [4, 48, 128, 512], F32)
    g0s = din("g0s", [128, 48], F32)
    u0s = din("u0s", [128, 48], F32)
    d0s = din("d0s", [128, 16], F32)
    Wq1T = din("Wq1T", [4, 16, 128, 512], F32)
    Wk1T = din("Wk1T", [4, 16, 128, 512], F32)
    Wv1T = din("Wv1T", [4, 16, 128, 512], F32)
    Wo1T = din("Wo1T", [4, 16, 128, 512], F32)
    GAMB = din("GAMB", [128, 512], F32)
    BETB = din("BETB", [128, 512], F32)
    G1w = din("G1w", [16, 16, 128, 512], I8)
    U1w = din("U1w", [16, 16, 128, 512], I8)
    D1w = din("D1w", [4, 64, 128, 512], I8)
    g1s = din("g1s", [128, 64], F32)
    u1s = din("u1s", [128, 64], F32)
    d1s = din("d1s", [128, 16], F32)
    COS = din("COS", [128, NTOK], F32)
    SIN = din("SIN", [128, NTOK], F32)
    MSK = din("MSK", [16, 128, 512], F32)
    RT = din("RT", [128, 128], F32)
    IDN = din("IDN", [128, 128], F32)
    OUT = nc.dram_tensor("OUT", [C, NTOK], F32, kind="ExternalOutput").ap()
    X1 = nc.dram_tensor("X1", [C, NTOK], F32).ap()
    GD = nc.dram_tensor("GD", [C, NTOK], BF16).ap()
    KVO = nc.dram_tensor("KVO", [KV_BLK], BF16)
    KVG = nc.dram_tensor("KVG", [2 * KV_BLK], BF16)

    def kvo_k(q, h):
        off = q * KV_Q + h * (128 * 512)
        return KVO.ap()[off:off + 128 * 512].rearrange("(p t) -> p t", p=128)

    def kvo_v(q, h, tc):
        off = q * KV_Q + KV_KQ + h * (4 * 128 * 128) + tc * (128 * 128)
        return KVO.ap()[off:off + 128 * 128].rearrange("(p t) -> p t", p=128)

    def kvg_k(gq, h):
        r, qq = GQ[gq]
        off = r * KV_BLK + qq * KV_Q + h * (128 * 512)
        return KVG.ap()[off:off + 128 * 512].rearrange("(p t) -> p t", p=128)

    def kvg_v(gq, h):
        r, qq = GQ[gq]
        off = r * KV_BLK + qq * KV_Q + KV_KQ + h * (4 * 128 * 128)
        return KVG.ap()[off:off + 4 * 128 * 128].rearrange("(tc p t) -> p tc t", p=128, tc=4)

    sb = nc.alloc_sbuf_tensor
    xn = sb("xn", [128, FC, NTOK], BF16)
    wXt = sb("wX", [128, 16, 512], BF16)
    wYt = sb("wY", [128, 16, 512], BF16)
    cosb = sb("cosb", [128, NTOK], F32)
    sinb = sb("sinb", [128, NTOK], F32)
    rtb = sb("rtb", [128, 128], BF16)
    idnb = sb("idnb", [128, 128], BF16)
    onesb = sb("onesb", [1, 128], BF16)
    onesf = sb("onesf", [1, 128], F32)
    onescol = sb("onescol", [128, 1], BF16)
    sc_a = sb("sc_a", [128, 1024], F32)
    sc_b = sb("sc_b", [128, 512], F32)
    sc_c = sb("sc_c", [128, 2, 512], BF16)
    gstg = sb("gstg", [128, 4, 512], BF16)
    sc_d = sb("sc_d", [128, 2, 512], BF16)
    pp = sb("pp", [128, 4, 512], BF16)
    pm = sb("pm", [128, 2, 512], BF16)
    rz = sb("rz", [1, 512], F32)
    rzb = sb("rzb", [1, 512], BF16)
    rzt = sb("rzt", [128, 512], BF16)
    bqs = sb("bqs", [128, 16], F32)
    expsh = sb("expsh", [128, 1], F32)
    gamb = sb("gamb", [128, 512], F32)
    betb = sb("betb", [128, 512], F32)
    st1 = sb("st1", [128, 2, 4], F32)
    st2 = sb("st2", [128, 2, 4], F32)
    stm = sb("stm", [128, 2, 4], F32)
    str_ = sb("str", [128, 2, 4], F32)
    fmy = sb("fmy", [128, 2, 512], F32)
    fmo = sb("fmo", [128, 2, 512], BF16)
    xout = sb("xout", [128, 512], F32)
    bsc = sb("bsc", [128, 48 * 3 + 16 * 3 + 64 * 2], F32)

    pALL = nc.alloc_psum_tensor("pALL", [128, 4096], F32)
    _pap = pALL.ap()
    seg = lambda i: _pap[:, i * 512:(i + 1) * 512]
    pA, pB, pC, pD = seg(0), seg(1), seg(2), seg(3)
    pE, pF, pZ = seg(4), seg(5), seg(6)
    pT7 = _pap.bitcast(BF16)[:, 7168:8192]
    pair45 = _pap[:, 2048:3072]
    pair67 = _pap[:, 3072:4096]
    PK = {0: (pA, "pA"), 1: (pB, "pB")}

    dC = b.sem("dC")
    dCg = b.sem("dCg")
    dX = [b.sem("dX0"), b.sem("dX1")]
    dWX, dWY = b.sem("dWX"), b.sem("dWY")
    dWZ, dWW = b.sem("dWZ"), b.sem("dWW")
    dKV = [b.sem("dKV0"), b.sem("dKV1")]
    dKVW = b.sem("dKVW")
    dVW = b.sem("dVW")
    dOUT = b.sem("dOUT")
    dGW = [b.sem("dGW0"), b.sem("dGW1")]
    tT = b.sem("tT")
    vV = b.sem("vV")
    sS = b.sem("sS")
    cc = b.sem("cc")
    gG = b.sem("gG")

    def VCH():
        W("vector", vV)
    WSEM = {id(wXt): dWX, id(wYt): dWY}

    def dma_gate(*engs):
        for eng in engs:
            for s in (tT, vV, sS):
                W(eng, s)

    def dma(eng, out, in_, sem):
        O(eng, lambda e, o=out, i=in_: e.dma_start(out=o, in_=i), sem, 16)
        return sem.n

    dma("gpsimd", rtb[:, :], RT, dCg)
    dma("gpsimd", idnb[:, :], IDN, dCg)
    dma("sync", cosb[:, :], COS, dC)
    dma("sync", sinb[:, :], SIN, dC)
    dma("sync", gamb[:, :], GAMB, dC)
    dma("sync", betb[:, :], BETB, dC)
    boff = {}
    _o = 0
    for nm, ap_, w_ in (("bqkv2", bqkv2, 48), ("g0s", g0s, 48), ("u0s", u0s, 48),
                        ("d0s", d0s, 16), ("bo2", bo2, 16), ("d1s", d1s, 16),
                        ("g1s", g1s, 64), ("u1s", u1s, 64)):
        boff[nm] = _o
        dma("sync", bsc[:, _o:_o + w_], ap_, dC)
        _o += w_

    def bcol(nm, j):
        return bsc[:, boff[nm] + j:boff[nm] + j + 1]

    W("vector", dC)
    W("vector", dCg)
    O("vector", lambda e: e.memset(onesb[:, :], 1.0))
    O("vector", lambda e: e.memset(onesf[:, :], 1.0))
    O("vector", lambda e: e.memset(expsh[:, :], EXP_SHIFT))
    O("vector", lambda e: e.memset(onescol[:, :], 1.0))
    O("vector", lambda e: e.tensor_scalar_mul(
        bqs[:, :], bsc[:, boff["bqkv2"]:boff["bqkv2"] + 16], RS), vV, 1)
    b.barrier()

    def wload(WSRC, grp, wbuf, view=None):
        sem = WSEM[id(wbuf)]
        b.fwait("gpsimd", ("w", id(wbuf)))
        dst = view if view is not None else wbuf[:, :, :]
        return dma("gpsimd", dst, WSRC[grp].rearrange("c p t -> p c t"), sem), sem

    def phase_ln(L):
        XSRC = X if L == 0 else X1
        XR = XSRC.rearrange("(c p) t -> p c t", p=128)
        if L == 1:
            W("sync", dOUT)
        from contextlib import ExitStack
        _uid[0] += 1
        u = _uid[0]
        with ExitStack() as es:
            xsg = es.enter_context(nc.sbuf_tensor(f"xs{u}", [128, 2, 4, 1024], F32))
            xqg = es.enter_context(nc.sbuf_tensor(f"xq{u}", [128, 2, 1024], BF16))
            xqs = es.enter_context(nc.sbuf_tensor(f"xsq{u}", [128, 2, 1024], BF16))
            mrow = es.enter_context(nc.sbuf_tensor(f"mrow{u}", [1, 1024], F32))
            rrow = es.enter_context(nc.sbuf_tensor(f"rrow{u}", [1, 1024], F32))
            trow = es.enter_context(nc.sbuf_tensor(f"trow{u}", [1, 1024], F32))
            t2row = es.enter_context(nc.sbuf_tensor(f"t2row{u}", [1, 1024], F32))
            xs = xsg
            dma_gate("sync")
            accs = [(pC, "pC"), (pD, "pD"), (pE, "pE"), (pF, "pF")]
            for a, k in accs:
                b.bwait("tensor", k)
            for grp in range(4):
                p = grp % 2
                b.fwait("sync", ("xs", p))
                b.fwait("sync", ("xs", p, "v"))
                snap = dma("sync", xs[:, p, :, :], XR[:, grp * 4:grp * 4 + 4, :], dX[p])
                for j in range(4):
                    c = grp * 4 + j
                    src = xs[:, p, j, :]
                    W("vector", dX[p], snap)
                    b.fwait("vector", ("xq", c % 2))
                    O("vector", lambda e, o=xqg[:, c % 2, :], i=src: e.tensor_copy(o, i), vV, 1)
                    W("scalar", dX[p], snap)
                    b.fwait("scalar", ("xsq", c % 2))
                    O("scalar", lambda e, o=xqs[:, c % 2, :], i=src:
                      e.activation(o, i, AF.Square), sS, 1)
                    W("tensor", vV)
                    W("tensor", sS)
                    for half in range(2):
                        hs = slice(half * 512, half * 512 + 512)
                        O("tensor", lambda e, o=accs[half][0][0:1, :], l=onescol[:, :],
                          r=xqg[:, c % 2, hs], s=(c == 0), p_=(c == 15):
                          e.matmul(o, l, r, start=s, stop=p_), tT, 1)
                        O("tensor", lambda e, o=accs[2 + half][0][0:1, :], l=onescol[:, :],
                          r=xqs[:, c % 2, hs], s=(c == 0), p_=(c == 15):
                          e.matmul(o, l, r, start=s, stop=p_), tT, 1)
                    b.fread(("xq", c % 2), tT)
                    b.fread(("xsq", c % 2), tT)
                b.fread(("xs", p), sS)
                b.fread(("xs", p, "v"), vV)
            W("scalar", tT)
            for half in range(2):
                hs = slice(half * 512, half * 512 + 512)
                O("scalar", lambda e, o=trow[0:1, hs], i=accs[half][0][0:1, :]:
                  e.copy(o, i), sS, 1)
                O("scalar", lambda e, o=t2row[0:1, hs], i=accs[2 + half][0][0:1, :]:
                  e.copy(o, i), sS, 1)
            for a, k in accs:
                b.bread(k, sS)
            W("vector", sS)
            O("vector", lambda e: e.tensor_scalar_mul(mrow[:, :], trow[:, :], 1.0 / 2048.0), vV, 1)
            O("vector", lambda e: e.tensor_scalar(t2row[:, :], t2row[:, :], 1.0 / 2048.0, EPS_LN,
                                                  ALU.mult, ALU.add), vV, 1)
            VCH()
            O("vector", lambda e: e.scalar_tensor_tensor(trow[:, :], mrow[:, :], 1.0, mrow[:, :],
                                                         ALU.mult, ALU.mult), vV, 1)
            VCH()
            O("vector", lambda e: e.tensor_tensor(t2row[:, :], t2row[:, :], trow[:, :],
                                                  ALU.subtract), vV, 1)
            VCH()
            O("vector", lambda e: e.reciprocal(t2row[:, :], t2row[:, :]), vV, 1)
            W("scalar", vV)
            O("scalar", lambda e: e.activation(rrow[:, :], t2row[:, :], AF.Sqrt), sS, 1)
            for grp in range(4):
                p = grp % 2
                b.fwait("sync", ("xs", p))
                b.fwait("sync", ("xs", p, "v"))
                snap = dma("sync", xs[:, p, :, :], XR[:, grp * 4:grp * 4 + 4, :], dX[p])
                for half in range(2):
                    hs = slice(half * 512, half * 512 + 512)
                    pk, bank = PK[half]
                    pk2, bank2 = [(pE, "pE"), (pF, "pF")][half]
                    if grp == 0:
                        W("tensor", sS)
                        b.bwait("tensor", bank)
                        O("tensor", lambda e, o=pk[:, :], r=mrow[0:1, hs]:
                          e.matmul(o, onesf[0:1, :], r, start=True, stop=True), tT, 1)
                        b.bwait("tensor", bank2)
                        O("tensor", lambda e, o=pk2[:, :], r=rrow[0:1, hs]:
                          e.matmul(o, onesf[0:1, :], r, start=True, stop=True), tT, 1)
                    for j in range(4):
                        c = grp * 4 + j
                        W("vector", dX[p], snap)
                        W("vector", tT)
                        b.fwait("vector", ("lntmp",))
                        O("vector", lambda e, o=sc_a[:, 0:512], i=xs[:, p, j, hs], m=pk[:, :]:
                          e.tensor_tensor(o, i, m, ALU.subtract), vV, 1)
                        VCH()
                        O("vector", lambda e, o=xn[:, c, hs], i=sc_a[:, 0:512], m=pk2[:, :]:
                          e.tensor_tensor(o, i, m, ALU.mult), vV, 1)
                        b.fread(("lntmp",), vV)
                b.fread(("xs", p), vV)
                b.fread(("xs", p, "v"), vV)
            b.bread("pA", vV)
            b.bread("pB", vV)
            b.bread("pE", vV)
            b.bread("pF", vV)
            b.barrier()

    def proj_rope(grps, kind, qdst=None, slot=None):
        pending = [None]

        def emit_fin():
            if pending[0] is not None:
                pending[0]()
                pending[0] = None

        for gi, grp in enumerate(grps):
            wbuf = (wXt, wYt)[gi % 2]
            snap, wsem = wload(WqkvT, grp, wbuf)
            for hh in range(4):
                h = gi * 4 + hh
                halves = range(2) if slot is None else [slot]
                for half in halves:
                    hs = slice(half * 512, half * 512 + 512)
                    pk, bank = PK[half % 2]
                    b.bwait("tensor", bank)
                    W("tensor", wsem, snap)
                    for kc in range(FC):
                        O("tensor", lambda e, o=pk[:, :], l=wbuf[:, kc, hh * 128:hh * 128 + 128],
                          r=xn[:, kc, hs], s=(kc == 0), p_=(kc == 15):
                          e.matmul(o, l, r, start=s, stop=p_), tT, 1)
                    b.fread(("w", id(wbuf)), tT)
                    emit_fin()
                    W("scalar", tT)
                    b.fwait("scalar", ("praw", half))
                    if kind == "q":
                        O("scalar", lambda e, o=sc_a[:, hs], i=pk[:, :], bb=bqs[:, h:h + 1]:
                          e.activation(o, i, AF.Identity, bias=bb, scale=RS), sS, 1)
                    else:
                        O("scalar", lambda e, o=sc_a[:, hs], i=pk[:, :],
                          bb=bcol("bqkv2", 16 + h): e.activation(o, i, AF.Identity, bias=bb), sS, 1)
                    b.bread(bank, sS)
                    W("vector", sS)
                    tile = (h * 2 + half) % 2
                    b.fwait("vector", ("pp", tile))
                    b.fwait("vector", ("pp2", tile))
                    O("vector", lambda e, o=pp[:, tile, :], i=sc_a[:, hs], m=sinb[:, hs]:
                      e.tensor_tensor(o, i, m, ALU.mult), vV, 1)
                    O("vector", lambda e, o=pm[:, tile, :], i=sc_a[:, hs], m=cosb[:, hs]:
                      e.tensor_tensor(o, i, m, ALU.mult), vV, 1)
                    b.fread(("praw", half), vV)
                    vsnap = vV.n

                    def fin(h=h, half=half, hs=hs, vsnap=vsnap, tile=tile):
                        b.bwait("tensor", "pF")
                        W("tensor", vV, vsnap)
                        O("tensor", lambda e, o=pF[:, :], l=rtb[:, :], r=pp[:, tile, :]:
                          e.matmul(o, l, r, start=True, stop=False), tT, 1)
                        O("tensor", lambda e, o=pF[:, :], l=idnb[:, :], r=pm[:, tile, :]:
                          e.matmul(o, l, r, start=False, stop=True), tT, 1)
                        b.fread(("pp", tile), tT)
                        b.fread(("pp2", tile), tT)
                        W("scalar", tT)
                        if kind == "q":
                            O("scalar", lambda e, o=qdst[:, h, 0:512], i=pF[:, :]: e.copy(o, i), sS, 1)
                            b.bread("pF", sS)
                        else:
                            b.fwait("scalar", ("kst",))
                            O("scalar", lambda e, o=rzt[:, :], i=pF[:, :]: e.copy(o, i), sS, 1)
                            b.bread("pF", sS)
                            W("sync", sS)
                            dma("sync", kvo_k(half, h), rzt[:, :], dKVW)
                            b.fread(("kst",), dKVW)
                    pending[0] = fin
        emit_fin()

    def proj_tm(WSRC, grps, consume, slot=None, depth=2):
        tcs = range(8) if slot is None else range(slot * 4, slot * 4 + 4)
        pending = []
        ti = [0]
        for gi, grp in enumerate(grps):
            wbuf = (wXt, wYt)[gi % 2]
            snap, wsem = wload(WSRC, grp, wbuf)
            for tc in tcs:
                pk, bank = PK[tc % 2]
                b.bwait("tensor", bank)
                W("tensor", wsem, snap)
                for kc in range(FC):
                    O("tensor", lambda e, o=pk[:, :], l=xn[:, kc, tc * 128:tc * 128 + 128],
                      r=wbuf[:, kc, :], s=(kc == 0), p_=(kc == 15):
                      e.matmul(o, l, r, start=s, stop=p_), tT, 1)
                b.fread(("w", id(wbuf)), tT)
                if len(pending) >= depth:
                    pending.pop(0)()
                fin = consume(gi, tc, pk, bank, ti[0] % 2)
                ti[0] += 1
                if fin is not None:
                    pending.append(fin)
        for f in pending:
            f()

    def v_consume(gi, tc, pk, bank, par=0):
        half, tcq = tc // 4, tc % 4
        W("scalar", tT)
        b.fwait("scalar", ("vst",))
        O("scalar", lambda e, o=sc_d[:, 1, :], i=pk[:, :]: e.copy(o, i), sS, 1)
        b.bread(bank, sS)
        W("sync", sS)
        for hh in range(4):
            h = gi * 4 + hh
            dma("sync", kvo_v(half, h, tcq), sc_d[:, 1, hh * 128:hh * 128 + 128], dVW)
        b.fread(("vst",), dVW)
        return None

    def fm_consume_mk(is_q, qdst=None, pT=None):
        def consume(gi, tc, pk, bank, par):
            half, tcq = tc // 4, tc % 4
            fy = fmy[:, par, :]
            fo_ = fmo[:, par, :]
            s1, s2 = st1[:, par, :], st2[:, par, :]
            sm, sr = stm[:, par, :], str_[:, par, :]
            W("vector", tT)
            b.fwait("vector", ("fmy", par))
            O("vector", lambda e, o=fy, i=pk[:, :], m=gamb[:, :]:
              e.tensor_tensor(o, i, m, ALU.mult), vV, 1)
            VCH()
            O("vector", lambda e, o=fy, i=fy, m=betb[:, :]:
              e.tensor_tensor(o, i, m, ALU.add), vV, 1)
            b.bread(bank, vV)
            VCH()
            O("vector", lambda e, o=s1,
              i=fy.rearrange("p (h d) -> p h d", d=128):
              e.reduce_sum(o, i, axis=AX.X), vV, 1)
            W("scalar", vV)
            b.fwait("scalar", ("fmsq", par))
            O("scalar", lambda e, o=sc_b[:, 0:512] if par == 0 else sc_a[:, 0:512], i=fy:
              e.activation(o, i, AF.Square), sS, 1)
            W("vector", sS)
            O("vector", lambda e, o=s2,
              i=(sc_b.ap()[:, 0:512] if par == 0 else sc_a.ap()[:, 0:512]).rearrange(
                  "p (h d) -> p h d", d=128):
              e.reduce_sum(o, i, axis=AX.X), vV, 1)
            b.fread(("fmsq", par), vV)
            VCH()
            O("vector", lambda e, o=sm, i=s1: e.tensor_scalar_mul(o, i, 1.0 / 128.0), vV, 1)
            O("vector", lambda e, o=sr, i=s2: e.tensor_scalar(o, i, 1.0 / 128.0, EPS_L1,
                                                              ALU.mult, ALU.add), vV, 1)
            VCH()
            O("vector", lambda e, o=s1, i=sm, m=sm: e.scalar_tensor_tensor(o, i, 1.0, m,
                                                                           ALU.mult, ALU.mult), vV, 1)
            VCH()
            O("vector", lambda e, o=sr, i=sr, m=s1: e.tensor_tensor(o, i, m, ALU.subtract), vV, 1)
            VCH()
            O("vector", lambda e, o=sr, i=sr: e.reciprocal(o, i), vV, 1)
            W("scalar", vV)
            O("scalar", lambda e, o=sr, i=sr: e.activation(o, i, AF.Sqrt), sS, 1)
            if is_q:
                W("vector", sS)
                VCH()
                O("vector", lambda e, o=sr, i=sr: e.tensor_scalar_mul(o, i, RS), vV, 1)
            for hh in range(4):
                W("vector", sS)
                VCH()
                b.fwait("vector", ("fmo", par, hh))
                O("vector", lambda e, o=fo_[:, hh * 128:hh * 128 + 128],
                  i=fy[:, hh * 128:hh * 128 + 128], m1=sm[:, hh:hh + 1],
                  m2=sr[:, hh:hh + 1]:
                  e.tensor_scalar(o, i, m1, m2, ALU.subtract, ALU.mult), vV, 1)
            b.fread(("fmy", par), vV)
            vsnap = vV.n

            def fin(gi=gi, tc=tc, half=half, tcq=tcq, vsnap=vsnap, par=par, fo_=fo_):
                for hh in range(4):
                    h = gi * 4 + hh
                    b.bwait("tensor", "pT")
                    W("tensor", vV, vsnap)
                    O("tensor", lambda e, o=pT[:, 0:128], i=fo_[:, hh * 128:hh * 128 + 128]:
                      e.transpose(o, i, idnb[:, :]), tT, 1)
                    b.fread(("fmo", par, hh), tT)
                    W("scalar", tT)
                    if is_q:
                        tl = tc % 4
                        O("scalar", lambda e, o=qdst[:, h, tl * 128:tl * 128 + 128],
                          i=pT[:, 0:128]: e.copy(o, i), sS, 1)
                        b.bread("pT", sS)
                    else:
                        b.fwait("scalar", ("kst",))
                        O("scalar", lambda e, o=rzt[:, 0:128], i=pT[:, 0:128]: e.copy(o, i), sS, 1)
                        b.bread("pT", sS)
                        W("sync", sS)
                        dma("sync",
                            kvo_k(half, h).rearrange("p (a t) -> p a t", a=4)[:, tcq, :],
                            rzt[:, 0:128], dKVW)
                        b.fread(("kst",), dKVW)
            return fin
        return consume

    def phase_ffn(L):
        if L == 0:
            NMG, NKD, WG, WU, WD = 12, 48, G0w, U0w, D0w
            gsn, usn, dsn = "g0s", "u0s", "d0s"
        else:
            NMG, NKD, WG, WU, WD = 16, 64, G1w, U1w, D1w
            gsn, usn, dsn = "g1s", "u1s", "d1s"
        nseg = NKD // 16
        GDr = GD.rearrange("(c p) t -> p c t", p=128)
        from contextlib import ExitStack
        _uid[0] += 1
        with ExitStack() as es:
            tokw = 512
            fuse = es.enter_context(nc.sbuf_tensor(f"fuse{_uid[0]}", [128, NKD, tokw], BF16))
            wZt = es.enter_context(nc.sbuf_tensor(f"wZ{_uid[0]}", [128, 16, 512], BF16))
            wWt = es.enter_context(nc.sbuf_tensor(f"wW{_uid[0]}", [128, 16, 512], BF16))
            WSEM[id(wZt)] = dWZ
            WSEM[id(wWt)] = dWW
            tiles = [wXt, wYt, wZt, wWt]
            if L == 0:
                views = [t.ap().bitcast(FP8)[:, :, 0:512] for t in tiles]
            else:
                views = [t[:, :, :] for t in tiles]

            def mk_loader(srcs):
                snaps = [None] * len(srcs)
                emitted = [0]

                nb = len(tiles)
                def ensure(i):
                    while emitted[0] <= min(i + (nb - 2), len(srcs) - 1):
                        j = emitted[0]
                        t = tiles[j % nb]
                        b.fwait("gpsimd", ("w", id(t)))
                        snaps[j] = dma("gpsimd", views[j % nb],
                                       srcs[j].rearrange("c p t -> p c t"), WSEM[id(t)])
                        emitted[0] += 1
                return snaps, ensure

            halves = [0, 1]
            for hpass in halves:
                srcs = []
                for mg in range(NMG):
                    srcs.append(WG[mg])
                    srcs.append(WU[mg])
                for fg in range(4):
                    WDr = WD[fg].rearrange("(s c) p t -> s c p t", c=16)
                    for seg in range(nseg):
                        srcs.append(WDr[seg])
                snaps, ensure = mk_loader(srcs)
                tok_halves = [hpass]
                li = 0
                for mg in range(NMG):
                    ensure(li + 1)
                    for mc in range(4):
                        m = mg * 4 + mc
                        for hi, half in enumerate(tok_halves):
                            hs = slice(half * 512, half * 512 + 512)
                            fs = slice(hi * 512, hi * 512 + 512)
                            slot = (m % 2) * 2 + hi
                            for (scn, func, banks, idx) in (
                                    (gsn, AF.Silu, ((pA, "pA"), (pB, "pB")), li),
                                    (usn, AF.Copy, ((pC, "pC"), (pD, "pD")), li + 1)):
                                wv = views[idx % len(tiles)]
                                wt = tiles[idx % len(tiles)]
                                pk, bank = banks[hi % len(banks)]
                                b.bwait("tensor", bank)
                                W("tensor", WSEM[id(wt)], snaps[idx])
                                for kc in range(FC):
                                    O("tensor", lambda e, o=pk[:, :],
                                      l=wv[:, kc, mc * 128:mc * 128 + 128],
                                      r=xn[:, kc, hs], s=(kc == 0), p_=(kc == 15):
                                      e.matmul(o, l, r, start=s, stop=p_), tT, 1)
                                b.fread(("w", id(wt)), tT)
                                W("scalar", tT)
                                if func is AF.Silu:
                                    b.fwait("scalar", ("gstg", slot))
                                    O("scalar", lambda e, o=gstg[:, slot, :], i=pk[:, :],
                                      sc=bcol(scn, m): e.activation(o, i, AF.Silu, scale=sc),
                                      sS, 1)
                                    b.bread(bank, sS)
                                else:
                                    b.fwait("scalar", ("gu", hi))
                                    O("scalar", lambda e, o=sc_d[:, hi, :], i=pk[:, :],
                                      sc=bcol(scn, m): e.activation(o, i, AF.Copy, scale=sc),
                                      sS, 1)
                                    b.bread(bank, sS)
                                    W("vector", sS)
                                    b.fwait("vector", ("fuse", m, hi))
                                    O("vector", lambda e, o=fuse[:, m, fs], a=gstg[:, slot, :],
                                      bb=sc_d[:, hi, :]: e.tensor_tensor(o, a, bb, ALU.mult),
                                      vV, 1)
                                    b.fread(("gstg", slot), vV)
                                    b.fread(("gu", hi), vV)
                    li += 2
                dbanks = [(pA, "pA"), (pB, "pB"), (pC, "pC"), (pD, "pD")]
                nunit = len(tok_halves)
                for fg in range(4):
                    for _, kk in dbanks:
                        b.bwait("tensor", kk)
                    for seg in range(nseg):
                        idx = li + seg
                        ensure(idx)
                        wv = views[idx % len(tiles)]
                        wt = tiles[idx % len(tiles)]
                        W("tensor", WSEM[id(wt)], snaps[idx])
                        if seg == 0:
                            W("tensor", vV)
                        for kc in range(16):
                            k = seg * 16 + kc
                            for fo4 in range(4):
                                for hi in range(nunit):
                                    fs = slice(hi * 512, hi * 512 + 512)
                                    dbk = dbanks[fo4 * nunit + hi][0]
                                    O("tensor", lambda e, o=dbk[:, :],
                                      l=wv[:, kc, fo4 * 128:fo4 * 128 + 128],
                                      r=fuse[:, k, fs],
                                      s=(k == 0), p_=(k == NKD - 1):
                                      e.matmul(o, l, r, start=s, stop=p_), tT, 1)
                        b.fread(("w", id(wt)), tT)
                    li += nseg
                    W("scalar", tT)
                    for fo4 in range(4):
                        fo = fg * 4 + fo4
                        for hi, half in enumerate(tok_halves):
                            hs = slice(half * 512, half * 512 + 512)
                            ui = (fo4 * nunit + hi) % 2
                            b.fwait("scalar", ("pp", ui))
                            O("scalar", lambda e, o=pp[:, ui, :],
                              i=dbanks[fo4 * nunit + hi][0][:, :],
                              sc=bcol(dsn, fo): e.activation(o, i, AF.Silu, scale=sc), sS, 1)
                            W("sync", sS)
                            dma("sync", GDr[:, fo, hs], pp[:, ui, :], dGW[ui])
                            b.fread(("pp", ui), dGW[ui])
                    for _, kk in dbanks:
                        b.bread(kk, sS)
                for m in range(NKD):
                    for hi in range(nunit):
                        b.fread(("fuse", m, hi), tT)
            b.barrier()

    def phase_attn(L, qbuf, obuf, slot, mskb, kbufs, vbufs):
        nch = 8 if slot == 0 else 16
        qs = slice(slot * 512, slot * 512 + 512)
        quarters = [0, 1] if slot == 0 else [0, 1, 2, 3]

        def issue_kv(h):
            p = h % 2
            b.fwait("sync", ("kv", p))
            for ji, gq in enumerate(quarters):
                dma("sync", kbufs[p][:, ji, :], kvg_k(gq, h), dKV[p])
                dma("sync", vbufs[p][:, ji * 4:ji * 4 + 4, :], kvg_v(gq, h), dKV[p])
            return dKV[p].n

        pend_norm = [None]

        def emit_norm():
            if pend_norm[0] is not None:
                pend_norm[0]()
                pend_norm[0] = None

        def mk_norm(h, ob, obk, zb, zbk, tchunks):
            def norm():
                W("vector", tT, tchunks)
                b.fwait("vector", ("rz",))
                if L == 0:
                    O("vector", lambda e: e.reciprocal(rz[:, :], zb[0:1, :]), vV, 1)
                else:
                    O("vector", lambda e: e.tensor_scalar_add(rz[:, :], zb[0:1, :], EPS_L1), vV, 1)
                    VCH()
                    O("vector", lambda e: e.reciprocal(rz[:, :], rz[:, :]), vV, 1)
                VCH()
                O("vector", lambda e: e.tensor_copy(rzb[:, :], rz[:, :]), vV, 1)
                b.bread(zbk, vV)
                b.bwait("tensor", zbk)
                W("tensor", vV)
                O("tensor", lambda e, o=zb[:, :]: e.matmul(o, onesb[:, :], rzb[:, :],
                                                           start=True, stop=True), tT, 1)
                W("scalar", tT)
                b.fwait("scalar", ("kst",))
                O("scalar", lambda e, o=rzt[:, :], i=zb[:, :]: e.copy(o, i), sS, 1)
                b.bread(zbk, sS)
                W("vector", sS)
                if L == 0:
                    b.fwait("vector", ("onrm",))
                    O("vector", lambda e, o=sc_c[:, 0, :], a=ob[:, :], bb=rzt[:, :]:
                      e.tensor_tensor(o, a, bb, ALU.mult), vV, 1)
                    VCH()
                    O("vector", lambda e, o=obuf[:, h, qs], i=sc_c[:, 0, :],
                      bb=bcol("bqkv2", 32 + h): e.tensor_scalar_add(o, i, bb), vV, 1)
                    b.fread(("onrm",), vV)
                else:
                    O("vector", lambda e, o=obuf[:, h, qs], a=ob[:, :], bb=rzt[:, :]:
                      e.tensor_tensor(o, a, bb, ALU.mult), vV, 1)
                b.bread(obk, vV)
                b.fread(("rz",), vV)
                b.fread(("kst",), vV)
            return norm

        if L == 0:
            npair = nch // 2
            pairs = [(pair45, "pr45"), (pair67, "pr67")]
            rot_i = [0]
            kvsnap = issue_kv(0)
            for h in range(H):
                p = h % 2
                mysnap = kvsnap
                if h + 1 < H:
                    kvsnap = issue_kv(h + 1)
                ob, obk = (pA, "pA") if h % 2 == 0 else (pB, "pB")
                zb, zbk = (pC, "pC") if h % 2 == 0 else (pD, "pD")
                W("tensor", dKV[p], mysnap)
                stage = []

                def pop_pair():
                    pi, pv, pk_, tsnap = stage.pop(0)
                    s_ = pi % 2
                    W("scalar", tT, tsnap)
                    b.fwait("scalar", ("ppp", s_))
                    O("scalar", lambda e, o=pp[:, 2 * s_:2 * s_ + 2, :],
                      i=pv.rearrange("p (a t) -> p a t", a=2):
                      e.activation(o, i, AF.Exp, bias=expsh[:, 0:1]), sS, 1)
                    b.bread(pk_, sS)
                    ssnap = sS.n
                    last_eng = sS
                    for ci in range(2):
                        c2 = 2 * pi + ci
                        masked = True if slot == 0 else (c2 >= 8)
                        mski = slot * 8 + (c2 if slot == 0 else c2 - 8)
                        if masked:
                            W("gpsimd", sS, ssnap)
                            b.fwait("gpsimd", ("pmm", c2 % 2))
                            O("gpsimd", lambda e, o=pm[:, c2 % 2, :], i=pp[:, 2 * s_ + ci, :],
                              m=mskb[:, mski, :]: e.tensor_tensor(o, i, m, ALU.mult), gG, 1)
                            src_ = pm[:, c2 % 2, :]
                            W("tensor", gG)
                        else:
                            src_ = pp[:, 2 * s_ + ci, :]
                            W("tensor", sS, ssnap)
                        if c2 == 0:
                            b.bwait("tensor", obk)
                            b.bwait("tensor", zbk)
                        O("tensor", lambda e, o=ob[:, :], l=vbufs[p][:, c2, :], r=src_,
                          s=(c2 == 0), p_=(c2 == nch - 1):
                          e.matmul(o, l, r, start=s, stop=p_), tT, 1)
                        O("tensor", lambda e, o=zb[0:1, :], l=onescol[:, :], r=src_,
                          s=(c2 == 0), p_=(c2 == nch - 1):
                          e.matmul(o, l, r, start=s, stop=p_), tT, 1)
                        if masked:
                            b.fread(("pmm", c2 % 2), tT)
                    b.fread(("ppp", s_), tT)

                for pi in range(npair + 2):
                    if stage and (len(stage) == 2 or pi >= npair):
                        pop_pair()
                    if pi == 1:
                        emit_norm()
                    if pi < npair:
                        pv, pk_ = pairs[rot_i[0] % 2]
                        rot_i[0] += 1
                        b.bwait("tensor", pk_)
                        for ci in range(2):
                            c = 2 * pi + ci
                            O("tensor", lambda e, o=pv[:, ci * 512:ci * 512 + 512],
                              l=kbufs[p][:, c // 4, (c % 4) * 128:(c % 4) * 128 + 128],
                              r=qbuf[:, h, 0:512]: e.matmul(o, l, r, start=True, stop=True),
                              tT, 1)
                        stage.append((pi, pv, pk_, tT.n))
                b.fread(("kv", p), tT)
                pend_norm[0] = mk_norm(h, ob, obk, zb, zbk, tT.n)
            emit_norm()
            return

        rot = [(pB, "pB"), (pC, "pC"), (pD, "pD")]
        rot_i = [0]

        def next_rot():
            r = rot[rot_i[0] % 3]
            rot_i[0] += 1
            return r

        kvsnap = issue_kv(0)
        for h in range(H):
            p = h % 2
            mysnap = kvsnap
            if h + 1 < H:
                kvsnap = issue_kv(h + 1)
            ob, obk = (pE, "pE") if h % 2 == 0 else (pA, "pA")
            zb, zbk = (pZ, "pZ") if h % 2 == 0 else (pF, "pF")
            W("tensor", dKV[p], mysnap)
            stage2 = []

            def pop_one():
                c2, pS2, bankS2, tsnap = stage2.pop(0)
                ppj = c2 % 3
                masked = True if slot == 0 else (c2 >= 8)
                mski = slot * 8 + (c2 if slot == 0 else c2 - 8)
                W("scalar", tT, tsnap)
                b.fwait("scalar", ("pp", ppj))
                O("scalar", lambda e, o=pp[:, ppj, :], i=pS2[:, :]:
                  e.activation(o, i, AF.Square), sS, 1)
                b.bread(bankS2, sS)
                ssnap = sS.n
                if masked:
                    W("gpsimd", sS, ssnap)
                    b.fwait("gpsimd", ("pmm", c2 % 2))
                    O("gpsimd", lambda e, o=pm[:, c2 % 2, :], i=pp[:, ppj, :],
                      m=mskb[:, mski, :]: e.tensor_tensor(o, i, m, ALU.mult), gG, 1)
                    b.fread(("pp", ppj), gG)
                    src_ = pm[:, c2 % 2, :]
                    W("tensor", gG)
                else:
                    src_ = pp[:, ppj, :]
                    W("tensor", sS, ssnap)
                if c2 == 0:
                    b.bwait("tensor", obk)
                    b.bwait("tensor", zbk)
                O("tensor", lambda e, o=ob[:, :], l=vbufs[p][:, c2, :], r=src_,
                  s=(c2 == 0), p_=(c2 == nch - 1):
                  e.matmul(o, l, r, start=s, stop=p_), tT, 1)
                O("tensor", lambda e, o=zb[0:1, :], l=onescol[:, :], r=src_,
                  s=(c2 == 0), p_=(c2 == nch - 1):
                  e.matmul(o, l, r, start=s, stop=p_), tT, 1)
                if masked:
                    b.fread(("pmm", c2 % 2), tT)
                else:
                    b.fread(("pp", ppj), tT)

            for c in range(nch + 2):
                if stage2 and (len(stage2) == 2 or c >= nch):
                    pop_one()
                if c == 2:
                    emit_norm()
                if c < nch:
                    pS, bankS = next_rot()
                    b.bwait("tensor", bankS)
                    O("tensor", lambda e, o=pS[:, :],
                      l=kbufs[p][:, c // 4, (c % 4) * 128:(c % 4) * 128 + 128],
                      r=qbuf[:, h, 0:512]: e.matmul(o, l, r, start=True, stop=True), tT, 1)
                    stage2.append((c, pS, bankS, tT.n))
            b.fread(("kv", p), tT)
            pend_norm[0] = mk_norm(h, ob, obk, zb, zbk, tT.n)
        emit_norm()

    def phase_wo(L, obuf):
        XSRC = X if L == 0 else X1
        XR = XSRC.rearrange("(c p) t -> p c t", p=128)
        GDr = GD.rearrange("(c p) t -> p c t", p=128)
        XD = (X1 if L == 0 else OUT).rearrange("(c p) t -> p c t", p=128)
        WoT = Wo0T if L == 0 else Wo1T
        from contextlib import ExitStack
        _uid[0] += 1
        with ExitStack() as es:
            xf = es.enter_context(nc.sbuf_tensor(f"xf{_uid[0]}", [128, 2, 1024], F32))
            gf = es.enter_context(nc.sbuf_tensor(f"gf{_uid[0]}", [128, 2, 1024], BF16))
            dma_gate("sync")
            W("sync", dGW[0])
            W("sync", dGW[1])
            for grp in range(4):
                wbuf = (wXt, wYt)[grp % 2]
                snap, wsem = wload(WoT, grp, wbuf)
                for hh in range(4):
                    fo = grp * 4 + hh
                    p = fo % 2
                    b.fwait("sync", ("xf", p))
                    xsnap = dma("sync", xf[:, p, :], XR[:, fo, :], dX[p])
                    dma("sync", gf[:, p, :], GDr[:, fo, :], dX[p])
                    xsnap = dX[p].n
                    for half in range(2):
                        hs = slice(half * 512, half * 512 + 512)
                        pk, bank = PK[half]
                        b.bwait("tensor", bank)
                        W("tensor", wsem, snap)
                        for kc in range(FC):
                            O("tensor", lambda e, o=pk[:, :],
                              l=wbuf[:, kc, hh * 128:hh * 128 + 128],
                              r=obuf[:, kc, hs], s=(kc == 0), p_=(kc == 15):
                              e.matmul(o, l, r, start=s, stop=p_), tT, 1)
                        b.fread(("w", id(wbuf)), tT)
                        W("scalar", tT)
                        b.fwait("scalar", ("wos", half))
                        if L == 0:
                            O("scalar", lambda e, o=sc_c[:, half, :], i=pk[:, :],
                              bb=bcol("bo2", fo): e.activation(o, i, AF.Identity, bias=bb), sS, 1)
                        else:
                            O("scalar", lambda e, o=sc_c[:, half, :], i=pk[:, :]:
                              e.copy(o, i), sS, 1)
                        b.bread(bank, sS)
                        W("vector", sS)
                        W("vector", dX[p], xsnap)
                        b.fwait("vector", ("cmb",))
                        O("vector", lambda e, o=sc_b[:, :], i=sc_c[:, half, :],
                          m=gf[:, p, hs]: e.tensor_tensor(o, i, m, ALU.mult), vV, 1)
                        b.fread(("wos", half), vV)
                        b.fwait("vector", ("xout",))
                        VCH()
                        O("vector", lambda e, o=xout[:, :], i=sc_b[:, :], m=xf[:, p, hs]:
                          e.tensor_tensor(o, i, m, ALU.add), vV, 1)
                        b.fread(("cmb",), vV)
                        W("sync", vV)
                        dma("sync", XD[:, fo, hs], xout[:, :], dOUT)
                        b.fread(("xout",), dOUT)
                    b.fread(("xf", p), vV)
            b.barrier()

    from contextlib import ExitStack
    for L in range(2):
        esL = ExitStack()
        pT = pT7
        phase_ln(L)
        if L == 0:
            proj_rope([4, 5, 6, 7], "k")
            proj_tm(WqkvT, [8, 9, 10, 11], v_consume)
        else:
            proj_tm(Wk1T, [0, 1, 2, 3], fm_consume_mk(False, pT=pT))
            proj_tm(Wv1T, [0, 1, 2, 3], v_consume)
        W("gpsimd", dKVW)
        W("gpsimd", dVW)
        O("gpsimd", lambda e: e.collective_compute(
            "AllGather", ALU.bypass,
            replica_groups=[[0, 1], [2, 3], [4, 5], [6, 7]],
            ins=[KVO.ap().opt()], outs=[KVG.ap().opt()]), cc, 1)
        phase_ffn(L)
        with ExitStack() as es:
            obuf = es.enter_context(nc.sbuf_tensor(f"obuf{L}", [128, FC, NTOK], BF16))
            with ExitStack() as es2:
                qbuf = es2.enter_context(nc.sbuf_tensor(f"qbuf{L}", [128, FC, 512], BF16))
                mskb = es2.enter_context(nc.sbuf_tensor(f"mskb{L}", [128, 16, 512], BF16))
                kbufs = [es2.enter_context(nc.sbuf_tensor(f"kb{L}{p}", [128, 4, 512], BF16))
                         for p in range(2)]
                vbufs = [es2.enter_context(nc.sbuf_tensor(f"vb{L}{p}", [128, 16, 128], BF16))
                         for p in range(2)]
                dma_gate("gpsimd", "sync")
                dma("gpsimd", mskb[:, :, :], MSK.rearrange("c p t -> p c t"), dCg)
                W("sync", cc)
                W("vector", dCg)
                for slot in range(2):
                    if L == 0:
                        proj_rope([0, 1, 2, 3], "q", qbuf, slot)
                    else:
                        proj_tm(Wq1T, [0, 1, 2, 3], fm_consume_mk(True, qbuf, pT=pT), slot)
                    b.barrier()
                    phase_attn(L, qbuf, obuf, slot, mskb, kbufs, vbufs)
                    b.barrier()
            phase_wo(L, obuf)
        esL.close()

    W("gpsimd", dOUT)
    W("gpsimd", dGW[0])
    W("gpsimd", dGW[1])
    W("gpsimd", dKVW)
    W("gpsimd", dVW)
    W("gpsimd", dC)
    W("gpsimd", dCg)
    for s in dX + dKV + [dWX, dWY, dWZ, dWW]:
        W("gpsimd", s)

    with nc.Block() as block:
        @block.tensor
        def _(e):
            for f in b.q["tensor"]:
                f(e)

        @block.vector
        def _(e):
            for f in b.q["vector"]:
                f(e)

        @block.scalar
        def _(e):
            for f in b.q["scalar"]:
                f(e)

        @block.gpsimd
        def _(e):
            for f in b.q["gpsimd"]:
                f(e)

        @block.sync
        def _(e):
            for f in b.q["sync"]:
                f(e)

    b.stack.close()
    return nc


def _blocked(w2d):
    K, M = w2d.shape
    return np.ascontiguousarray(
        w2d.reshape(K // 128, 128, M // 512, 512).transpose(2, 0, 1, 3))


def _chunk_cols(v):
    return np.ascontiguousarray(v.reshape(-1, 128).T)


def _rope_tables(pos):
    p = np.arange(64, dtype=np.float32)
    theta = np.power(np.float32(10000.0),
                     (-(np.float32(2.0) ** p) / np.float32(128.0)).astype(np.float32))
    ang = pos.astype(np.float32)[None, :] * theta[:, None]
    ang = np.concatenate([ang, ang], axis=0)
    return np.cos(ang).astype(np.float32), np.sin(ang).astype(np.float32)


def _rot_matrix():
    R = np.zeros((128, 128), np.float32)
    for i in range(64):
        R[i, i + 64] = -1.0
        R[i + 64, i] = 1.0
    return np.ascontiguousarray(R.T)


def prepare_in_maps(ii):
    x = np.asarray(ii["L0_attn_x"], np.float32)
    shared = {
        "WqkvT": _blocked(np.ascontiguousarray(np.asarray(ii["L0_attn_Wqkv"], np.float32).T)),
        "bqkv2": _chunk_cols(np.asarray(ii["L0_attn_bqkv"], np.float32)),
        "Wo0T": _blocked(np.ascontiguousarray(np.asarray(ii["L0_attn_Wo"], np.float32).T)),
        "bo2": _chunk_cols(np.asarray(ii["L0_attn_bo"], np.float32)),
        "G0w": _blocked(np.asarray(ii["L0_ffn_gate_w"], np.float32)),
        "U0w": _blocked(np.asarray(ii["L0_ffn_up_w"], np.float32)),
        "D0w": _blocked(np.asarray(ii["L0_ffn_down_w"], np.float32)),
        "g0s": _chunk_cols(np.asarray(ii["L0_ffn_gate_scale"], np.float32)),
        "u0s": _chunk_cols(np.asarray(ii["L0_ffn_up_scale"], np.float32)),
        "d0s": _chunk_cols(np.asarray(ii["L0_ffn_down_scale"], np.float32)),
        "Wq1T": _blocked(np.ascontiguousarray(np.asarray(ii["L1_attn_Wq"], np.float32).T)),
        "Wk1T": _blocked(np.ascontiguousarray(np.asarray(ii["L1_attn_Wk"], np.float32).T)),
        "Wv1T": _blocked(np.ascontiguousarray(np.asarray(ii["L1_attn_Wv"], np.float32).T)),
        "Wo1T": _blocked(np.ascontiguousarray(np.asarray(ii["L1_attn_Wo"], np.float32).T)),
        "GAMB": np.ascontiguousarray(
            np.tile(np.asarray(ii["L1_attn_gamma"], np.float32)[None, :], (128, 4))),
        "BETB": np.ascontiguousarray(
            np.tile(np.asarray(ii["L1_attn_beta"], np.float32)[None, :], (128, 4))),
        "G1w": _blocked(np.asarray(ii["L1_ffn_w_gate"])),
        "U1w": _blocked(np.asarray(ii["L1_ffn_w_up"])),
        "D1w": _blocked(np.asarray(ii["L1_ffn_w_down"])),
        "g1s": _chunk_cols(np.asarray(ii["L1_ffn_s_gate"], np.float32)),
        "u1s": _chunk_cols(np.asarray(ii["L1_ffn_s_up"], np.float32)),
        "d1s": _chunk_cols(np.asarray(ii["L1_ffn_s_down"], np.float32)),
        "RT": _rot_matrix(),
        "IDN": np.eye(128, dtype=np.float32),
    }

    in_maps = []
    for c in range(NCORES):
        s, par = c // 2, c % 2
        qE, qL = (0, 3) if par == 0 else (1, 2)
        xc = np.concatenate([x[s, qE * 512:qE * 512 + 512, :],
                             x[s, qL * 512:qL * 512 + 512, :]], axis=0)
        pos = np.concatenate([np.arange(qE * 512, qE * 512 + 512),
                              np.arange(qL * 512, qL * 512 + 512)]) + 1
        cosc, sinc = _rope_tables(pos)
        mskc = np.zeros((16, 128, 512), np.float32)
        for slot in range(2):
            tqb = (qE if slot == 0 else qL) * 512
            for j in range(8):
                cglob = j if slot == 0 else 8 + j
                tk = cglob * 128 + np.arange(128)[:, None]
                tq = tqb + np.arange(512)[None, :]
                mskc[slot * 8 + j] = (tk <= tq).astype(np.float32)
        m = dict(shared)
        m["X"] = np.ascontiguousarray(xc.T)
        m["COS"] = cosc
        m["SIN"] = sinc
        m["MSK"] = mskc
        in_maps.append(m)
    return in_maps


def assemble_out(core_outs):
    out = np.empty((B, T, C), np.float32)
    for c in range(NCORES):
        s, par = c // 2, c % 2
        qE, qL = (0, 3) if par == 0 else (1, 2)
        oc = np.asarray(core_outs[c], np.float32).T
        out[s, qE * 512:qE * 512 + 512, :] = oc[:512]
        out[s, qL * 512:qL * 512 + 512, :] = oc[512:]
    return out


def kernel(**ii):
    in_maps = prepare_in_maps(ii)
    nc = build_program()
    res = run_bass_kernel_spmd(nc, in_maps, core_ids=list(range(NCORES)))
    return assemble_out([res.results[c]["OUT"] for c in range(NCORES)])


# version 08 (final)
# speedup vs baseline: 1.1578x; 1.0237x over previous
import numpy as np
import concourse.bass as bass
import concourse.mybir as mybir
from concourse.bass_utils import run_bass_kernel_spmd

F32 = mybir.dt.float32
BF16 = mybir.dt.bfloat16
FP8 = mybir.dt.float8e4
I8 = mybir.dt.int8
AF = mybir.ActivationFunctionType
ALU = mybir.AluOpType
AX = mybir.AxisListType

B, T, C, H = 4, 2048, 2048, 16
FC = C // 128
NTOK = 1024
RS = float(1.0 / np.sqrt(128.0))
EXP_SHIFT = -12.0
EPS_LN = 1e-5
EPS_L1 = 1e-5
NCORES = 8

GQ = [(0, 0), (1, 0), (1, 1), (0, 1)]
KV_KQ = 16 * 128 * 512
KV_VQ = 16 * 4 * 128 * 128
KV_Q = KV_KQ + KV_VQ
KV_BLK = 2 * KV_Q


class Sem:
    def __init__(self, nc, stack, name):
        self.h = stack.enter_context(nc.semaphore(name))
        self.n = 0


class Builder:
    def __init__(self):
        from contextlib import ExitStack
        self.nc = bass.Bass("TRN2", target_bir_lowering=False, debug=False)
        self.stack = ExitStack()
        self.q = {e: [] for e in ("tensor", "vector", "scalar", "gpsimd", "sync")}
        self.sems = []
        self.bank_free = {}
        self.buf_free = {}

    def sem(self, name):
        s = Sem(self.nc, self.stack, name)
        self.sems.append(s)
        return s

    def op(self, eng, f, inc=None, k=1):
        ss = getattr(self, "self_serial", {}).get(eng)
        if ss is not None and ss.n > 0:
            self.wait(eng, ss, ss.n)
        if inc is not None:
            inc.n += k
            self.q[eng].append(lambda e, f=f, s=inc.h, k=k: f(e).then_inc(s, k))
        else:
            self.q[eng].append(lambda e, f=f: f(e))

    def wait(self, eng, sem, val=None):
        v = sem.n if val is None else val
        if v <= 0:
            return
        self.q[eng].append(lambda e, s=sem.h, v=v: e.wait_ge(s, v))

    def barrier(self):
        snaps = [(s, s.n) for s in self.sems]
        for e in ("tensor", "vector", "scalar"):
            for s, v in snaps:
                if v > 0:
                    self.wait(e, s, v)

    def bwait(self, eng, key):
        if key in self.bank_free:
            s, v = self.bank_free[key]
            self.wait(eng, s, v)

    def bread(self, key, sem):
        self.bank_free[key] = (sem, sem.n)

    def fwait(self, eng, key):
        if key in self.buf_free:
            s, v = self.buf_free[key]
            self.wait(eng, s, v)

    def fread(self, key, sem):
        self.buf_free[key] = (sem, sem.n)


def build_program():
    _uid = [0]
    b = Builder()
    nc = b.nc
    O, W = b.op, b.wait

    din = lambda n, sh, dt: nc.dram_tensor(n, sh, dt, kind="ExternalInput").ap()
    X = din("X", [C, NTOK], F32)
    WqkvT = din("WqkvT", [12, 16, 128, 512], F32)
    bqkv2 = din("bqkv2", [128, 48], F32)
    Wo0T = din("Wo0T", [4, 16, 128, 512], F32)
    bo2 = din("bo2", [128, 16], F32)
    G0w = din("G0w", [12, 16, 128, 512], F32)
    U0w = din("U0w", [12, 16, 128, 512], F32)
    D0w = din("D0w", [4, 48, 128, 512], F32)
    g0s = din("g0s", [128, 48], F32)
    u0s = din("u0s", [128, 48], F32)
    d0s = din("d0s", [128, 16], F32)
    Wq1T = din("Wq1T", [4, 16, 128, 512], F32)
    Wk1T = din("Wk1T", [4, 16, 128, 512], F32)
    Wv1T = din("Wv1T", [4, 16, 128, 512], F32)
    Wo1T = din("Wo1T", [4, 16, 128, 512], F32)
    GAMB = din("GAMB", [128, 512], F32)
    BETB = din("BETB", [128, 512], F32)
    G1w = din("G1w", [16, 16, 128, 512], I8)
    U1w = din("U1w", [16, 16, 128, 512], I8)
    D1w = din("D1w", [4, 64, 128, 512], I8)
    g1s = din("g1s", [128, 64], F32)
    u1s = din("u1s", [128, 64], F32)
    d1s = din("d1s", [128, 16], F32)
    COS = din("COS", [128, NTOK], F32)
    SIN = din("SIN", [128, NTOK], F32)
    MSK = din("MSK", [16, 128, 512], F32)
    RT = din("RT", [128, 128], F32)
    IDN = din("IDN", [128, 128], F32)
    OUT = nc.dram_tensor("OUT", [C, NTOK], F32, kind="ExternalOutput").ap()
    X1 = nc.dram_tensor("X1", [C, NTOK], F32).ap()
    GD = nc.dram_tensor("GD", [C, NTOK], BF16).ap()
    KVO = nc.dram_tensor("KVO", [KV_BLK], BF16)
    KVG = nc.dram_tensor("KVG", [2 * KV_BLK], BF16)

    def kvo_k(q, h):
        off = q * KV_Q + h * (128 * 512)
        return KVO.ap()[off:off + 128 * 512].rearrange("(p t) -> p t", p=128)

    def kvo_v(q, h, tc):
        off = q * KV_Q + KV_KQ + h * (4 * 128 * 128) + tc * (128 * 128)
        return KVO.ap()[off:off + 128 * 128].rearrange("(p t) -> p t", p=128)

    def kvg_k(gq, h):
        r, qq = GQ[gq]
        off = r * KV_BLK + qq * KV_Q + h * (128 * 512)
        return KVG.ap()[off:off + 128 * 512].rearrange("(p t) -> p t", p=128)

    def kvg_v(gq, h):
        r, qq = GQ[gq]
        off = r * KV_BLK + qq * KV_Q + KV_KQ + h * (4 * 128 * 128)
        return KVG.ap()[off:off + 4 * 128 * 128].rearrange("(tc p t) -> p tc t", p=128, tc=4)

    sb = nc.alloc_sbuf_tensor
    xn = sb("xn", [128, FC, NTOK], BF16)
    wXt = sb("wX", [128, 16, 512], BF16)
    wYt = sb("wY", [128, 16, 512], BF16)
    cosb = sb("cosb", [128, NTOK], F32)
    sinb = sb("sinb", [128, NTOK], F32)
    rtb = sb("rtb", [128, 128], BF16)
    idnb = sb("idnb", [128, 128], BF16)
    onesb = sb("onesb", [1, 128], BF16)
    onesf = sb("onesf", [1, 128], F32)
    onescol = sb("onescol", [128, 1], BF16)
    sc_a = sb("sc_a", [128, 1024], F32)
    sc_c = sb("sc_c", [128, 2, 512], BF16)
    gstg = sb("gstg", [128, 2, 512], BF16)
    sc_d = sb("sc_d", [128, 2, 512], BF16)
    pp = sb("pp", [128, 4, 512], BF16)
    pm = sb("pm", [128, 2, 512], BF16)
    rz = sb("rz", [1, 512], F32)
    rzb = sb("rzb", [1, 512], BF16)
    rzt = sb("rzt", [128, 512], BF16)
    bqs = sb("bqs", [128, 16], F32)
    expsh = sb("expsh", [128, 1], F32)
    gamb = sb("gamb", [128, 512], F32)
    betb = sb("betb", [128, 512], F32)
    st1 = sb("st1", [128, 3, 4], F32)
    st2 = sb("st2", [128, 3, 4], F32)
    stm = sb("stm", [128, 3, 4], F32)
    str_ = sb("str", [128, 3, 4], F32)
    fmy = sb("fmy", [128, 3, 512], F32)
    fmo = sb("fmo", [128, 3, 512], BF16)
    xout = sb("xout", [128, 512], F32)
    bsc = sb("bsc", [128, 48 * 3 + 16 * 3 + 64 * 2], F32)

    pALL = nc.alloc_psum_tensor("pALL", [128, 4096], F32)
    _pap = pALL.ap()
    seg = lambda i: _pap[:, i * 512:(i + 1) * 512]
    pA, pB, pC, pD = seg(0), seg(1), seg(2), seg(3)
    pE, pF, pZ = seg(4), seg(5), seg(6)
    pT7 = _pap.bitcast(BF16)[:, 7168:8192]
    pair45 = _pap[:, 2048:3072]
    pair67 = _pap[:, 3072:4096]
    PK = {0: (pA, "pA"), 1: (pB, "pB")}

    dC = b.sem("dC")
    dCg = b.sem("dCg")
    dX = [b.sem("dX0"), b.sem("dX1")]
    dWX, dWY = b.sem("dWX"), b.sem("dWY")
    dWZ, dWW = b.sem("dWZ"), b.sem("dWW")
    dKV = [b.sem("dKV0"), b.sem("dKV1")]
    dKVW = b.sem("dKVW")
    dVW = b.sem("dVW")
    dOUT = b.sem("dOUT")
    dGW = [b.sem("dGW0"), b.sem("dGW1")]
    tT = b.sem("tT")
    vV = b.sem("vV")
    sS = b.sem("sS")
    cc = b.sem("cc")
    gG = b.sem("gG")

    def VCH():
        W("vector", vV)
    WSEM = {id(wXt): dWX, id(wYt): dWY}

    def dma_gate(*engs):
        for eng in engs:
            for s in (tT, vV, sS):
                W(eng, s)

    def dma(eng, out, in_, sem):
        O(eng, lambda e, o=out, i=in_: e.dma_start(out=o, in_=i), sem, 16)
        return sem.n

    dma("gpsimd", rtb[:, :], RT, dCg)
    dma("gpsimd", idnb[:, :], IDN, dCg)
    dma("sync", cosb[:, :], COS, dC)
    dma("sync", sinb[:, :], SIN, dC)
    dma("sync", gamb[:, :], GAMB, dC)
    dma("sync", betb[:, :], BETB, dC)
    boff = {}
    _o = 0
    for nm, ap_, w_ in (("bqkv2", bqkv2, 48), ("g0s", g0s, 48), ("u0s", u0s, 48),
                        ("d0s", d0s, 16), ("bo2", bo2, 16), ("d1s", d1s, 16),
                        ("g1s", g1s, 64), ("u1s", u1s, 64)):
        boff[nm] = _o
        dma("sync", bsc[:, _o:_o + w_], ap_, dC)
        _o += w_

    def bcol(nm, j):
        return bsc[:, boff[nm] + j:boff[nm] + j + 1]

    W("vector", dC)
    W("vector", dCg)
    O("vector", lambda e: e.memset(onesb[:, :], 1.0))
    O("vector", lambda e: e.memset(onesf[:, :], 1.0))
    O("vector", lambda e: e.memset(expsh[:, :], EXP_SHIFT))
    O("vector", lambda e: e.memset(onescol[:, :], 1.0))
    O("vector", lambda e: e.tensor_scalar_mul(
        bqs[:, :], bsc[:, boff["bqkv2"]:boff["bqkv2"] + 16], RS), vV, 1)
    b.barrier()

    def wload(WSRC, grp, wbuf, view=None):
        sem = WSEM[id(wbuf)]
        b.fwait("gpsimd", ("w", id(wbuf)))
        dst = view if view is not None else wbuf[:, :, :]
        return dma("gpsimd", dst, WSRC[grp].rearrange("c p t -> p c t"), sem), sem

    def phase_ln(L):
        XSRC = X if L == 0 else X1
        XR = XSRC.rearrange("(c p) t -> p c t", p=128)
        if L == 1:
            W("sync", dOUT)
        from contextlib import ExitStack
        _uid[0] += 1
        u = _uid[0]
        with ExitStack() as es:
            xsg = es.enter_context(nc.sbuf_tensor(f"xs{u}", [128, 2, 4, 1024], F32))
            xqg = es.enter_context(nc.sbuf_tensor(f"xq{u}", [128, 2, 1024], BF16))
            xqs = es.enter_context(nc.sbuf_tensor(f"xsq{u}", [128, 2, 1024], BF16))
            mrow = es.enter_context(nc.sbuf_tensor(f"mrow{u}", [1, 1024], F32))
            rrow = es.enter_context(nc.sbuf_tensor(f"rrow{u}", [1, 1024], F32))
            trow = es.enter_context(nc.sbuf_tensor(f"trow{u}", [1, 1024], F32))
            t2row = es.enter_context(nc.sbuf_tensor(f"t2row{u}", [1, 1024], F32))
            xs = xsg
            dma_gate("sync")
            accs = [(pC, "pC"), (pD, "pD"), (pE, "pE"), (pF, "pF")]
            for a, k in accs:
                b.bwait("tensor", k)
            for grp in range(4):
                p = grp % 2
                b.fwait("sync", ("xs", p))
                b.fwait("sync", ("xs", p, "v"))
                snap = dma("sync", xs[:, p, :, :], XR[:, grp * 4:grp * 4 + 4, :], dX[p])
                for j in range(4):
                    c = grp * 4 + j
                    src = xs[:, p, j, :]
                    W("vector", dX[p], snap)
                    b.fwait("vector", ("xq", c % 2))
                    O("vector", lambda e, o=xqg[:, c % 2, :], i=src: e.tensor_copy(o, i), vV, 1)
                    W("scalar", dX[p], snap)
                    b.fwait("scalar", ("xsq", c % 2))
                    O("scalar", lambda e, o=xqs[:, c % 2, :], i=src:
                      e.activation(o, i, AF.Square), sS, 1)
                    W("tensor", vV)
                    W("tensor", sS)
                    for half in range(2):
                        hs = slice(half * 512, half * 512 + 512)
                        O("tensor", lambda e, o=accs[half][0][0:1, :], l=onescol[:, :],
                          r=xqg[:, c % 2, hs], s=(c == 0), p_=(c == 15):
                          e.matmul(o, l, r, start=s, stop=p_), tT, 1)
                        O("tensor", lambda e, o=accs[2 + half][0][0:1, :], l=onescol[:, :],
                          r=xqs[:, c % 2, hs], s=(c == 0), p_=(c == 15):
                          e.matmul(o, l, r, start=s, stop=p_), tT, 1)
                    b.fread(("xq", c % 2), tT)
                    b.fread(("xsq", c % 2), tT)
                b.fread(("xs", p), sS)
                b.fread(("xs", p, "v"), vV)
            W("scalar", tT)
            for half in range(2):
                hs = slice(half * 512, half * 512 + 512)
                O("scalar", lambda e, o=trow[0:1, hs], i=accs[half][0][0:1, :]:
                  e.copy(o, i), sS, 1)
                O("scalar", lambda e, o=t2row[0:1, hs], i=accs[2 + half][0][0:1, :]:
                  e.copy(o, i), sS, 1)
            for a, k in accs:
                b.bread(k, sS)
            W("vector", sS)
            O("vector", lambda e: e.tensor_scalar_mul(mrow[:, :], trow[:, :], 1.0 / 2048.0), vV, 1)
            O("vector", lambda e: e.tensor_scalar(t2row[:, :], t2row[:, :], 1.0 / 2048.0, EPS_LN,
                                                  ALU.mult, ALU.add), vV, 1)
            VCH()
            O("vector", lambda e: e.scalar_tensor_tensor(trow[:, :], mrow[:, :], 1.0, mrow[:, :],
                                                         ALU.mult, ALU.mult), vV, 1)
            VCH()
            O("vector", lambda e: e.tensor_tensor(t2row[:, :], t2row[:, :], trow[:, :],
                                                  ALU.subtract), vV, 1)
            VCH()
            O("vector", lambda e: e.reciprocal(t2row[:, :], t2row[:, :]), vV, 1)
            W("scalar", vV)
            O("scalar", lambda e: e.activation(rrow[:, :], t2row[:, :], AF.Sqrt), sS, 1)
            for grp in range(4):
                p = grp % 2
                b.fwait("sync", ("xs", p))
                b.fwait("sync", ("xs", p, "v"))
                snap = dma("sync", xs[:, p, :, :], XR[:, grp * 4:grp * 4 + 4, :], dX[p])
                for half in range(2):
                    hs = slice(half * 512, half * 512 + 512)
                    pk, bank = PK[half]
                    pk2, bank2 = [(pE, "pE"), (pF, "pF")][half]
                    if grp == 0:
                        W("tensor", sS)
                        b.bwait("tensor", bank)
                        O("tensor", lambda e, o=pk[:, :], r=mrow[0:1, hs]:
                          e.matmul(o, onesf[0:1, :], r, start=True, stop=True), tT, 1)
                        b.bwait("tensor", bank2)
                        O("tensor", lambda e, o=pk2[:, :], r=rrow[0:1, hs]:
                          e.matmul(o, onesf[0:1, :], r, start=True, stop=True), tT, 1)
                    for j in range(4):
                        c = grp * 4 + j
                        W("vector", dX[p], snap)
                        W("vector", tT)
                        b.fwait("vector", ("lntmp",))
                        O("vector", lambda e, o=sc_a[:, 0:512], i=xs[:, p, j, hs], m=pk[:, :]:
                          e.tensor_tensor(o, i, m, ALU.subtract), vV, 1)
                        VCH()
                        O("vector", lambda e, o=xn[:, c, hs], i=sc_a[:, 0:512], m=pk2[:, :]:
                          e.tensor_tensor(o, i, m, ALU.mult), vV, 1)
                        b.fread(("lntmp",), vV)
                b.fread(("xs", p), vV)
                b.fread(("xs", p, "v"), vV)
            b.bread("pA", vV)
            b.bread("pB", vV)
            b.bread("pE", vV)
            b.bread("pF", vV)
            b.barrier()

    def proj_rope(grps, kind, qdst=None, slot=None):
        pending = [None]

        def emit_fin():
            if pending[0] is not None:
                pending[0]()
                pending[0] = None

        for gi, grp in enumerate(grps):
            wbuf = (wXt, wYt)[gi % 2]
            snap, wsem = wload(WqkvT, grp, wbuf)
            for hh in range(4):
                h = gi * 4 + hh
                halves = range(2) if slot is None else [slot]
                for half in halves:
                    hs = slice(half * 512, half * 512 + 512)
                    pk, bank = PK[half % 2]
                    b.bwait("tensor", bank)
                    W("tensor", wsem, snap)
                    for kc in range(FC):
                        O("tensor", lambda e, o=pk[:, :], l=wbuf[:, kc, hh * 128:hh * 128 + 128],
                          r=xn[:, kc, hs], s=(kc == 0), p_=(kc == 15):
                          e.matmul(o, l, r, start=s, stop=p_), tT, 1)
                    b.fread(("w", id(wbuf)), tT)
                    emit_fin()
                    W("scalar", tT)
                    b.fwait("scalar", ("praw", half))
                    if kind == "q":
                        O("scalar", lambda e, o=sc_a[:, hs], i=pk[:, :], bb=bqs[:, h:h + 1]:
                          e.activation(o, i, AF.Identity, bias=bb, scale=RS), sS, 1)
                    else:
                        O("scalar", lambda e, o=sc_a[:, hs], i=pk[:, :],
                          bb=bcol("bqkv2", 16 + h): e.activation(o, i, AF.Identity, bias=bb), sS, 1)
                    b.bread(bank, sS)
                    W("vector", sS)
                    tile = (h * 2 + half) % 2
                    b.fwait("vector", ("pp", tile))
                    b.fwait("vector", ("pp2", tile))
                    O("vector", lambda e, o=pp[:, tile, :], i=sc_a[:, hs], m=sinb[:, hs]:
                      e.tensor_tensor(o, i, m, ALU.mult), vV, 1)
                    O("vector", lambda e, o=pm[:, tile, :], i=sc_a[:, hs], m=cosb[:, hs]:
                      e.tensor_tensor(o, i, m, ALU.mult), vV, 1)
                    b.fread(("praw", half), vV)
                    vsnap = vV.n

                    def fin(h=h, half=half, hs=hs, vsnap=vsnap, tile=tile):
                        b.bwait("tensor", "pF")
                        W("tensor", vV, vsnap)
                        O("tensor", lambda e, o=pF[:, :], l=rtb[:, :], r=pp[:, tile, :]:
                          e.matmul(o, l, r, start=True, stop=False), tT, 1)
                        O("tensor", lambda e, o=pF[:, :], l=idnb[:, :], r=pm[:, tile, :]:
                          e.matmul(o, l, r, start=False, stop=True), tT, 1)
                        b.fread(("pp", tile), tT)
                        b.fread(("pp2", tile), tT)
                        W("scalar", tT)
                        if kind == "q":
                            O("scalar", lambda e, o=qdst[:, h, 0:512], i=pF[:, :]: e.copy(o, i), sS, 1)
                            b.bread("pF", sS)
                        else:
                            b.fwait("scalar", ("kst",))
                            O("scalar", lambda e, o=rzt[:, :], i=pF[:, :]: e.copy(o, i), sS, 1)
                            b.bread("pF", sS)
                            W("sync", sS)
                            dma("sync", kvo_k(half, h), rzt[:, :], dKVW)
                            b.fread(("kst",), dKVW)
                    pending[0] = fin
        emit_fin()

    def proj_tm(WSRC, grps, consume, slot=None, depth=3):
        tcs = range(8) if slot is None else range(slot * 4, slot * 4 + 4)
        pending = []
        ti = [0]
        for gi, grp in enumerate(grps):
            wbuf = (wXt, wYt)[gi % 2]
            snap, wsem = wload(WSRC, grp, wbuf)
            for tc in tcs:
                pk, bank = PK[tc % 2]
                b.bwait("tensor", bank)
                W("tensor", wsem, snap)
                for kc in range(FC):
                    O("tensor", lambda e, o=pk[:, :], l=xn[:, kc, tc * 128:tc * 128 + 128],
                      r=wbuf[:, kc, :], s=(kc == 0), p_=(kc == 15):
                      e.matmul(o, l, r, start=s, stop=p_), tT, 1)
                b.fread(("w", id(wbuf)), tT)
                if len(pending) >= depth:
                    pending.pop(0)()
                fin = consume(gi, tc, pk, bank, ti[0] % 3)
                ti[0] += 1
                if fin is not None:
                    pending.append(fin)
        for f in pending:
            f()

    def v_consume(gi, tc, pk, bank, par=0):
        half, tcq = tc // 4, tc % 4
        W("scalar", tT)
        b.fwait("scalar", ("vst",))
        O("scalar", lambda e, o=sc_d[:, 1, :], i=pk[:, :]: e.copy(o, i), sS, 1)
        b.bread(bank, sS)
        W("sync", sS)
        for hh in range(4):
            h = gi * 4 + hh
            dma("sync", kvo_v(half, h, tcq), sc_d[:, 1, hh * 128:hh * 128 + 128], dVW)
        b.fread(("vst",), dVW)
        return None

    def fm_consume_mk(is_q, qdst=None, pT=None):
        def consume(gi, tc, pk, bank, par):
            half, tcq = tc // 4, tc % 4
            fy = fmy[:, par, :]
            fo_ = fmo[:, par, :]
            s1, s2 = st1[:, par, :], st2[:, par, :]
            sm, sr = stm[:, par, :], str_[:, par, :]
            W("vector", tT)
            b.fwait("vector", ("fmy", par))
            O("vector", lambda e, o=fy, i=pk[:, :], m=gamb[:, :]:
              e.tensor_tensor(o, i, m, ALU.mult), vV, 1)
            VCH()
            O("vector", lambda e, o=fy, i=fy, m=betb[:, :]:
              e.tensor_tensor(o, i, m, ALU.add), vV, 1)
            b.bread(bank, vV)
            VCH()
            O("vector", lambda e, o=s1,
              i=fy.rearrange("p (h d) -> p h d", d=128):
              e.reduce_sum(o, i, axis=AX.X), vV, 1)
            W("scalar", vV)
            b.fwait("scalar", ("fmsq", par))
            sqsl = [xout.ap()[:, 0:512], sc_a.ap()[:, 0:512], sc_a.ap()[:, 512:1024]][par]
            O("scalar", lambda e, o=sqsl, i=fy:
              e.activation(o, i, AF.Square), sS, 1)
            W("vector", sS)
            O("vector", lambda e, o=s2,
              i=sqsl.rearrange("p (h d) -> p h d", d=128):
              e.reduce_sum(o, i, axis=AX.X), vV, 1)
            b.fread(("fmsq", par), vV)
            VCH()
            O("vector", lambda e, o=sm, i=s1: e.tensor_scalar_mul(o, i, 1.0 / 128.0), vV, 1)
            O("vector", lambda e, o=sr, i=s2: e.tensor_scalar(o, i, 1.0 / 128.0, EPS_L1,
                                                              ALU.mult, ALU.add), vV, 1)
            VCH()
            O("vector", lambda e, o=s1, i=sm, m=sm: e.scalar_tensor_tensor(o, i, 1.0, m,
                                                                           ALU.mult, ALU.mult), vV, 1)
            VCH()
            O("vector", lambda e, o=sr, i=sr, m=s1: e.tensor_tensor(o, i, m, ALU.subtract), vV, 1)
            VCH()
            O("vector", lambda e, o=sr, i=sr: e.reciprocal(o, i), vV, 1)
            W("scalar", vV)
            O("scalar", lambda e, o=sr, i=sr: e.activation(o, i, AF.Sqrt), sS, 1)
            if is_q:
                W("vector", sS)
                VCH()
                O("vector", lambda e, o=sr, i=sr: e.tensor_scalar_mul(o, i, RS), vV, 1)
            for hh in range(4):
                W("vector", sS)
                VCH()
                b.fwait("vector", ("fmo", par, hh))
                O("vector", lambda e, o=fo_[:, hh * 128:hh * 128 + 128],
                  i=fy[:, hh * 128:hh * 128 + 128], m1=sm[:, hh:hh + 1],
                  m2=sr[:, hh:hh + 1]:
                  e.tensor_scalar(o, i, m1, m2, ALU.subtract, ALU.mult), vV, 1)
            b.fread(("fmy", par), vV)
            vsnap = vV.n

            def fin(gi=gi, tc=tc, half=half, tcq=tcq, vsnap=vsnap, par=par, fo_=fo_):
                for hh in range(4):
                    h = gi * 4 + hh
                    b.bwait("tensor", "pT")
                    W("tensor", vV, vsnap)
                    O("tensor", lambda e, o=pT[:, 0:128], i=fo_[:, hh * 128:hh * 128 + 128]:
                      e.transpose(o, i, idnb[:, :]), tT, 1)
                    b.fread(("fmo", par, hh), tT)
                    W("scalar", tT)
                    if is_q:
                        tl = tc % 4
                        O("scalar", lambda e, o=qdst[:, h, tl * 128:tl * 128 + 128],
                          i=pT[:, 0:128]: e.copy(o, i), sS, 1)
                        b.bread("pT", sS)
                    else:
                        b.fwait("scalar", ("kst",))
                        O("scalar", lambda e, o=rzt[:, 0:128], i=pT[:, 0:128]: e.copy(o, i), sS, 1)
                        b.bread("pT", sS)
                        W("sync", sS)
                        dma("sync",
                            kvo_k(half, h).rearrange("p (a t) -> p a t", a=4)[:, tcq, :],
                            rzt[:, 0:128], dKVW)
                        b.fread(("kst",), dKVW)
            return fin
        return consume

    def phase_ffn(L, kick=None):
        if L == 0:
            NMG, NKD, WG, WU, WD = 12, 48, G0w, U0w, D0w
            gsn, usn, dsn = "g0s", "u0s", "d0s"
        else:
            NMG, NKD, WG, WU, WD = 16, 64, G1w, U1w, D1w
            gsn, usn, dsn = "g1s", "u1s", "d1s"
        nseg = NKD // 16
        GDr = GD.rearrange("(c p) t -> p c t", p=128)
        from contextlib import ExitStack
        _uid[0] += 1
        with ExitStack() as es:
            tokw = 512
            fuse = es.enter_context(nc.sbuf_tensor(f"fuse{_uid[0]}", [128, NKD, tokw], BF16))
            wZt = es.enter_context(nc.sbuf_tensor(f"wZ{_uid[0]}", [128, 16, 512], BF16))
            wWt = es.enter_context(nc.sbuf_tensor(f"wW{_uid[0]}", [128, 16, 512], BF16))
            WSEM[id(wZt)] = dWZ
            WSEM[id(wWt)] = dWW
            tiles = [wXt, wYt, wZt, wWt]
            if L == 0:
                views = [t.ap().bitcast(FP8)[:, :, 0:512] for t in tiles]
            else:
                views = [t[:, :, :] for t in tiles]

            def mk_loader(srcs):
                snaps = [None] * len(srcs)
                emitted = [0]

                nb = len(tiles)
                def ensure(i):
                    while emitted[0] <= min(i + (nb - 2), len(srcs) - 1):
                        j = emitted[0]
                        t = tiles[j % nb]
                        b.fwait("gpsimd", ("w", id(t)))
                        snaps[j] = dma("gpsimd", views[j % nb],
                                       srcs[j].rearrange("c p t -> p c t"), WSEM[id(t)])
                        emitted[0] += 1
                return snaps, ensure

            halves = [0, 1]
            for hpass in halves:
                srcs = []
                for mg in range(NMG):
                    srcs.append(WG[mg])
                    srcs.append(WU[mg])
                for fg in range(4):
                    WDr = WD[fg].rearrange("(s c) p t -> s c p t", c=16)
                    for seg in range(nseg):
                        srcs.append(WDr[seg])
                snaps, ensure = mk_loader(srcs)
                if kick is not None and hpass == 0:
                    ensure(1)
                    kick()
                    kick = None
                tok_halves = [hpass]
                li = 0
                for mg in range(NMG):
                    ensure(li + 1)
                    for mc in range(4):
                        m = mg * 4 + mc
                        for hi, half in enumerate(tok_halves):
                            hs = slice(half * 512, half * 512 + 512)
                            fs = slice(hi * 512, hi * 512 + 512)
                            slot = m % 2
                            for (scn, func, banks, idx) in (
                                    (gsn, AF.Silu, ((pA, "pA"), (pB, "pB")), li),
                                    (usn, AF.Copy, ((pC, "pC"), (pD, "pD")), li + 1)):
                                wv = views[idx % len(tiles)]
                                wt = tiles[idx % len(tiles)]
                                pk, bank = banks[hi % len(banks)]
                                b.bwait("tensor", bank)
                                W("tensor", WSEM[id(wt)], snaps[idx])
                                for kc in range(FC):
                                    O("tensor", lambda e, o=pk[:, :],
                                      l=wv[:, kc, mc * 128:mc * 128 + 128],
                                      r=xn[:, kc, hs], s=(kc == 0), p_=(kc == 15):
                                      e.matmul(o, l, r, start=s, stop=p_), tT, 1)
                                b.fread(("w", id(wt)), tT)
                                W("scalar", tT)
                                if func is AF.Silu:
                                    b.fwait("scalar", ("gstg", slot))
                                    O("scalar", lambda e, o=gstg[:, slot, :], i=pk[:, :],
                                      sc=bcol(scn, m): e.activation(o, i, AF.Silu, scale=sc),
                                      sS, 1)
                                    b.bread(bank, sS)
                                else:
                                    b.fwait("scalar", ("gu", hi))
                                    O("scalar", lambda e, o=sc_d[:, hi, :], i=pk[:, :],
                                      sc=bcol(scn, m): e.activation(o, i, AF.Copy, scale=sc),
                                      sS, 1)
                                    b.bread(bank, sS)
                                    W("vector", sS)
                                    b.fwait("vector", ("fuse", m, hi))
                                    O("vector", lambda e, o=fuse[:, m, fs], a=gstg[:, slot, :],
                                      bb=sc_d[:, hi, :]: e.tensor_tensor(o, a, bb, ALU.mult),
                                      vV, 1)
                                    b.fread(("gstg", slot), vV)
                                    b.fread(("gu", hi), vV)
                    li += 2
                dbanks = [(pA, "pA"), (pB, "pB"), (pC, "pC"), (pD, "pD")]
                nunit = len(tok_halves)
                for fg in range(4):
                    for _, kk in dbanks:
                        b.bwait("tensor", kk)
                    for seg in range(nseg):
                        idx = li + seg
                        ensure(idx)
                        wv = views[idx % len(tiles)]
                        wt = tiles[idx % len(tiles)]
                        W("tensor", WSEM[id(wt)], snaps[idx])
                        if seg == 0:
                            W("tensor", vV)
                        for kc in range(16):
                            k = seg * 16 + kc
                            for fo4 in range(4):
                                for hi in range(nunit):
                                    fs = slice(hi * 512, hi * 512 + 512)
                                    dbk = dbanks[fo4 * nunit + hi][0]
                                    O("tensor", lambda e, o=dbk[:, :],
                                      l=wv[:, kc, fo4 * 128:fo4 * 128 + 128],
                                      r=fuse[:, k, fs],
                                      s=(k == 0), p_=(k == NKD - 1):
                                      e.matmul(o, l, r, start=s, stop=p_), tT, 1)
                        b.fread(("w", id(wt)), tT)
                    li += nseg
                    W("scalar", tT)
                    for fo4 in range(4):
                        fo = fg * 4 + fo4
                        for hi, half in enumerate(tok_halves):
                            hs = slice(half * 512, half * 512 + 512)
                            ui = (fo4 * nunit + hi) % 2
                            b.fwait("scalar", ("pp", ui))
                            O("scalar", lambda e, o=pp[:, ui, :],
                              i=dbanks[fo4 * nunit + hi][0][:, :],
                              sc=bcol(dsn, fo): e.activation(o, i, AF.Silu, scale=sc), sS, 1)
                            W("sync", sS)
                            dma("sync", GDr[:, fo, hs], pp[:, ui, :], dGW[ui])
                            b.fread(("pp", ui), dGW[ui])
                    for _, kk in dbanks:
                        b.bread(kk, sS)
                for m in range(NKD):
                    for hi in range(nunit):
                        b.fread(("fuse", m, hi), tT)
            b.barrier()

    def phase_attn(L, qbuf, obuf, slot, mskb, kbufs, vbufs):
        nch = 8 if slot == 0 else 16
        qs = slice(slot * 512, slot * 512 + 512)
        quarters = [0, 1] if slot == 0 else [0, 1, 2, 3]

        def issue_kv(h):
            p = h % 2
            b.fwait("sync", ("kv", p))
            for ji, gq in enumerate(quarters):
                dma("sync", kbufs[p][:, ji, :], kvg_k(gq, h), dKV[p])
                dma("sync", vbufs[p][:, ji * 4:ji * 4 + 4, :], kvg_v(gq, h), dKV[p])
            return dKV[p].n

        pend_norm = [None]

        def emit_norm():
            if pend_norm[0] is not None:
                pend_norm[0]()
                pend_norm[0] = None

        def mk_norm(h, ob, obk, zb, zbk, tchunks):
            def norm():
                W("vector", tT, tchunks)
                b.fwait("vector", ("rz",))
                if L == 0:
                    O("vector", lambda e: e.reciprocal(rz[:, :], zb[0:1, :]), vV, 1)
                else:
                    O("vector", lambda e: e.tensor_scalar_add(rz[:, :], zb[0:1, :], EPS_L1), vV, 1)
                    VCH()
                    O("vector", lambda e: e.reciprocal(rz[:, :], rz[:, :]), vV, 1)
                VCH()
                O("vector", lambda e: e.tensor_copy(rzb[:, :], rz[:, :]), vV, 1)
                b.bread(zbk, vV)
                b.bwait("tensor", zbk)
                W("tensor", vV)
                O("tensor", lambda e, o=zb[:, :]: e.matmul(o, onesb[:, :], rzb[:, :],
                                                           start=True, stop=True), tT, 1)
                W("scalar", tT)
                b.fwait("scalar", ("kst",))
                O("scalar", lambda e, o=rzt[:, :], i=zb[:, :]: e.copy(o, i), sS, 1)
                b.bread(zbk, sS)
                W("vector", sS)
                if L == 0:
                    b.fwait("vector", ("onrm",))
                    O("vector", lambda e, o=sc_c[:, 0, :], a=ob[:, :], bb=rzt[:, :]:
                      e.tensor_tensor(o, a, bb, ALU.mult), vV, 1)
                    VCH()
                    O("vector", lambda e, o=obuf[:, h, qs], i=sc_c[:, 0, :],
                      bb=bcol("bqkv2", 32 + h): e.tensor_scalar_add(o, i, bb), vV, 1)
                    b.fread(("onrm",), vV)
                else:
                    O("vector", lambda e, o=obuf[:, h, qs], a=ob[:, :], bb=rzt[:, :]:
                      e.tensor_tensor(o, a, bb, ALU.mult), vV, 1)
                b.bread(obk, vV)
                b.fread(("rz",), vV)
                b.fread(("kst",), vV)
            return norm

        if True:
            npair = nch // 2
            pairs = [(pair45, "pr45"), (pair67, "pr67")]
            rot_i = [0]
            kvsnap = issue_kv(0)
            for h in range(H):
                p = h % 2
                mysnap = kvsnap
                if h + 1 < H:
                    kvsnap = issue_kv(h + 1)
                ob, obk = (pA, "pA") if h % 2 == 0 else (pB, "pB")
                zb, zbk = (pC, "pC") if h % 2 == 0 else (pD, "pD")
                W("tensor", dKV[p], mysnap)
                stage = []

                def pop_pair():
                    pi, pv, pk_, tsnap = stage.pop(0)
                    s_ = pi % 2
                    W("scalar", tT, tsnap)
                    b.fwait("scalar", ("ppp", s_))
                    if L == 0:
                        O("scalar", lambda e, o=pp[:, 2 * s_:2 * s_ + 2, :],
                          i=pv.rearrange("p (a t) -> p a t", a=2):
                          e.activation(o, i, AF.Exp, bias=expsh[:, 0:1]), sS, 1)
                    else:
                        O("scalar", lambda e, o=pp[:, 2 * s_:2 * s_ + 2, :],
                          i=pv.rearrange("p (a t) -> p a t", a=2):
                          e.activation(o, i, AF.Square), sS, 1)
                    b.bread(pk_, sS)
                    ssnap = sS.n
                    last_eng = sS
                    for ci in range(2):
                        c2 = 2 * pi + ci
                        masked = True if slot == 0 else (c2 >= 8)
                        mski = slot * 8 + (c2 if slot == 0 else c2 - 8)
                        if masked:
                            W("gpsimd", sS, ssnap)
                            b.fwait("gpsimd", ("pmm", c2 % 2))
                            O("gpsimd", lambda e, o=pm[:, c2 % 2, :], i=pp[:, 2 * s_ + ci, :],
                              m=mskb[:, mski, :]: e.tensor_tensor(o, i, m, ALU.mult), gG, 1)
                            src_ = pm[:, c2 % 2, :]
                            W("tensor", gG)
                        else:
                            src_ = pp[:, 2 * s_ + ci, :]
                            W("tensor", sS, ssnap)
                        if c2 == 0:
                            b.bwait("tensor", obk)
                            b.bwait("tensor", zbk)
                        O("tensor", lambda e, o=ob[:, :], l=vbufs[p][:, c2, :], r=src_,
                          s=(c2 == 0), p_=(c2 == nch - 1):
                          e.matmul(o, l, r, start=s, stop=p_), tT, 1)
                        O("tensor", lambda e, o=zb[0:1, :], l=onescol[:, :], r=src_,
                          s=(c2 == 0), p_=(c2 == nch - 1):
                          e.matmul(o, l, r, start=s, stop=p_), tT, 1)
                        if masked:
                            b.fread(("pmm", c2 % 2), tT)
                    b.fread(("ppp", s_), tT)

                for pi in range(npair + 2):
                    if stage and (len(stage) == 2 or pi >= npair):
                        pop_pair()
                    if pi == 1:
                        emit_norm()
                    if pi < npair:
                        pv, pk_ = pairs[rot_i[0] % 2]
                        rot_i[0] += 1
                        b.bwait("tensor", pk_)
                        for ci in range(2):
                            c = 2 * pi + ci
                            O("tensor", lambda e, o=pv[:, ci * 512:ci * 512 + 512],
                              l=kbufs[p][:, c // 4, (c % 4) * 128:(c % 4) * 128 + 128],
                              r=qbuf[:, h, 0:512]: e.matmul(o, l, r, start=True, stop=True),
                              tT, 1)
                        stage.append((pi, pv, pk_, tT.n))
                b.fread(("kv", p), tT)
                pend_norm[0] = mk_norm(h, ob, obk, zb, zbk, tT.n)
            emit_norm()

    def phase_wo(L, obuf):
        XSRC = X if L == 0 else X1
        XR = XSRC.rearrange("(c p) t -> p c t", p=128)
        GDr = GD.rearrange("(c p) t -> p c t", p=128)
        XD = (X1 if L == 0 else OUT).rearrange("(c p) t -> p c t", p=128)
        WoT = Wo0T if L == 0 else Wo1T
        from contextlib import ExitStack
        _uid[0] += 1
        with ExitStack() as es:
            xf = es.enter_context(nc.sbuf_tensor(f"xf{_uid[0]}", [128, 2, 1024], F32))
            gf = es.enter_context(nc.sbuf_tensor(f"gf{_uid[0]}", [128, 2, 1024], BF16))
            dma_gate("sync")
            W("sync", dGW[0])
            W("sync", dGW[1])
            for grp in range(4):
                wbuf = (wXt, wYt)[grp % 2]
                snap, wsem = wload(WoT, grp, wbuf)
                for hh in range(4):
                    fo = grp * 4 + hh
                    p = fo % 2
                    b.fwait("sync", ("xf", p))
                    xsnap = dma("sync", xf[:, p, :], XR[:, fo, :], dX[p])
                    dma("sync", gf[:, p, :], GDr[:, fo, :], dX[p])
                    xsnap = dX[p].n
                    for half in range(2):
                        hs = slice(half * 512, half * 512 + 512)
                        pk, bank = PK[half]
                        b.bwait("tensor", bank)
                        W("tensor", wsem, snap)
                        for kc in range(FC):
                            O("tensor", lambda e, o=pk[:, :],
                              l=wbuf[:, kc, hh * 128:hh * 128 + 128],
                              r=obuf[:, kc, hs], s=(kc == 0), p_=(kc == 15):
                              e.matmul(o, l, r, start=s, stop=p_), tT, 1)
                        b.fread(("w", id(wbuf)), tT)
                        W("scalar", tT)
                        b.fwait("scalar", ("wos", half))
                        if L == 0:
                            O("scalar", lambda e, o=sc_c[:, half, :], i=pk[:, :],
                              bb=bcol("bo2", fo): e.activation(o, i, AF.Identity, bias=bb), sS, 1)
                        else:
                            O("scalar", lambda e, o=sc_c[:, half, :], i=pk[:, :]:
                              e.copy(o, i), sS, 1)
                        b.bread(bank, sS)
                        W("vector", sS)
                        W("vector", dX[p], xsnap)
                        b.fwait("vector", ("cmb",))
                        O("vector", lambda e, o=sc_a[:, 0:512], i=sc_c[:, half, :],
                          m=gf[:, p, hs]: e.tensor_tensor(o, i, m, ALU.mult), vV, 1)
                        b.fread(("wos", half), vV)
                        b.fwait("vector", ("xout",))
                        VCH()
                        O("vector", lambda e, o=xout[:, :], i=sc_a[:, 0:512], m=xf[:, p, hs]:
                          e.tensor_tensor(o, i, m, ALU.add), vV, 1)
                        b.fread(("cmb",), vV)
                        W("sync", vV)
                        dma("sync", XD[:, fo, hs], xout[:, :], dOUT)
                        b.fread(("xout",), dOUT)
                    b.fread(("xf", p), vV)
            b.barrier()

    from contextlib import ExitStack
    for L in range(2):
        esL = ExitStack()
        pT = pT7
        phase_ln(L)
        if L == 0:
            proj_rope([4, 5, 6, 7], "k")
            proj_tm(WqkvT, [8, 9, 10, 11], v_consume)
        else:
            proj_tm(Wk1T, [0, 1, 2, 3], fm_consume_mk(False, pT=pT))
            proj_tm(Wv1T, [0, 1, 2, 3], v_consume)
        def kick():
            W("gpsimd", dKVW)
            W("gpsimd", dVW)
            O("gpsimd", lambda e: e.collective_compute(
                "AllGather", ALU.bypass,
                replica_groups=[[0, 1], [2, 3], [4, 5], [6, 7]],
                ins=[KVO.ap().opt()], outs=[KVG.ap().opt()]), cc, 1)
        phase_ffn(L, kick)
        with ExitStack() as es:
            obuf = es.enter_context(nc.sbuf_tensor(f"obuf{L}", [128, FC, NTOK], BF16))
            with ExitStack() as es2:
                qbuf = es2.enter_context(nc.sbuf_tensor(f"qbuf{L}", [128, FC, 512], BF16))
                mskb = es2.enter_context(nc.sbuf_tensor(f"mskb{L}", [128, 16, 512], BF16))
                kbufs = [es2.enter_context(nc.sbuf_tensor(f"kb{L}{p}", [128, 4, 512], BF16))
                         for p in range(2)]
                vbufs = [es2.enter_context(nc.sbuf_tensor(f"vb{L}{p}", [128, 16, 128], BF16))
                         for p in range(2)]
                dma_gate("gpsimd", "sync")
                dma("gpsimd", mskb[:, :, :], MSK.rearrange("c p t -> p c t"), dCg)
                W("sync", cc)
                W("vector", dCg)
                for slot in range(2):
                    if L == 0:
                        proj_rope([0, 1, 2, 3], "q", qbuf, slot)
                    else:
                        proj_tm(Wq1T, [0, 1, 2, 3], fm_consume_mk(True, qbuf, pT=pT), slot)
                    b.barrier()
                    phase_attn(L, qbuf, obuf, slot, mskb, kbufs, vbufs)
                    b.barrier()
            phase_wo(L, obuf)
        esL.close()

    W("gpsimd", dOUT)
    W("gpsimd", dGW[0])
    W("gpsimd", dGW[1])
    W("gpsimd", dKVW)
    W("gpsimd", dVW)
    W("gpsimd", dC)
    W("gpsimd", dCg)
    for s in dX + dKV + [dWX, dWY, dWZ, dWW]:
        W("gpsimd", s)

    with nc.Block() as block:
        @block.tensor
        def _(e):
            for f in b.q["tensor"]:
                f(e)

        @block.vector
        def _(e):
            for f in b.q["vector"]:
                f(e)

        @block.scalar
        def _(e):
            for f in b.q["scalar"]:
                f(e)

        @block.gpsimd
        def _(e):
            for f in b.q["gpsimd"]:
                f(e)

        @block.sync
        def _(e):
            for f in b.q["sync"]:
                f(e)

    b.stack.close()
    return nc


def _blocked(w2d):
    K, M = w2d.shape
    return np.ascontiguousarray(
        w2d.reshape(K // 128, 128, M // 512, 512).transpose(2, 0, 1, 3))


def _chunk_cols(v):
    return np.ascontiguousarray(v.reshape(-1, 128).T)


def _rope_tables(pos):
    p = np.arange(64, dtype=np.float32)
    theta = np.power(np.float32(10000.0),
                     (-(np.float32(2.0) ** p) / np.float32(128.0)).astype(np.float32))
    ang = pos.astype(np.float32)[None, :] * theta[:, None]
    ang = np.concatenate([ang, ang], axis=0)
    return np.cos(ang).astype(np.float32), np.sin(ang).astype(np.float32)


def _rot_matrix():
    R = np.zeros((128, 128), np.float32)
    for i in range(64):
        R[i, i + 64] = -1.0
        R[i + 64, i] = 1.0
    return np.ascontiguousarray(R.T)


def prepare_in_maps(ii):
    x = np.asarray(ii["L0_attn_x"], np.float32)
    shared = {
        "WqkvT": _blocked(np.ascontiguousarray(np.asarray(ii["L0_attn_Wqkv"], np.float32).T)),
        "bqkv2": _chunk_cols(np.asarray(ii["L0_attn_bqkv"], np.float32)),
        "Wo0T": _blocked(np.ascontiguousarray(np.asarray(ii["L0_attn_Wo"], np.float32).T)),
        "bo2": _chunk_cols(np.asarray(ii["L0_attn_bo"], np.float32)),
        "G0w": _blocked(np.asarray(ii["L0_ffn_gate_w"], np.float32)),
        "U0w": _blocked(np.asarray(ii["L0_ffn_up_w"], np.float32)),
        "D0w": _blocked(np.asarray(ii["L0_ffn_down_w"], np.float32)),
        "g0s": _chunk_cols(np.asarray(ii["L0_ffn_gate_scale"], np.float32)),
        "u0s": _chunk_cols(np.asarray(ii["L0_ffn_up_scale"], np.float32)),
        "d0s": _chunk_cols(np.asarray(ii["L0_ffn_down_scale"], np.float32)),
        "Wq1T": _blocked(np.ascontiguousarray(np.asarray(ii["L1_attn_Wq"], np.float32).T)),
        "Wk1T": _blocked(np.ascontiguousarray(np.asarray(ii["L1_attn_Wk"], np.float32).T)),
        "Wv1T": _blocked(np.ascontiguousarray(np.asarray(ii["L1_attn_Wv"], np.float32).T)),
        "Wo1T": _blocked(np.ascontiguousarray(np.asarray(ii["L1_attn_Wo"], np.float32).T)),
        "GAMB": np.ascontiguousarray(
            np.tile(np.asarray(ii["L1_attn_gamma"], np.float32)[None, :], (128, 4))),
        "BETB": np.ascontiguousarray(
            np.tile(np.asarray(ii["L1_attn_beta"], np.float32)[None, :], (128, 4))),
        "G1w": _blocked(np.asarray(ii["L1_ffn_w_gate"])),
        "U1w": _blocked(np.asarray(ii["L1_ffn_w_up"])),
        "D1w": _blocked(np.asarray(ii["L1_ffn_w_down"])),
        "g1s": _chunk_cols(np.asarray(ii["L1_ffn_s_gate"], np.float32)),
        "u1s": _chunk_cols(np.asarray(ii["L1_ffn_s_up"], np.float32)),
        "d1s": _chunk_cols(np.asarray(ii["L1_ffn_s_down"], np.float32)),
        "RT": _rot_matrix(),
        "IDN": np.eye(128, dtype=np.float32),
    }

    in_maps = []
    for c in range(NCORES):
        s, par = c // 2, c % 2
        qE, qL = (0, 3) if par == 0 else (1, 2)
        xc = np.concatenate([x[s, qE * 512:qE * 512 + 512, :],
                             x[s, qL * 512:qL * 512 + 512, :]], axis=0)
        pos = np.concatenate([np.arange(qE * 512, qE * 512 + 512),
                              np.arange(qL * 512, qL * 512 + 512)]) + 1
        cosc, sinc = _rope_tables(pos)
        mskc = np.zeros((16, 128, 512), np.float32)
        for slot in range(2):
            tqb = (qE if slot == 0 else qL) * 512
            for j in range(8):
                cglob = j if slot == 0 else 8 + j
                tk = cglob * 128 + np.arange(128)[:, None]
                tq = tqb + np.arange(512)[None, :]
                mskc[slot * 8 + j] = (tk <= tq).astype(np.float32)
        m = dict(shared)
        m["X"] = np.ascontiguousarray(xc.T)
        m["COS"] = cosc
        m["SIN"] = sinc
        m["MSK"] = mskc
        in_maps.append(m)
    return in_maps


def assemble_out(core_outs):
    out = np.empty((B, T, C), np.float32)
    for c in range(NCORES):
        s, par = c // 2, c % 2
        qE, qL = (0, 3) if par == 0 else (1, 2)
        oc = np.asarray(core_outs[c], np.float32).T
        out[s, qE * 512:qE * 512 + 512, :] = oc[:512]
        out[s, qL * 512:qL * 512 + 512, :] = oc[512:]
    return out


def kernel(**ii):
    in_maps = prepare_in_maps(ii)
    nc = build_program()
    res = run_bass_kernel_spmd(nc, in_maps, core_ids=list(range(NCORES)))
    return assemble_out([res.results[c]["OUT"] for c in range(NCORES)])


# version 09
# speedup vs baseline: 1.1690x; 1.0097x over previous
import numpy as np
import concourse.bass as bass
import concourse.mybir as mybir
from concourse.bass_utils import run_bass_kernel_spmd

F32 = mybir.dt.float32
BF16 = mybir.dt.bfloat16
FP8 = mybir.dt.float8e4
I8 = mybir.dt.int8
AF = mybir.ActivationFunctionType
ALU = mybir.AluOpType
AX = mybir.AxisListType

B, T, C, H = 4, 2048, 2048, 16
FC = C // 128
NTOK = 1024
RS = float(1.0 / np.sqrt(128.0))
EXP_SHIFT = -12.0
EPS_LN = 1e-5
EPS_L1 = 1e-5
NCORES = 8

GQ = [(0, 0), (1, 0), (1, 1), (0, 1)]
KV_KQ = 16 * 128 * 512
KV_VQ = 16 * 4 * 128 * 128
KV_Q = KV_KQ + KV_VQ
KV_BLK = 2 * KV_Q


class Sem:
    def __init__(self, nc, stack, name):
        self.h = stack.enter_context(nc.semaphore(name))
        self.n = 0


class Builder:
    def __init__(self):
        from contextlib import ExitStack
        self.nc = bass.Bass("TRN2", target_bir_lowering=False, debug=False)
        self.stack = ExitStack()
        self.q = {e: [] for e in ("tensor", "vector", "scalar", "gpsimd", "sync")}
        self.sems = []
        self.bank_free = {}
        self.buf_free = {}

    def sem(self, name):
        s = Sem(self.nc, self.stack, name)
        self.sems.append(s)
        return s

    def op(self, eng, f, inc=None, k=1):
        ss = getattr(self, "self_serial", {}).get(eng)
        if ss is not None and ss.n > 0:
            self.wait(eng, ss, ss.n)
        if inc is not None:
            inc.n += k
            self.q[eng].append(lambda e, f=f, s=inc.h, k=k: f(e).then_inc(s, k))
        else:
            self.q[eng].append(lambda e, f=f: f(e))

    def wait(self, eng, sem, val=None):
        v = sem.n if val is None else val
        if v <= 0:
            return
        self.q[eng].append(lambda e, s=sem.h, v=v: e.wait_ge(s, v))

    def barrier(self):
        snaps = [(s, s.n) for s in self.sems]
        for e in ("tensor", "vector", "scalar"):
            for s, v in snaps:
                if v > 0:
                    self.wait(e, s, v)

    def bwait(self, eng, key):
        if key in self.bank_free:
            s, v = self.bank_free[key]
            self.wait(eng, s, v)

    def bread(self, key, sem):
        self.bank_free[key] = (sem, sem.n)

    def fwait(self, eng, key):
        if key in self.buf_free:
            s, v = self.buf_free[key]
            self.wait(eng, s, v)

    def fread(self, key, sem):
        self.buf_free[key] = (sem, sem.n)


def build_program():
    _uid = [0]
    b = Builder()
    nc = b.nc
    O, W = b.op, b.wait

    din = lambda n, sh, dt: nc.dram_tensor(n, sh, dt, kind="ExternalInput").ap()
    X = din("X", [C, NTOK], F32)
    WqkvT = din("WqkvT", [12, 16, 128, 512], F32)
    bqkv2 = din("bqkv2", [128, 48], F32)
    Wo0T = din("Wo0T", [4, 16, 128, 512], F32)
    bo2 = din("bo2", [128, 16], F32)
    G0w = din("G0w", [12, 16, 128, 512], F32)
    U0w = din("U0w", [12, 16, 128, 512], F32)
    D0w = din("D0w", [4, 48, 128, 512], F32)
    g0s = din("g0s", [128, 48], F32)
    u0s = din("u0s", [128, 48], F32)
    d0s = din("d0s", [128, 16], F32)
    Wq1T = din("Wq1T", [4, 16, 128, 512], F32)
    Wk1T = din("Wk1T", [4, 16, 128, 512], F32)
    Wv1T = din("Wv1T", [4, 16, 128, 512], F32)
    Wo1T = din("Wo1T", [4, 16, 128, 512], F32)
    GAMB = din("GAMB", [128, 512], F32)
    BETB = din("BETB", [128, 512], F32)
    G1w = din("G1w", [16, 16, 128, 512], I8)
    U1w = din("U1w", [16, 16, 128, 512], I8)
    D1w = din("D1w", [4, 64, 128, 512], I8)
    g1s = din("g1s", [128, 64], F32)
    u1s = din("u1s", [128, 64], F32)
    d1s = din("d1s", [128, 16], F32)
    COS = din("COS", [128, NTOK], F32)
    SIN = din("SIN", [128, NTOK], F32)
    MSK = din("MSK", [16, 128, 512], F32)
    RT = din("RT", [128, 128], F32)
    IDN = din("IDN", [128, 128], F32)
    OUT = nc.dram_tensor("OUT", [C, NTOK], F32, kind="ExternalOutput").ap()
    X1 = nc.dram_tensor("X1", [C, NTOK], F32).ap()
    GD = nc.dram_tensor("GD", [C, NTOK], BF16).ap()
    KVO = nc.dram_tensor("KVO", [KV_BLK], BF16)
    KVG = nc.dram_tensor("KVG", [2 * KV_BLK], BF16)

    def kvo_k(q, h):
        off = q * KV_Q + h * (128 * 512)
        return KVO.ap()[off:off + 128 * 512].rearrange("(p t) -> p t", p=128)

    def kvo_v(q, h, tc):
        off = q * KV_Q + KV_KQ + h * (4 * 128 * 128) + tc * (128 * 128)
        return KVO.ap()[off:off + 128 * 128].rearrange("(p t) -> p t", p=128)

    def kvg_k(gq, h):
        r, qq = GQ[gq]
        off = r * KV_BLK + qq * KV_Q + h * (128 * 512)
        return KVG.ap()[off:off + 128 * 512].rearrange("(p t) -> p t", p=128)

    def kvg_v(gq, h):
        r, qq = GQ[gq]
        off = r * KV_BLK + qq * KV_Q + KV_KQ + h * (4 * 128 * 128)
        return KVG.ap()[off:off + 4 * 128 * 128].rearrange("(tc p t) -> p tc t", p=128, tc=4)

    sb = nc.alloc_sbuf_tensor
    xn = sb("xn", [128, FC, NTOK], BF16)
    wXt = sb("wX", [128, 16, 512], BF16)
    wYt = sb("wY", [128, 16, 512], BF16)
    cosb = sb("cosb", [128, NTOK], F32)
    sinb = sb("sinb", [128, NTOK], F32)
    rtb = sb("rtb", [128, 128], BF16)
    idnb = sb("idnb", [128, 128], BF16)
    onesb = sb("onesb", [1, 128], BF16)
    onesf = sb("onesf", [1, 128], F32)
    onescol = sb("onescol", [128, 1], BF16)
    sc_a = sb("sc_a", [128, 1024], F32)
    sc_c = sb("sc_c", [128, 2, 512], BF16)
    gstg = sb("gstg", [128, 2, 512], BF16)
    sc_d = sb("sc_d", [128, 2, 512], BF16)
    pp = sb("pp", [128, 4, 512], BF16)
    pm = sb("pm", [128, 2, 512], BF16)
    rz = sb("rz", [1, 512], F32)
    rzb = sb("rzb", [1, 512], BF16)
    rzt = sb("rzt", [128, 512], BF16)
    bqs = sb("bqs", [128, 16], F32)
    expsh = sb("expsh", [128, 1], F32)
    epsb = sb("epsb", [128, 1], F32)
    bnag = sb("bnag", [128, 4, 6], F32)
    bnmv = sb("bnmv", [128, 3, 4, 2], F32)
    gamb = sb("gamb", [128, 512], F32)
    betb = sb("betb", [128, 512], F32)
    st1 = sb("st1", [128, 3, 4], F32)
    st2 = sb("st2", [128, 3, 4], F32)
    stm = sb("stm", [128, 3, 4], F32)
    str_ = sb("str", [128, 3, 4], F32)
    fmy = sb("fmy", [128, 3, 512], F32)
    fmo = sb("fmo", [128, 3, 512], BF16)
    xout = sb("xout", [128, 512], F32)
    bsc = sb("bsc", [128, 48 * 3 + 16 * 3 + 64 * 2], F32)

    pALL = nc.alloc_psum_tensor("pALL", [128, 4096], F32)
    _pap = pALL.ap()
    seg = lambda i: _pap[:, i * 512:(i + 1) * 512]
    pA, pB, pC, pD = seg(0), seg(1), seg(2), seg(3)
    pE, pF, pZ = seg(4), seg(5), seg(6)
    pT7 = _pap.bitcast(BF16)[:, 7168:8192]
    pair45 = _pap[:, 2048:3072]
    pair67 = _pap[:, 3072:4096]
    PK = {0: (pA, "pA"), 1: (pB, "pB")}

    dC = b.sem("dC")
    dCg = b.sem("dCg")
    dX = [b.sem("dX0"), b.sem("dX1")]
    dWX, dWY = b.sem("dWX"), b.sem("dWY")
    dWZ, dWW = b.sem("dWZ"), b.sem("dWW")
    dKV = [b.sem("dKV0"), b.sem("dKV1")]
    dKVW = b.sem("dKVW")
    dVW = b.sem("dVW")
    dOUT = b.sem("dOUT")
    dGW = [b.sem("dGW0"), b.sem("dGW1")]
    tT = b.sem("tT")
    vV = b.sem("vV")
    sS = b.sem("sS")
    cc = b.sem("cc")
    gG = b.sem("gG")

    def VCH():
        W("vector", vV)
    WSEM = {id(wXt): dWX, id(wYt): dWY}

    def dma_gate(*engs):
        for eng in engs:
            for s in (tT, vV, sS):
                W(eng, s)

    def dma(eng, out, in_, sem):
        O(eng, lambda e, o=out, i=in_: e.dma_start(out=o, in_=i), sem, 16)
        return sem.n

    dma("gpsimd", rtb[:, :], RT, dCg)
    dma("gpsimd", idnb[:, :], IDN, dCg)
    dma("sync", cosb[:, :], COS, dC)
    dma("sync", sinb[:, :], SIN, dC)
    dma("sync", gamb[:, :], GAMB, dC)
    dma("sync", betb[:, :], BETB, dC)
    boff = {}
    _o = 0
    for nm, ap_, w_ in (("bqkv2", bqkv2, 48), ("g0s", g0s, 48), ("u0s", u0s, 48),
                        ("d0s", d0s, 16), ("bo2", bo2, 16), ("d1s", d1s, 16),
                        ("g1s", g1s, 64), ("u1s", u1s, 64)):
        boff[nm] = _o
        dma("sync", bsc[:, _o:_o + w_], ap_, dC)
        _o += w_

    def bcol(nm, j):
        return bsc[:, boff[nm] + j:boff[nm] + j + 1]

    W("vector", dC)
    W("vector", dCg)
    O("vector", lambda e: e.memset(onesb[:, :], 1.0))
    O("vector", lambda e: e.memset(onesf[:, :], 1.0))
    O("vector", lambda e: e.memset(expsh[:, :], EXP_SHIFT))
    O("vector", lambda e: e.memset(epsb[:, :], EPS_L1))
    O("vector", lambda e: e.memset(onescol[:, :], 1.0))
    O("vector", lambda e: e.tensor_scalar_mul(
        bqs[:, :], bsc[:, boff["bqkv2"]:boff["bqkv2"] + 16], RS), vV, 1)
    b.barrier()

    def wload(WSRC, grp, wbuf, view=None):
        sem = WSEM[id(wbuf)]
        b.fwait("gpsimd", ("w", id(wbuf)))
        dst = view if view is not None else wbuf[:, :, :]
        return dma("gpsimd", dst, WSRC[grp].rearrange("c p t -> p c t"), sem), sem

    def phase_ln(L):
        XSRC = X if L == 0 else X1
        XR = XSRC.rearrange("(c p) t -> p c t", p=128)
        if L == 1:
            W("sync", dOUT)
        from contextlib import ExitStack
        _uid[0] += 1
        u = _uid[0]
        with ExitStack() as es:
            xsg = es.enter_context(nc.sbuf_tensor(f"xs{u}", [128, 2, 4, 1024], F32))
            xqg = es.enter_context(nc.sbuf_tensor(f"xq{u}", [128, 2, 1024], BF16))
            xqs = es.enter_context(nc.sbuf_tensor(f"xsq{u}", [128, 2, 1024], BF16))
            mrow = es.enter_context(nc.sbuf_tensor(f"mrow{u}", [1, 1024], F32))
            rrow = es.enter_context(nc.sbuf_tensor(f"rrow{u}", [1, 1024], F32))
            trow = es.enter_context(nc.sbuf_tensor(f"trow{u}", [1, 1024], F32))
            t2row = es.enter_context(nc.sbuf_tensor(f"t2row{u}", [1, 1024], F32))
            xs = xsg
            dma_gate("sync")
            accs = [(pC, "pC"), (pD, "pD"), (pE, "pE"), (pF, "pF")]
            for a, k in accs:
                b.bwait("tensor", k)
            for grp in range(4):
                p = grp % 2
                b.fwait("sync", ("xs", p))
                b.fwait("sync", ("xs", p, "v"))
                snap = dma("sync", xs[:, p, :, :], XR[:, grp * 4:grp * 4 + 4, :], dX[p])
                for j in range(4):
                    c = grp * 4 + j
                    src = xs[:, p, j, :]
                    W("vector", dX[p], snap)
                    b.fwait("vector", ("xq", c % 2))
                    O("vector", lambda e, o=xqg[:, c % 2, :], i=src: e.tensor_copy(o, i), vV, 1)
                    W("scalar", dX[p], snap)
                    b.fwait("scalar", ("xsq", c % 2))
                    O("scalar", lambda e, o=xqs[:, c % 2, :], i=src:
                      e.activation(o, i, AF.Square), sS, 1)
                    W("tensor", vV)
                    W("tensor", sS)
                    for half in range(2):
                        hs = slice(half * 512, half * 512 + 512)
                        O("tensor", lambda e, o=accs[half][0][0:1, :], l=onescol[:, :],
                          r=xqg[:, c % 2, hs], s=(c == 0), p_=(c == 15):
                          e.matmul(o, l, r, start=s, stop=p_), tT, 1)
                        O("tensor", lambda e, o=accs[2 + half][0][0:1, :], l=onescol[:, :],
                          r=xqs[:, c % 2, hs], s=(c == 0), p_=(c == 15):
                          e.matmul(o, l, r, start=s, stop=p_), tT, 1)
                    b.fread(("xq", c % 2), tT)
                    b.fread(("xsq", c % 2), tT)
                b.fread(("xs", p), sS)
                b.fread(("xs", p, "v"), vV)
            W("scalar", tT)
            for half in range(2):
                hs = slice(half * 512, half * 512 + 512)
                O("scalar", lambda e, o=trow[0:1, hs], i=accs[half][0][0:1, :]:
                  e.copy(o, i), sS, 1)
                O("scalar", lambda e, o=t2row[0:1, hs], i=accs[2 + half][0][0:1, :]:
                  e.copy(o, i), sS, 1)
            for a, k in accs:
                b.bread(k, sS)
            W("vector", sS)
            O("vector", lambda e: e.tensor_scalar_mul(mrow[:, :], trow[:, :], 1.0 / 2048.0), vV, 1)
            O("vector", lambda e: e.tensor_scalar(t2row[:, :], t2row[:, :], 1.0 / 2048.0, EPS_LN,
                                                  ALU.mult, ALU.add), vV, 1)
            VCH()
            O("vector", lambda e: e.scalar_tensor_tensor(trow[:, :], mrow[:, :], 1.0, mrow[:, :],
                                                         ALU.mult, ALU.mult), vV, 1)
            VCH()
            O("vector", lambda e: e.tensor_tensor(t2row[:, :], t2row[:, :], trow[:, :],
                                                  ALU.subtract), vV, 1)
            VCH()
            O("vector", lambda e: e.reciprocal(t2row[:, :], t2row[:, :]), vV, 1)
            W("scalar", vV)
            O("scalar", lambda e: e.activation(rrow[:, :], t2row[:, :], AF.Sqrt), sS, 1)
            for grp in range(4):
                p = grp % 2
                b.fwait("sync", ("xs", p))
                b.fwait("sync", ("xs", p, "v"))
                snap = dma("sync", xs[:, p, :, :], XR[:, grp * 4:grp * 4 + 4, :], dX[p])
                for half in range(2):
                    hs = slice(half * 512, half * 512 + 512)
                    pk, bank = PK[half]
                    pk2, bank2 = [(pE, "pE"), (pF, "pF")][half]
                    if grp == 0:
                        W("tensor", sS)
                        b.bwait("tensor", bank)
                        O("tensor", lambda e, o=pk[:, :], r=mrow[0:1, hs]:
                          e.matmul(o, onesf[0:1, :], r, start=True, stop=True), tT, 1)
                        b.bwait("tensor", bank2)
                        O("tensor", lambda e, o=pk2[:, :], r=rrow[0:1, hs]:
                          e.matmul(o, onesf[0:1, :], r, start=True, stop=True), tT, 1)
                    for j in range(4):
                        c = grp * 4 + j
                        W("vector", dX[p], snap)
                        W("vector", tT)
                        b.fwait("vector", ("lntmp",))
                        O("vector", lambda e, o=sc_a[:, 0:512], i=xs[:, p, j, hs], m=pk[:, :]:
                          e.tensor_tensor(o, i, m, ALU.subtract), vV, 1)
                        VCH()
                        O("vector", lambda e, o=xn[:, c, hs], i=sc_a[:, 0:512], m=pk2[:, :]:
                          e.tensor_tensor(o, i, m, ALU.mult), vV, 1)
                        b.fread(("lntmp",), vV)
                b.fread(("xs", p), vV)
                b.fread(("xs", p, "v"), vV)
            b.bread("pA", vV)
            b.bread("pB", vV)
            b.bread("pE", vV)
            b.bread("pF", vV)
            b.barrier()

    def proj_rope(grps, kind, qdst=None, slot=None):
        pending = [None]

        def emit_fin():
            if pending[0] is not None:
                pending[0]()
                pending[0] = None

        for gi, grp in enumerate(grps):
            wbuf = (wXt, wYt)[gi % 2]
            snap, wsem = wload(WqkvT, grp, wbuf)
            for hh in range(4):
                h = gi * 4 + hh
                halves = range(2) if slot is None else [slot]
                for half in halves:
                    hs = slice(half * 512, half * 512 + 512)
                    pk, bank = PK[half % 2]
                    b.bwait("tensor", bank)
                    W("tensor", wsem, snap)
                    for kc in range(FC):
                        O("tensor", lambda e, o=pk[:, :], l=wbuf[:, kc, hh * 128:hh * 128 + 128],
                          r=xn[:, kc, hs], s=(kc == 0), p_=(kc == 15):
                          e.matmul(o, l, r, start=s, stop=p_), tT, 1)
                    b.fread(("w", id(wbuf)), tT)
                    emit_fin()
                    W("scalar", tT)
                    b.fwait("scalar", ("praw", half))
                    if kind == "q":
                        O("scalar", lambda e, o=sc_a[:, hs], i=pk[:, :], bb=bqs[:, h:h + 1]:
                          e.activation(o, i, AF.Identity, bias=bb, scale=RS), sS, 1)
                    else:
                        O("scalar", lambda e, o=sc_a[:, hs], i=pk[:, :],
                          bb=bcol("bqkv2", 16 + h): e.activation(o, i, AF.Identity, bias=bb), sS, 1)
                    b.bread(bank, sS)
                    W("vector", sS)
                    tile = (h * 2 + half) % 2
                    b.fwait("vector", ("pp", tile))
                    b.fwait("vector", ("pp2", tile))
                    O("vector", lambda e, o=pp[:, tile, :], i=sc_a[:, hs], m=sinb[:, hs]:
                      e.tensor_tensor(o, i, m, ALU.mult), vV, 1)
                    O("vector", lambda e, o=pm[:, tile, :], i=sc_a[:, hs], m=cosb[:, hs]:
                      e.tensor_tensor(o, i, m, ALU.mult), vV, 1)
                    b.fread(("praw", half), vV)
                    vsnap = vV.n

                    def fin(h=h, half=half, hs=hs, vsnap=vsnap, tile=tile):
                        b.bwait("tensor", "pF")
                        W("tensor", vV, vsnap)
                        O("tensor", lambda e, o=pF[:, :], l=rtb[:, :], r=pp[:, tile, :]:
                          e.matmul(o, l, r, start=True, stop=False), tT, 1)
                        O("tensor", lambda e, o=pF[:, :], l=idnb[:, :], r=pm[:, tile, :]:
                          e.matmul(o, l, r, start=False, stop=True), tT, 1)
                        b.fread(("pp", tile), tT)
                        b.fread(("pp2", tile), tT)
                        W("scalar", tT)
                        if kind == "q":
                            O("scalar", lambda e, o=qdst[:, h, 0:512], i=pF[:, :]: e.copy(o, i), sS, 1)
                            b.bread("pF", sS)
                        else:
                            b.fwait("scalar", ("kst",))
                            O("scalar", lambda e, o=rzt[:, :], i=pF[:, :]: e.copy(o, i), sS, 1)
                            b.bread("pF", sS)
                            W("sync", sS)
                            dma("sync", kvo_k(half, h), rzt[:, :], dKVW)
                            b.fread(("kst",), dKVW)
                    pending[0] = fin
        emit_fin()

    def proj_tm(WSRC, grps, consume, slot=None, depth=3):
        tcs = range(8) if slot is None else range(slot * 4, slot * 4 + 4)
        pending = []
        ti = [0]
        for gi, grp in enumerate(grps):
            wbuf = (wXt, wYt)[gi % 2]
            snap, wsem = wload(WSRC, grp, wbuf)
            for tc in tcs:
                pk, bank = PK[tc % 2]
                b.bwait("tensor", bank)
                W("tensor", wsem, snap)
                for kc in range(FC):
                    O("tensor", lambda e, o=pk[:, :], l=xn[:, kc, tc * 128:tc * 128 + 128],
                      r=wbuf[:, kc, :], s=(kc == 0), p_=(kc == 15):
                      e.matmul(o, l, r, start=s, stop=p_), tT, 1)
                b.fread(("w", id(wbuf)), tT)
                if len(pending) >= depth:
                    pending.pop(0)()
                fin = consume(gi, tc, pk, bank, ti[0] % 3)
                ti[0] += 1
                if fin is not None:
                    pending.append(fin)
        for f in pending:
            f()

    def v_consume(gi, tc, pk, bank, par=0):
        half, tcq = tc // 4, tc % 4
        W("scalar", tT)
        b.fwait("scalar", ("vst",))
        O("scalar", lambda e, o=sc_d[:, 1, :], i=pk[:, :]: e.copy(o, i), sS, 1)
        b.bread(bank, sS)
        W("sync", sS)
        for hh in range(4):
            h = gi * 4 + hh
            dma("sync", kvo_v(half, h, tcq), sc_d[:, 1, hh * 128:hh * 128 + 128], dVW)
        b.fread(("vst",), dVW)
        return None

    def fm_consume_mk(is_q, qdst=None, pT=None):
        def consume(gi, tc, pk, bank, par):
            half, tcq = tc // 4, tc % 4
            fy = fmy[:, par, :]
            fo_ = fmo[:, par, :]
            s1, s2 = st1[:, par, :], st2[:, par, :]
            sm, sr = stm[:, par, :], str_[:, par, :]
            W("vector", tT)
            b.fwait("vector", ("fmy", par))
            O("vector", lambda e, o=fy, i=pk[:, :], m=gamb[:, :]:
              e.tensor_tensor(o, i, m, ALU.mult), vV, 1)
            VCH()
            O("vector", lambda e, o=fy, i=fy, m=betb[:, :]:
              e.tensor_tensor(o, i, m, ALU.add), vV, 1)
            b.bread(bank, vV)
            VCH()
            b.fwait("vector", ("bnag",))
            for hh in range(4):
                O("vector", lambda e, o=bnag[:, hh, :], i=fy[:, hh * 128:hh * 128 + 128]:
                  e.bn_stats(o, i), vV, 1)
            VCH()
            b.fwait("vector", ("bnmv", par))
            for hh in range(4):
                O("vector", lambda e, o=bnmv[:, par, hh, :], i=bnag[:, hh, :]:
                  e.bn_aggr(o, i), vV, 1)
            b.fread(("bnag",), vV)
            VCH()
            b.fwait("vector", ("str", par))
            O("vector", lambda e, o=sr, i=bnmv.ap()[:, par, :, 1]:
              e.tensor_scalar_add(o, i, EPS_L1), vV, 1)
            VCH()
            O("vector", lambda e, o=sr, i=sr: e.reciprocal(o, i), vV, 1)
            W("scalar", vV)
            O("scalar", lambda e, o=sr, i=sr: e.activation(o, i, AF.Sqrt), sS, 1)
            if is_q:
                W("vector", sS)
                VCH()
                O("vector", lambda e, o=sr, i=sr: e.tensor_scalar_mul(o, i, RS), vV, 1)
            for hh in range(4):
                W("vector", sS)
                VCH()
                b.fwait("vector", ("fmo", par, hh))
                O("vector", lambda e, o=fo_[:, hh * 128:hh * 128 + 128],
                  i=fy[:, hh * 128:hh * 128 + 128], m1=bnmv.ap()[:, par, hh, 0:1],
                  m2=sr[:, hh:hh + 1]:
                  e.tensor_scalar(o, i, m1, m2, ALU.subtract, ALU.mult), vV, 1)
            b.fread(("bnmv", par), vV)
            b.fread(("fmy", par), vV)
            vsnap = vV.n

            def fin(gi=gi, tc=tc, half=half, tcq=tcq, vsnap=vsnap, par=par, fo_=fo_):
                for hh in range(4):
                    h = gi * 4 + hh
                    b.bwait("tensor", "pT")
                    W("tensor", vV, vsnap)
                    O("tensor", lambda e, o=pT[:, 0:128], i=fo_[:, hh * 128:hh * 128 + 128]:
                      e.transpose(o, i, idnb[:, :]), tT, 1)
                    b.fread(("fmo", par, hh), tT)
                    W("scalar", tT)
                    if is_q:
                        tl = tc % 4
                        O("scalar", lambda e, o=qdst[:, h, tl * 128:tl * 128 + 128],
                          i=pT[:, 0:128]: e.copy(o, i), sS, 1)
                        b.bread("pT", sS)
                    else:
                        b.fwait("scalar", ("kst",))
                        O("scalar", lambda e, o=rzt[:, 0:128], i=pT[:, 0:128]: e.copy(o, i), sS, 1)
                        b.bread("pT", sS)
                        W("sync", sS)
                        dma("sync",
                            kvo_k(half, h).rearrange("p (a t) -> p a t", a=4)[:, tcq, :],
                            rzt[:, 0:128], dKVW)
                        b.fread(("kst",), dKVW)
            return fin
        return consume

    def phase_ffn(L, kick=None):
        if L == 0:
            NMG, NKD, WG, WU, WD = 12, 48, G0w, U0w, D0w
            gsn, usn, dsn = "g0s", "u0s", "d0s"
        else:
            NMG, NKD, WG, WU, WD = 16, 64, G1w, U1w, D1w
            gsn, usn, dsn = "g1s", "u1s", "d1s"
        nseg = NKD // 16
        GDr = GD.rearrange("(c p) t -> p c t", p=128)
        from contextlib import ExitStack
        _uid[0] += 1
        with ExitStack() as es:
            tokw = 512
            fuse = es.enter_context(nc.sbuf_tensor(f"fuse{_uid[0]}", [128, NKD, tokw], BF16))
            wZt = es.enter_context(nc.sbuf_tensor(f"wZ{_uid[0]}", [128, 16, 512], BF16))
            wWt = es.enter_context(nc.sbuf_tensor(f"wW{_uid[0]}", [128, 16, 512], BF16))
            WSEM[id(wZt)] = dWZ
            WSEM[id(wWt)] = dWW
            tiles = [wXt, wYt, wZt, wWt]
            if L == 0:
                views = [t.ap().bitcast(FP8)[:, :, 0:512] for t in tiles]
            else:
                views = [t[:, :, :] for t in tiles]

            def mk_loader(srcs):
                snaps = [None] * len(srcs)
                emitted = [0]

                nb = len(tiles)
                def ensure(i):
                    while emitted[0] <= min(i + (nb - 2), len(srcs) - 1):
                        j = emitted[0]
                        t = tiles[j % nb]
                        b.fwait("gpsimd", ("w", id(t)))
                        snaps[j] = dma("gpsimd", views[j % nb],
                                       srcs[j].rearrange("c p t -> p c t"), WSEM[id(t)])
                        emitted[0] += 1
                return snaps, ensure

            halves = [0, 1]
            for hpass in halves:
                srcs = []
                for mg in range(NMG):
                    srcs.append(WG[mg])
                    srcs.append(WU[mg])
                for fg in range(4):
                    WDr = WD[fg].rearrange("(s c) p t -> s c p t", c=16)
                    for seg in range(nseg):
                        srcs.append(WDr[seg])
                snaps, ensure = mk_loader(srcs)
                if kick is not None and hpass == 0:
                    ensure(1)
                    kick()
                    kick = None
                tok_halves = [hpass]
                li = 0
                for mg in range(NMG):
                    ensure(li + 1)
                    for mc in range(4):
                        m = mg * 4 + mc
                        for hi, half in enumerate(tok_halves):
                            hs = slice(half * 512, half * 512 + 512)
                            fs = slice(hi * 512, hi * 512 + 512)
                            slot = m % 2
                            for (scn, func, banks, idx) in (
                                    (gsn, AF.Silu, ((pA, "pA"), (pB, "pB")), li),
                                    (usn, AF.Copy, ((pC, "pC"), (pD, "pD")), li + 1)):
                                wv = views[idx % len(tiles)]
                                wt = tiles[idx % len(tiles)]
                                pk, bank = banks[hi % len(banks)]
                                b.bwait("tensor", bank)
                                W("tensor", WSEM[id(wt)], snaps[idx])
                                for kc in range(FC):
                                    O("tensor", lambda e, o=pk[:, :],
                                      l=wv[:, kc, mc * 128:mc * 128 + 128],
                                      r=xn[:, kc, hs], s=(kc == 0), p_=(kc == 15):
                                      e.matmul(o, l, r, start=s, stop=p_), tT, 1)
                                b.fread(("w", id(wt)), tT)
                                W("scalar", tT)
                                if func is AF.Silu:
                                    b.fwait("scalar", ("gstg", slot))
                                    O("scalar", lambda e, o=gstg[:, slot, :], i=pk[:, :],
                                      sc=bcol(scn, m): e.activation(o, i, AF.Silu, scale=sc),
                                      sS, 1)
                                    b.bread(bank, sS)
                                else:
                                    b.fwait("scalar", ("gu", hi))
                                    O("scalar", lambda e, o=sc_d[:, hi, :], i=pk[:, :],
                                      sc=bcol(scn, m): e.activation(o, i, AF.Copy, scale=sc),
                                      sS, 1)
                                    b.bread(bank, sS)
                                    W("vector", sS)
                                    b.fwait("vector", ("fuse", m, hi))
                                    O("vector", lambda e, o=fuse[:, m, fs], a=gstg[:, slot, :],
                                      bb=sc_d[:, hi, :]: e.tensor_tensor(o, a, bb, ALU.mult),
                                      vV, 1)
                                    b.fread(("gstg", slot), vV)
                                    b.fread(("gu", hi), vV)
                    li += 2
                dbanks = [(pA, "pA"), (pB, "pB"), (pC, "pC"), (pD, "pD")]
                nunit = len(tok_halves)
                for fg in range(4):
                    for _, kk in dbanks:
                        b.bwait("tensor", kk)
                    for seg in range(nseg):
                        idx = li + seg
                        ensure(idx)
                        wv = views[idx % len(tiles)]
                        wt = tiles[idx % len(tiles)]
                        W("tensor", WSEM[id(wt)], snaps[idx])
                        if seg == 0:
                            W("tensor", vV)
                        for kc in range(16):
                            k = seg * 16 + kc
                            for fo4 in range(4):
                                for hi in range(nunit):
                                    fs = slice(hi * 512, hi * 512 + 512)
                                    dbk = dbanks[fo4 * nunit + hi][0]
                                    O("tensor", lambda e, o=dbk[:, :],
                                      l=wv[:, kc, fo4 * 128:fo4 * 128 + 128],
                                      r=fuse[:, k, fs],
                                      s=(k == 0), p_=(k == NKD - 1):
                                      e.matmul(o, l, r, start=s, stop=p_), tT, 1)
                        b.fread(("w", id(wt)), tT)
                    li += nseg
                    W("scalar", tT)
                    for fo4 in range(4):
                        fo = fg * 4 + fo4
                        for hi, half in enumerate(tok_halves):
                            hs = slice(half * 512, half * 512 + 512)
                            ui = (fo4 * nunit + hi) % 2
                            b.fwait("scalar", ("pp", ui))
                            O("scalar", lambda e, o=pp[:, ui, :],
                              i=dbanks[fo4 * nunit + hi][0][:, :],
                              sc=bcol(dsn, fo): e.activation(o, i, AF.Silu, scale=sc), sS, 1)
                            W("sync", sS)
                            dma("sync", GDr[:, fo, hs], pp[:, ui, :], dGW[ui])
                            b.fread(("pp", ui), dGW[ui])
                    for _, kk in dbanks:
                        b.bread(kk, sS)
                for m in range(NKD):
                    for hi in range(nunit):
                        b.fread(("fuse", m, hi), tT)
            b.barrier()

    def phase_attn(L, qbuf, obuf, slot, mskb, kbufs, vbufs):
        nch = 8 if slot == 0 else 16
        qs = slice(slot * 512, slot * 512 + 512)
        quarters = [0, 1] if slot == 0 else [0, 1, 2, 3]

        def issue_kv(h):
            p = h % 2
            b.fwait("sync", ("kv", p))
            for ji, gq in enumerate(quarters):
                dma("sync", kbufs[p][:, ji, :], kvg_k(gq, h), dKV[p])
                dma("sync", vbufs[p][:, ji * 4:ji * 4 + 4, :], kvg_v(gq, h), dKV[p])
            return dKV[p].n

        pend_norm = [None]

        def emit_norm():
            if pend_norm[0] is not None:
                pend_norm[0]()
                pend_norm[0] = None

        def mk_norm(h, ob, obk, zb, zbk, tchunks):
            def norm():
                W("vector", tT, tchunks)
                b.fwait("vector", ("rz",))
                if L == 0:
                    O("vector", lambda e: e.reciprocal(rz[:, :], zb[0:1, :]), vV, 1)
                else:
                    O("vector", lambda e: e.tensor_scalar_add(rz[:, :], zb[0:1, :], EPS_L1), vV, 1)
                    VCH()
                    O("vector", lambda e: e.reciprocal(rz[:, :], rz[:, :]), vV, 1)
                VCH()
                O("vector", lambda e: e.tensor_copy(rzb[:, :], rz[:, :]), vV, 1)
                b.bread(zbk, vV)
                b.bwait("tensor", zbk)
                W("tensor", vV)
                O("tensor", lambda e, o=zb[:, :]: e.matmul(o, onesb[:, :], rzb[:, :],
                                                           start=True, stop=True), tT, 1)
                W("scalar", tT)
                b.fwait("scalar", ("kst",))
                O("scalar", lambda e, o=rzt[:, :], i=zb[:, :]: e.copy(o, i), sS, 1)
                b.bread(zbk, sS)
                W("vector", sS)
                if L == 0:
                    b.fwait("vector", ("onrm",))
                    O("vector", lambda e, o=sc_c[:, 0, :], a=ob[:, :], bb=rzt[:, :]:
                      e.tensor_tensor(o, a, bb, ALU.mult), vV, 1)
                    VCH()
                    O("vector", lambda e, o=obuf[:, h, qs], i=sc_c[:, 0, :],
                      bb=bcol("bqkv2", 32 + h): e.tensor_scalar_add(o, i, bb), vV, 1)
                    b.fread(("onrm",), vV)
                else:
                    O("vector", lambda e, o=obuf[:, h, qs], a=ob[:, :], bb=rzt[:, :]:
                      e.tensor_tensor(o, a, bb, ALU.mult), vV, 1)
                b.bread(obk, vV)
                b.fread(("rz",), vV)
                b.fread(("kst",), vV)
            return norm

        if True:
            npair = nch // 2
            pairs = [(pair45, "pr45"), (pair67, "pr67")]
            rot_i = [0]
            kvsnap = issue_kv(0)
            for h in range(H):
                p = h % 2
                mysnap = kvsnap
                if h + 1 < H:
                    kvsnap = issue_kv(h + 1)
                ob, obk = (pA, "pA") if h % 2 == 0 else (pB, "pB")
                zb, zbk = (pC, "pC") if h % 2 == 0 else (pD, "pD")
                W("tensor", dKV[p], mysnap)
                stage = []

                def pop_pair():
                    pi, pv, pk_, tsnap = stage.pop(0)
                    s_ = pi % 2
                    W("scalar", tT, tsnap)
                    b.fwait("scalar", ("ppp", s_))
                    if L == 0:
                        O("scalar", lambda e, o=pp[:, 2 * s_:2 * s_ + 2, :],
                          i=pv.rearrange("p (a t) -> p a t", a=2):
                          e.activation(o, i, AF.Exp, bias=expsh[:, 0:1]), sS, 1)
                    else:
                        O("scalar", lambda e, o=pp[:, 2 * s_:2 * s_ + 2, :],
                          i=pv.rearrange("p (a t) -> p a t", a=2):
                          e.activation(o, i, AF.Square), sS, 1)
                    b.bread(pk_, sS)
                    ssnap = sS.n
                    last_eng = sS
                    for ci in range(2):
                        c2 = 2 * pi + ci
                        masked = True if slot == 0 else (c2 >= 8)
                        mski = slot * 8 + (c2 if slot == 0 else c2 - 8)
                        if masked:
                            W("gpsimd", sS, ssnap)
                            b.fwait("gpsimd", ("pmm", c2 % 2))
                            O("gpsimd", lambda e, o=pm[:, c2 % 2, :], i=pp[:, 2 * s_ + ci, :],
                              m=mskb[:, mski, :]: e.tensor_tensor(o, i, m, ALU.mult), gG, 1)
                            src_ = pm[:, c2 % 2, :]
                            W("tensor", gG)
                        else:
                            src_ = pp[:, 2 * s_ + ci, :]
                            W("tensor", sS, ssnap)
                        if c2 == 0:
                            b.bwait("tensor", obk)
                            b.bwait("tensor", zbk)
                        O("tensor", lambda e, o=ob[:, :], l=vbufs[p][:, c2, :], r=src_,
                          s=(c2 == 0), p_=(c2 == nch - 1):
                          e.matmul(o, l, r, start=s, stop=p_), tT, 1)
                        O("tensor", lambda e, o=zb[0:1, :], l=onescol[:, :], r=src_,
                          s=(c2 == 0), p_=(c2 == nch - 1):
                          e.matmul(o, l, r, start=s, stop=p_), tT, 1)
                        if masked:
                            b.fread(("pmm", c2 % 2), tT)
                    b.fread(("ppp", s_), tT)

                for pi in range(npair + 2):
                    if stage and (len(stage) == 2 or pi >= npair):
                        pop_pair()
                    if pi == 1:
                        emit_norm()
                    if pi < npair:
                        pv, pk_ = pairs[rot_i[0] % 2]
                        rot_i[0] += 1
                        b.bwait("tensor", pk_)
                        for ci in range(2):
                            c = 2 * pi + ci
                            O("tensor", lambda e, o=pv[:, ci * 512:ci * 512 + 512],
                              l=kbufs[p][:, c // 4, (c % 4) * 128:(c % 4) * 128 + 128],
                              r=qbuf[:, h, 0:512]: e.matmul(o, l, r, start=True, stop=True),
                              tT, 1)
                        stage.append((pi, pv, pk_, tT.n))
                b.fread(("kv", p), tT)
                pend_norm[0] = mk_norm(h, ob, obk, zb, zbk, tT.n)
            emit_norm()

    def phase_wo(L, obuf):
        XSRC = X if L == 0 else X1
        XR = XSRC.rearrange("(c p) t -> p c t", p=128)
        GDr = GD.rearrange("(c p) t -> p c t", p=128)
        XD = (X1 if L == 0 else OUT).rearrange("(c p) t -> p c t", p=128)
        WoT = Wo0T if L == 0 else Wo1T
        from contextlib import ExitStack
        _uid[0] += 1
        with ExitStack() as es:
            xf = es.enter_context(nc.sbuf_tensor(f"xf{_uid[0]}", [128, 2, 1024], F32))
            gf = es.enter_context(nc.sbuf_tensor(f"gf{_uid[0]}", [128, 2, 1024], BF16))
            dma_gate("sync")
            W("sync", dGW[0])
            W("sync", dGW[1])
            for grp in range(4):
                wbuf = (wXt, wYt)[grp % 2]
                snap, wsem = wload(WoT, grp, wbuf)
                for hh in range(4):
                    fo = grp * 4 + hh
                    p = fo % 2
                    b.fwait("sync", ("xf", p))
                    xsnap = dma("sync", xf[:, p, :], XR[:, fo, :], dX[p])
                    dma("sync", gf[:, p, :], GDr[:, fo, :], dX[p])
                    xsnap = dX[p].n
                    for half in range(2):
                        hs = slice(half * 512, half * 512 + 512)
                        pk, bank = PK[half]
                        b.bwait("tensor", bank)
                        W("tensor", wsem, snap)
                        for kc in range(FC):
                            O("tensor", lambda e, o=pk[:, :],
                              l=wbuf[:, kc, hh * 128:hh * 128 + 128],
                              r=obuf[:, kc, hs], s=(kc == 0), p_=(kc == 15):
                              e.matmul(o, l, r, start=s, stop=p_), tT, 1)
                        b.fread(("w", id(wbuf)), tT)
                        W("scalar", tT)
                        b.fwait("scalar", ("wos", half))
                        if L == 0:
                            O("scalar", lambda e, o=sc_c[:, half, :], i=pk[:, :],
                              bb=bcol("bo2", fo): e.activation(o, i, AF.Identity, bias=bb), sS, 1)
                        else:
                            O("scalar", lambda e, o=sc_c[:, half, :], i=pk[:, :]:
                              e.copy(o, i), sS, 1)
                        b.bread(bank, sS)
                        W("vector", sS)
                        W("vector", dX[p], xsnap)
                        b.fwait("vector", ("cmb",))
                        O("vector", lambda e, o=sc_a[:, 0:512], i=sc_c[:, half, :],
                          m=gf[:, p, hs]: e.tensor_tensor(o, i, m, ALU.mult), vV, 1)
                        b.fread(("wos", half), vV)
                        b.fwait("vector", ("xout",))
                        VCH()
                        O("vector", lambda e, o=xout[:, :], i=sc_a[:, 0:512], m=xf[:, p, hs]:
                          e.tensor_tensor(o, i, m, ALU.add), vV, 1)
                        b.fread(("cmb",), vV)
                        W("sync", vV)
                        dma("sync", XD[:, fo, hs], xout[:, :], dOUT)
                        b.fread(("xout",), dOUT)
                    b.fread(("xf", p), vV)
            b.barrier()

    from contextlib import ExitStack
    for L in range(2):
        esL = ExitStack()
        pT = pT7
        phase_ln(L)
        if L == 0:
            proj_rope([4, 5, 6, 7], "k")
            proj_tm(WqkvT, [8, 9, 10, 11], v_consume)
        else:
            proj_tm(Wk1T, [0, 1, 2, 3], fm_consume_mk(False, pT=pT))
            proj_tm(Wv1T, [0, 1, 2, 3], v_consume)
        def kick():
            W("gpsimd", dKVW)
            W("gpsimd", dVW)
            O("gpsimd", lambda e: e.collective_compute(
                "AllGather", ALU.bypass,
                replica_groups=[[0, 1], [2, 3], [4, 5], [6, 7]],
                ins=[KVO.ap().opt()], outs=[KVG.ap().opt()]), cc, 1)
        phase_ffn(L, kick)
        with ExitStack() as es:
            obuf = es.enter_context(nc.sbuf_tensor(f"obuf{L}", [128, FC, NTOK], BF16))
            with ExitStack() as es2:
                qbuf = es2.enter_context(nc.sbuf_tensor(f"qbuf{L}", [128, FC, 512], BF16))
                mskb = es2.enter_context(nc.sbuf_tensor(f"mskb{L}", [128, 16, 512], BF16))
                kbufs = [es2.enter_context(nc.sbuf_tensor(f"kb{L}{p}", [128, 4, 512], BF16))
                         for p in range(2)]
                vbufs = [es2.enter_context(nc.sbuf_tensor(f"vb{L}{p}", [128, 16, 128], BF16))
                         for p in range(2)]
                dma_gate("gpsimd", "sync")
                dma("gpsimd", mskb[:, :, :], MSK.rearrange("c p t -> p c t"), dCg)
                W("sync", cc)
                W("vector", dCg)
                for slot in range(2):
                    if L == 0:
                        proj_rope([0, 1, 2, 3], "q", qbuf, slot)
                    else:
                        proj_tm(Wq1T, [0, 1, 2, 3], fm_consume_mk(True, qbuf, pT=pT), slot)
                    b.barrier()
                    phase_attn(L, qbuf, obuf, slot, mskb, kbufs, vbufs)
                    b.barrier()
            phase_wo(L, obuf)
        esL.close()

    W("gpsimd", dOUT)
    W("gpsimd", dGW[0])
    W("gpsimd", dGW[1])
    W("gpsimd", dKVW)
    W("gpsimd", dVW)
    W("gpsimd", dC)
    W("gpsimd", dCg)
    for s in dX + dKV + [dWX, dWY, dWZ, dWW]:
        W("gpsimd", s)

    with nc.Block() as block:
        @block.tensor
        def _(e):
            for f in b.q["tensor"]:
                f(e)

        @block.vector
        def _(e):
            for f in b.q["vector"]:
                f(e)

        @block.scalar
        def _(e):
            for f in b.q["scalar"]:
                f(e)

        @block.gpsimd
        def _(e):
            for f in b.q["gpsimd"]:
                f(e)

        @block.sync
        def _(e):
            for f in b.q["sync"]:
                f(e)

    b.stack.close()
    return nc


def _blocked(w2d):
    K, M = w2d.shape
    return np.ascontiguousarray(
        w2d.reshape(K // 128, 128, M // 512, 512).transpose(2, 0, 1, 3))


def _chunk_cols(v):
    return np.ascontiguousarray(v.reshape(-1, 128).T)


def _rope_tables(pos):
    p = np.arange(64, dtype=np.float32)
    theta = np.power(np.float32(10000.0),
                     (-(np.float32(2.0) ** p) / np.float32(128.0)).astype(np.float32))
    ang = pos.astype(np.float32)[None, :] * theta[:, None]
    ang = np.concatenate([ang, ang], axis=0)
    return np.cos(ang).astype(np.float32), np.sin(ang).astype(np.float32)


def _rot_matrix():
    R = np.zeros((128, 128), np.float32)
    for i in range(64):
        R[i, i + 64] = -1.0
        R[i + 64, i] = 1.0
    return np.ascontiguousarray(R.T)


def prepare_in_maps(ii):
    x = np.asarray(ii["L0_attn_x"], np.float32)
    shared = {
        "WqkvT": _blocked(np.ascontiguousarray(np.asarray(ii["L0_attn_Wqkv"], np.float32).T)),
        "bqkv2": _chunk_cols(np.asarray(ii["L0_attn_bqkv"], np.float32)),
        "Wo0T": _blocked(np.ascontiguousarray(np.asarray(ii["L0_attn_Wo"], np.float32).T)),
        "bo2": _chunk_cols(np.asarray(ii["L0_attn_bo"], np.float32)),
        "G0w": _blocked(np.asarray(ii["L0_ffn_gate_w"], np.float32)),
        "U0w": _blocked(np.asarray(ii["L0_ffn_up_w"], np.float32)),
        "D0w": _blocked(np.asarray(ii["L0_ffn_down_w"], np.float32)),
        "g0s": _chunk_cols(np.asarray(ii["L0_ffn_gate_scale"], np.float32)),
        "u0s": _chunk_cols(np.asarray(ii["L0_ffn_up_scale"], np.float32)),
        "d0s": _chunk_cols(np.asarray(ii["L0_ffn_down_scale"], np.float32)),
        "Wq1T": _blocked(np.ascontiguousarray(np.asarray(ii["L1_attn_Wq"], np.float32).T)),
        "Wk1T": _blocked(np.ascontiguousarray(np.asarray(ii["L1_attn_Wk"], np.float32).T)),
        "Wv1T": _blocked(np.ascontiguousarray(np.asarray(ii["L1_attn_Wv"], np.float32).T)),
        "Wo1T": _blocked(np.ascontiguousarray(np.asarray(ii["L1_attn_Wo"], np.float32).T)),
        "GAMB": np.ascontiguousarray(
            np.tile(np.asarray(ii["L1_attn_gamma"], np.float32)[None, :], (128, 4))),
        "BETB": np.ascontiguousarray(
            np.tile(np.asarray(ii["L1_attn_beta"], np.float32)[None, :], (128, 4))),
        "G1w": _blocked(np.asarray(ii["L1_ffn_w_gate"])),
        "U1w": _blocked(np.asarray(ii["L1_ffn_w_up"])),
        "D1w": _blocked(np.asarray(ii["L1_ffn_w_down"])),
        "g1s": _chunk_cols(np.asarray(ii["L1_ffn_s_gate"], np.float32)),
        "u1s": _chunk_cols(np.asarray(ii["L1_ffn_s_up"], np.float32)),
        "d1s": _chunk_cols(np.asarray(ii["L1_ffn_s_down"], np.float32)),
        "RT": _rot_matrix(),
        "IDN": np.eye(128, dtype=np.float32),
    }

    in_maps = []
    for c in range(NCORES):
        s, par = c // 2, c % 2
        qE, qL = (0, 3) if par == 0 else (1, 2)
        xc = np.concatenate([x[s, qE * 512:qE * 512 + 512, :],
                             x[s, qL * 512:qL * 512 + 512, :]], axis=0)
        pos = np.concatenate([np.arange(qE * 512, qE * 512 + 512),
                              np.arange(qL * 512, qL * 512 + 512)]) + 1
        cosc, sinc = _rope_tables(pos)
        mskc = np.zeros((16, 128, 512), np.float32)
        for slot in range(2):
            tqb = (qE if slot == 0 else qL) * 512
            for j in range(8):
                cglob = j if slot == 0 else 8 + j
                tk = cglob * 128 + np.arange(128)[:, None]
                tq = tqb + np.arange(512)[None, :]
                mskc[slot * 8 + j] = (tk <= tq).astype(np.float32)
        m = dict(shared)
        m["X"] = np.ascontiguousarray(xc.T)
        m["COS"] = cosc
        m["SIN"] = sinc
        m["MSK"] = mskc
        in_maps.append(m)
    return in_maps


def assemble_out(core_outs):
    out = np.empty((B, T, C), np.float32)
    for c in range(NCORES):
        s, par = c // 2, c % 2
        qE, qL = (0, 3) if par == 0 else (1, 2)
        oc = np.asarray(core_outs[c], np.float32).T
        out[s, qE * 512:qE * 512 + 512, :] = oc[:512]
        out[s, qL * 512:qL * 512 + 512, :] = oc[512:]
    return out


def kernel(**ii):
    in_maps = prepare_in_maps(ii)
    nc = build_program()
    res = run_bass_kernel_spmd(nc, in_maps, core_ids=list(range(NCORES)))
    return assemble_out([res.results[c]["OUT"] for c in range(NCORES)])
